# version 18
# baseline (speedup 1.0000x reference)
import contextlib
import numpy as np

import concourse.bass as bass
import concourse.mybir as mybir
from concourse.bass_utils import run_bass_kernel_spmd

F32 = mybir.dt.float32
BF16 = mybir.dt.bfloat16
AF = mybir.ActivationFunctionType
ALU = mybir.AluOpType

S = 2048
D = 1024
DC = 8
TC = 16
HA = 16
HB = 8
FF = 4096
RK = 32
CSH = 8.0

_CFG = {"stages": ("A", "B", "C", "D"), "dump": ()}


def _hilo(a):
    a = np.asarray(a, np.float64)
    hi = a.astype(np.float32)
    lo = (a - hi.astype(np.float64)).astype(np.float32)
    return hi, lo


def _host_consts():
    import jax
    import jax.numpy as jnp
    cpu_dev = jax.devices("cpu")[0]
    with jax.default_device(cpu_dev):
        f = jnp.float32
        c = {}
        c["IDF"] = np.eye(128, dtype=np.float32)
        P = np.zeros((128, 128), np.float32)
        for i in range(64):
            P[2 * i, 2 * i + 1] = -1.0
            P[2 * i + 1, 2 * i] = 1.0
        c["ROTT"] = np.ascontiguousarray(P.T)
        d, half = 128, 64
        base = (jnp.arange(0, d, 2, dtype=f) + 0.4 * d) / (1.4 * d)
        pos = jnp.arange(S, dtype=f)
        inv_freq = 1.0 / 10000.0 ** (jnp.arange(half, dtype=f) / half)
        sinusoid = pos[:, None] * inv_freq[None, :]
        sin, cos = jnp.sin(sinusoid), jnp.cos(sinusoid)
        scale = base[None, :] ** (pos[:, None] / 512.0)
        for tag, sc in (("Q", scale), ("K", 1.0 / scale)):
            sin2 = jnp.repeat(sin * sc, 2, axis=-1)
            cos2 = jnp.repeat(cos * sc, 2, axis=-1)
            c["XC%s_h" % tag] = np.ascontiguousarray(np.asarray(cos2, np.float32).T)
            c["XC%s_l" % tag] = np.zeros((128, S), np.float32)
            c["XS%s_h" % tag] = np.ascontiguousarray(np.asarray(sin2, np.float32).T)
            c["XS%s_l" % tag] = np.zeros((128, S), np.float32)
        gamj = 1.0 - 2.0 ** (-5.0 - jnp.arange(HB, dtype=f))
        ii32 = jnp.arange(128, dtype=f)
        diff = ii32[None, :] - ii32[:, None]
        decj = np.asarray(gamj[:, None, None] ** diff[None], np.float32)
        diff2 = (ii32[None, :] + 128.0) - ii32[:, None]
        decj2 = np.asarray(gamj[:, None, None] ** diff2[None], np.float32)
        dct = np.zeros((128, HB * 128), np.float32)
        dct2 = np.zeros((128, HB * 128), np.float32)
        for h in range(HB):
            dct[:, h * 128:(h + 1) * 128] = np.where(
                np.arange(128)[None, :] >= np.arange(128)[:, None], decj[h], 0.0)
            dct2[:, h * 128:(h + 1) * 128] = decj2[h]
        c["DCT_h"] = dct
        c["DCT_l"] = np.zeros_like(dct)
        c["DCT2_h"] = dct2
        c["DCT2_l"] = np.zeros_like(dct2)
    gam = 1.0 - 2.0 ** (-5.0 - np.arange(HB, dtype=np.float64))
    ii = np.arange(128, dtype=np.float64)
    gq = np.zeros((128, HB * 128), np.float64)
    gq2 = np.zeros((128, HB * 128), np.float64)
    for h in range(HB):
        gq[:, h * 128:(h + 1) * 128] = (gam[h] ** (ii + 1.0))[None, :]
        gq2[:, h * 128:(h + 1) * 128] = (gam[h] ** (ii + 129.0))[None, :]
    c["GQ_h"], c["GQ_l"] = _hilo(gq)
    c["GQ2_h"], c["GQ2_l"] = _hilo(gq2)
    gk = np.stack([gam[h] ** (127.0 - ii) for h in range(HB)], axis=1)
    c["GK_h"], c["GK_l"] = _hilo(gk)
    gk2 = np.stack([gam[h] ** (255.0 - ii) for h in range(HB)], axis=1)
    c["GK255_h"], c["GK255_l"] = _hilo(gk2)
    dgk = np.zeros((128, 16 * 128), np.float64)
    for par in range(2):
        for h in range(HB):
            col = np.diag(gam[h] ** ((255.0 if par == 0 else 127.0) - ii))
            dgk[:, (par * 8 + h) * 128:(par * 8 + h + 1) * 128] = col
    c["DGK_h"], c["DGK_l"] = _hilo(dgk)
    gc = np.tile((gam ** 256.0)[None, :], (128, 1))
    c["GC_h"], c["GC_l"] = _hilo(gc)
    vb = np.full((128, 256), -1e9, np.float32)
    for t in range(2):
        for p in range(128):
            j = 128 * t + p
            lo, hi = max(0, j - 64), min(128, j + 1)
            if lo < hi:
                vb[p, 128 * t + lo:128 * t + hi] = -8.0 * CSH
    c["VBIG"] = vb
    cst = np.zeros((128, 8), np.float32)
    cst[:, 0] = 1e-5
    cst[:, 1] = 1e-6
    cst[:, 2] = 1.0
    cst[:, 3] = 1.0 / 1024.0
    cst[:, 4] = 1.0 / 128.0
    cst[:, 5] = 1e-6
    c["CST"] = cst
    c["ONES32"] = np.ones((1, 512), np.float32)
    c["ONESB"] = np.ones((1, 512), np.float32)
    return c


def _toep(rel_bias):
    rb = np.asarray(rel_bias, np.float32)
    out = np.zeros((128, HA * 256), np.float32)
    i = np.arange(128)
    for h in range(HA):
        for t in range(2):
            for p in range(128):
                off = (128 * t + p) - i
                valid = (off >= 0) & (off <= 64)
                out[p, 256 * h + 128 * t:256 * h + 128 * t + 128] = np.where(
                    valid, rb[np.clip(off + 32, 0, 128), h], 0.0)
    return out


class Sem:
    __slots__ = ("h", "step", "n")

    def __init__(self, handle, step):
        self.h, self.step, self.n = handle, step, 0


class Prog:
    ENGS = ("sp", "pe", "act", "dve", "pool")

    def __init__(self, nc, es):
        self.nc = nc
        self.q = {e: [] for e in self.ENGS}
        self.last_wait = {e: {} for e in self.ENGS}
        mk = lambda n, s: Sem(es.enter_context(nc.semaphore(n)), s)
        self.dL = mk("dL", 16)
        self.dC = mk("dC", 16)
        self.dU = [mk("dU%d" % i, 16) for i in range(4)]
        self.dH = [mk("dH%d" % i, 16) for i in range(4)]
        self.own = {"sp": mk("sS", 1), "pe": mk("sP", 1), "act": mk("sA", 1),
                    "dve": mk("sV", 1), "pool": mk("sG", 1)}
        self.lastc = {e: None for e in self.ENGS}
        self.uncapped = {e: False for e in self.ENGS}

    def do(self, eng, meth, *args, inc=None, **kw):
        if eng in ("act", "dve", "pool") and meth != "dma_start":
            self.wait(eng, self.own[eng])
            if inc is None:
                inc = self.own[eng]
        if inc is not None:
            inc.n += inc.step
        cell = [inc]
        def thunk(h, meth=meth, args=args, kw=kw, cell=cell):
            ins = getattr(h, meth)(*args, **kw)
            if cell[0] is not None:
                ins.then_inc(cell[0].h, cell[0].step)
        self.q[eng].append(thunk)
        if meth != "dma_start":
            if meth == "matmul" and kw.get("stop") is False:
                self.lastc[eng] = None
                self.uncapped[eng] = True
            else:
                self.lastc[eng] = cell
                self.uncapped[eng] = inc is None
        return inc.n if inc is not None else None

    def inc(self, eng, sem):
        sem.n += 1
        c = self.lastc[eng]
        if c is not None and c[0] is None:
            c[0] = sem
        else:
            def thunk(h, sem=sem):
                h.nop().then_inc(sem.h, 1)
            self.q[eng].append(thunk)
            self.lastc[eng] = None
        self.uncapped[eng] = False
        return sem.n

    def wait(self, eng, sem, val=None):
        if val is None:
            val = sem.n
        if val <= 0 or self.last_wait[eng].get(sem, 0) >= val:
            return
        self.last_wait[eng][sem] = val
        def thunk(h, sem=sem, val=val):
            h.wait_ge(sem.h, val)
        self.q[eng].append(thunk)

    def barrier(self, dma=True):
        for e in ("pe", "act", "dve", "pool"):
            if self.uncapped[e]:
                self.inc(e, self.own[e])
        for e in self.ENGS:
            for o in ("pe", "act", "dve", "pool"):
                self.wait(e, self.own[o])
            if dma:
                self.wait(e, self.dL)
                self.wait(e, self.dC)

    def finalize(self, block):
        block.sync(lambda h: [f(h) for f in self.q["sp"]])
        block.tensor(lambda h: [f(h) for f in self.q["pe"]])
        block.scalar(lambda h: [f(h) for f in self.q["act"]])
        block.vector(lambda h: [f(h) for f in self.q["dve"]])
        block.gpsimd(lambda h: [f(h) for f in self.q["pool"]])


def _build(nc, cfg):
    stages = list(cfg["stages"])
    dump = set(cfg.get("dump", ()))

    def din(name, shape):
        return nc.dram_tensor(name, list(shape), F32, kind="ExternalInput").ap()

    xT = din("xT", [D, S])
    aW = {k: din("a_" + k, [3 * D, D] if k == "wg1" else [D, D]) for k in
          ["wq_lin", "wk_lin", "wv_lin", "wq_loc", "wk_loc", "wv_loc",
           "wo", "wg1", "wg2"]}
    aB = {k: din("a_" + k, [D]) for k in
          ["bq_lin", "bk_lin", "bv_lin", "bq_loc", "bk_loc", "bv_loc",
           "bo", "bg1", "bg2", "ln_g", "ln_b"]}
    TOEP = din("TOEP", [128, HA * 256])
    fWt = din("f_Wt", [D, D])
    fB = din("f_b", [D])
    fVr = din("f_Vr", [D, RK])
    fCt = din("f_Ct", [RK, D])
    bWq = din("b_Wq", [2, HB, D, 128])
    bWk = din("b_Wk", [2, HB, D, 128])
    bWv = din("b_Wv", [2, HB, D, 128])
    bWg = din("b_Wg", [2, D, D])
    bWo = din("b_Wo", [2, D, D])
    bw1 = din("b_w1", [2, D, FF])
    bw2 = din("b_w2", [2, FF, D])
    bV = {k: din("b_" + k, [2, D]) for k in
          ["gn_scale", "gn_bias", "ln1_scale", "ln1_bias",
           "ln2_scale", "ln2_bias"]}
    CN = {k: din("C_" + k, v.shape) for k, v in _host_consts().items()}
    yT = nc.dram_tensor("yT", [128, DC * S], F32, kind="ExternalOutput").ap()
    r_sp = nc.dram_tensor("r_spill", [128, DC * S], F32, kind="Internal").ap()
    xb_sp = nc.dram_tensor("xb_spill", [128, DC * S], F32, kind="Internal").ap()
    lin_sp = nc.dram_tensor("lin_spill", [128, DC * S], BF16, kind="Internal").ap()
    tab_sp = nc.dram_tensor("tab_cache", [128, 4 * S], BF16, kind="Internal").ap()
    w1c = nc.dram_tensor("w1_cache", [2, 128, 32 * 1024], BF16, kind="Internal").ap()
    dbg_t = {t: nc.dram_tensor("dbg_" + t, [128, DC * S], F32,
                               kind="ExternalOutput").ap() for t in dump}

    es = contextlib.ExitStack()
    with es:
        P = Prog(nc, es)
        dL, dC = P.dL, P.dC
        sP, sA, sV, sG = P.own["pe"], P.own["act"], P.own["dve"], P.own["pool"]

        _uid = [0]

        def _un(name):
            _uid[0] += 1
            return "%s_%d" % (name, _uid[0])

        def sb(stack, name, shape, dt=F32, side=None):
            return stack.enter_context(
                nc.sbuf_tensor(_un(name), shape, dt, side=side))

        def psum(stack, name, shape, dt=F32):
            return stack.enter_context(nc.psum_tensor(_un(name), shape, dt))

        rowMU = sb(es, "rowMU", [1, S])
        rowSC = sb(es, "rowSC", [1, S])
        cst = sb(es, "cst", [128, 8])
        ones32 = sb(es, "ones32", [1, 512])
        onesb = sb(es, "onesb", [1, 512], BF16)
        kcol32 = sb(es, "kcol32", [128, 1])
        kcolb = sb(es, "kcolb", [128, 1], BF16)
        idb = sb(es, "idb", [128, 128], BF16)
        idf8 = sb(es, "idf8", [8, 8])

        P.do("sp", "dma_start", out=cst[:], in_=CN["CST"], inc=dL)
        P.do("sp", "dma_start", out=ones32[:], in_=CN["ONES32"], inc=dL)
        P.do("sp", "dma_start", out=idf8[:], in_=CN["IDF"][0:8, 0:8], inc=dL)
        P.do("pool", "dma_start", out=onesb[:], in_=CN["ONESB"], inc=dC)
        P.do("pool", "dma_start", out=idb[:], in_=CN["IDF"], inc=dC)
        P.barrier()
        P.do("dve", "memset", kcol32[:], 1.0 / 1024.0)
        P.do("dve", "memset", kcolb[:], 1.0 / 1024.0, inc=sV)
        P.barrier()

        EPS5 = cst[:, 0:1]
        EPS6 = cst[:, 1:2]



        def load_bf16(dst_ap, src_ap):
            return P.do("pool", "dma_start", out=dst_ap, in_=src_ap, inc=dC)

        def load_f32(dst_ap, src_ap):
            return P.do("sp", "dma_start", out=dst_ap, in_=src_ap, inc=dL)

        def w_fm(dst, src2d):
            nch = src2d.shape[0] // 128
            return load_bf16(dst[:].rearrange("p (c n) -> p c n", c=nch),
                             src2d.rearrange("(c p) n -> p c n", p=128))

        def row_bf16(dst, vec):
            return load_bf16(dst, vec.rearrange("(o d) -> o d", o=1))

        def ln_stats(src_fm, mu_row, sc_row, eps_col, dbl=False, sc2_row=None):
            P.barrier(dma=False)
            with contextlib.ExitStack() as st:
                pr = [psum(st, "lnp%d" % i, [1, 512], F32) for i in range(4)]
                sq = sb(st, "lnsq", [128, DC * S], BF16)
                msq = sb(st, "lnmsq", [1, S])
                t1 = sb(st, "lnt1", [1, S])
                uu = sb(st, "lnu", [1, S])
                cast_done = {}
                for c in range(DC):
                    if c % 2 == 0:
                        P.do("dve", "tensor_copy", sq[:, c * S:(c + 1) * S],
                             src_fm[:, c * S:(c + 1) * S], inc=sV)
                        cast_done[c] = (sV, sV.n)
                    else:
                        P.do("act", "copy", sq[:, c * S:(c + 1) * S],
                             src_fm[:, c * S:(c + 1) * S], inc=sA)
                        cast_done[c] = (sA, sA.n)
                for c in range(DC):
                    P.wait("pe", cast_done[c][0], cast_done[c][1])
                    for sl in range(4):
                        P.do("pe", "matmul", pr[sl][0:1, :], kcolb[:],
                             sq[:, c * S + 512 * sl:c * S + 512 * (sl + 1)],
                             start=(c == 0), stop=(c == DC - 1),
                             inc=(sP if (c == DC - 1 and sl == 3) else None))
                P.wait("act", sP)
                for sl in range(4):
                    P.do("act", "copy", mu_row[:, 512 * sl:512 * (sl + 1)],
                         pr[sl][0:1, :], inc=(sA if sl == 3 else None))
                mu_done = sA.n
                P.wait("dve", sP)
                sq_done = {}
                for c in range(DC):
                    P.do("dve", "tensor_tensor", sq[:, c * S:(c + 1) * S],
                         sq[:, c * S:(c + 1) * S], sq[:, c * S:(c + 1) * S],
                         ALU.mult, inc=sV)
                    sq_done[c] = sV.n
                    if c == 1:
                        P.wait("dve", sA, mu_done)
                        P.do("dve", "tensor_tensor", t1[0:1, :], mu_row,
                             mu_row, ALU.mult, inc=sV)
                P.wait("pe", sA, mu_done)
                for c in range(DC):
                    P.wait("pe", sV, sq_done[c])
                    for sl in range(4):
                        P.do("pe", "matmul", pr[sl][0:1, :], kcolb[:],
                             sq[:, c * S + 512 * sl:c * S + 512 * (sl + 1)],
                             start=(c == 0), stop=(c == DC - 1),
                             inc=(sP if (c == DC - 1 and sl == 3) else None))
                P.wait("act", sP)
                for sl in range(4):
                    P.do("act", "copy", msq[0:1, 512 * sl:512 * (sl + 1)],
                         pr[sl][0:1, :], inc=(sA if sl == 3 else None))
                P.wait("dve", sA)
                P.do("dve", "tensor_tensor", t1[0:1, :], msq[0:1, :], t1[0:1, :],
                     ALU.subtract, inc=sV)
                P.wait("act", sV)
                P.do("act", "activation", uu[0:1, :], t1[0:1, :], AF.Ln,
                     bias=eps_col[0:1, :])
                P.do("act", "activation", sc_row, uu[0:1, :],
                     AF.Exp, scale=-0.5, inc=sA)
                if dbl:
                    P.wait("dve", sA)
                    P.do("dve", "tensor_tensor", uu[0:1, :], sc_row, sc_row,
                         ALU.mult)
                    P.do("dve", "tensor_tensor", uu[0:1, :], t1[0:1, :],
                         uu[0:1, :], ALU.mult, inc=sV)
                    P.wait("act", sV)
                    P.do("act", "activation", uu[0:1, :], uu[0:1, :], AF.Ln,
                         bias=eps_col[0:1, :])
                    P.do("act", "activation", uu[0:1, :], uu[0:1, :], AF.Exp,
                         scale=-0.5, inc=sA)
                    P.wait("dve", sA)
                    P.do("dve", "tensor_tensor", sc2_row, sc_row, uu[0:1, :],
                         ALU.mult, inc=sV)
            P.barrier(dma=False)

        def bcast(src_row, width, dst32=None, dstb=None, plus1=False):
            P.barrier(dma=False)
            with contextlib.ExitStack() as st:
                pb = [psum(st, "bcp%d" % i, [128, 512], F32) for i in range(2)]
                rb16 = sb(st, "bcr", [1, width], BF16)
                P.do("act", "copy", rb16[0:1, :], src_row[:, 0:width], inc=sA)
                P.wait("pe", sA)
                nsl = (width + 511) // 512
                edone = [0, 0]
                for sl in range(nsl):
                    w = min(512, width - 512 * sl)
                    bank = sl % 2
                    P.wait("pe", sA, edone[bank])
                    P.do("pe", "matmul", pb[bank][:, 0:w], onesb[0:1, 0:128],
                         rb16[0:1, 512 * sl:512 * sl + w], start=True,
                         stop=not plus1)
                    if plus1:
                        P.do("pe", "matmul", pb[bank][:, 0:w], onesb[0:1, 0:128],
                             onesb[0:1, 0:w], start=False, stop=True)
                    P.inc("pe", sP)
                    P.wait("act", sP)
                    if dst32 is not None:
                        P.do("act", "copy", dst32[:, 512 * sl:512 * sl + w],
                             pb[bank][:, 0:w],
                             inc=(None if dstb is not None else sA))
                    if dstb is not None:
                        P.do("act", "copy", dstb[:, 512 * sl:512 * sl + w],
                             pb[bank][:, 0:w], inc=sA)
                    edone[bank] = sA.n
            P.barrier(dma=False)

        def bcast2(mu_src, sc_src, ns2, bcA, bcBb, bcB=None):
            P.barrier(dma=False)
            with contextlib.ExitStack() as st:
                pb = [psum(st, "bcp%d" % i, [128, 512], F32) for i in range(2)]
                rb16 = sb(st, "bcr2", [1, 2 * ns2], BF16)
                P.do("act", "copy", rb16[0:1, 0:ns2], mu_src, inc=sA)
                mu_cast = sA.n
                P.do("dve", "tensor_copy", rb16[0:1, ns2:2 * ns2], sc_src,
                     inc=sV)
                sc_cast = sV.n
                nsl = (ns2 + 511) // 512
                edA, edV = 0, 0
                for sl in range(nsl):
                    w = min(512, ns2 - 512 * sl)
                    P.wait("pe", sA, mu_cast)
                    P.wait("pe", sA, edA)
                    P.do("pe", "matmul", pb[0][:, 0:w], onesb[0:1, 0:128],
                         rb16[0:1, 512 * sl:512 * sl + w],
                         start=True, stop=True, inc=sP)
                    P.wait("act", sP)
                    P.do("act", "copy", bcA[:, 512 * sl:512 * sl + w],
                         pb[0][:, 0:w], inc=sA)
                    edA = sA.n
                    P.wait("pe", sV, sc_cast)
                    P.wait("pe", sV, edV)
                    if bcB is not None:
                        P.wait("pe", sA, edA)
                    P.do("pe", "matmul", pb[1][:, 0:w], onesb[0:1, 0:128],
                         rb16[0:1, ns2 + 512 * sl:ns2 + 512 * sl + w],
                         start=True, stop=True, inc=sP)
                    P.wait("dve", sP)
                    P.do("dve", "tensor_copy",
                         bcBb[:, 512 * sl:512 * sl + w], pb[1][:, 0:w],
                         inc=sV)
                    edV = sV.n
                    if bcB is not None:
                        P.wait("act", sP)
                        P.do("act", "copy", bcB[:, 512 * sl:512 * sl + w],
                             pb[1][:, 0:w], inc=sA)
                        edA = sA.n
            P.barrier(dma=False)

        def vec_fm(stack, name, src8x128):
            t8 = sb(stack, name + "_r", [8, 128])
            out = sb(stack, name, [128, 8])
            load_f32(t8[:], src8x128)
            P.barrier(dma=False)
            P.wait("pe", P.dL)
            with contextlib.ExitStack() as st:
                pp = psum(st, name + "_p", [128, 8], F32)
                P.do("pe", "transpose", pp[:], t8[:], idf8[:], inc=sP)
                P.wait("dve", sP)
                P.do("dve", "tensor_copy", out[:], pp[:], inc=sV)
            P.barrier(dma=False)
            return out

        def apply_ln(src_fm, dst, mu_row, sc_row, gcol=None, bcol=None,
                     s0=0, ns=S, dst_f32=False, src_dram=None):
            with contextlib.ExitStack() as st:
                bcA = sb(st, "bcAl", [128, ns])
                bcB = sb(st, "bcBl", [128, ns]) if dst_f32 else None
                bcBb = sb(st, "bcBbl", [128, ns], BF16)
                if dst_f32:
                    bcast(mu_row[:, s0:s0 + ns], ns, dst32=bcA)
                    bcast(sc_row[:, s0:s0 + ns], ns, dstb=bcBb, dst32=bcB)
                else:
                    bcast2(mu_row[:, s0:s0 + ns], sc_row[:, s0:s0 + ns], ns,
                           bcA, bcBb)
                t = sb(st, "apl_t", [128, ns], F32 if dst_f32 else BF16)
                tmp = sb(st, "apl_s", [128, ns]) if src_dram is not None else None
                for c in range(DC):
                    if src_dram is not None:
                        P.wait("sp", sG)
                        P.wait("sp", sV)
                        P.do("sp", "dma_start", out=tmp[:],
                             in_=src_dram[:, c * S + s0:c * S + s0 + ns], inc=dL)
                        P.wait("dve", dL)
                        srcap = tmp[:]
                    else:
                        srcap = src_fm[:, c * S + s0:c * S + s0 + ns]
                    P.wait("dve", sG)
                    P.do("dve", "tensor_tensor", t[:], srcap,
                         bcA[:], ALU.subtract, inc=sV)
                    P.wait("pool", sV)
                    dd = dst[:, c * ns:(c + 1) * ns]
                    if dst_f32:
                        P.do("pool", "tensor_tensor", dd, t[:],
                             bcB[:], ALU.mult, inc=sG)
                    elif gcol is None:
                        P.do("pool", "tensor_tensor", dd, t[:],
                             bcBb[:], ALU.mult, inc=sG)
                    else:
                        P.do("pool", "tensor_tensor", t[:], t[:],
                             bcBb[:], ALU.mult, inc=sG)
                        P.wait("dve", sG)
                        P.do("dve", "tensor_scalar", dd, t[:],
                             gcol[:, c:c + 1], bcol[:, c:c + 1],
                             ALU.mult, ALU.add, inc=sV)
            P.barrier(dma=False)

        def dump_sbuf(tag, src_fm):
            if tag in dump:
                P.barrier()
                P.do("sp", "dma_start", out=dbg_t[tag], in_=src_fm[:], inc=dL)
                P.barrier()

        def dump_dram(tag):
            if tag in dump:
                P.barrier()
                with contextlib.ExitStack() as st:
                    t = sb(st, "dmp", [128, S])
                    for c in range(DC):
                        P.do("sp", "dma_start", out=t[:],
                             in_=r_sp[:, c * S:(c + 1) * S], inc=dL)
                        P.wait("sp", dL)
                        P.do("sp", "dma_start",
                             out=dbg_t[tag][:, c * S:(c + 1) * S], in_=t[:],
                             inc=dL)
                        P.wait("sp", dL)
                P.barrier()

        def stage_A():
            with contextlib.ExitStack() as sa:
                h = sb(sa, "h", [128, DC * S], BF16)
                with contextlib.ExitStack() as s0:
                    rb = sb(s0, "rbA", [128, DC * S])
                    load_f32(rb[:].rearrange("p (c s) -> p c s", c=DC),
                             xT.rearrange("(c p) s -> p c s", p=128))
                    P.barrier()
                    rowCS = sb(s0, "rowCS", [1, S])
                    ln_stats(rb, rowMU[0:1, :], rowSC[0:1, :], EPS5, dbl=True, sc2_row=rowCS[0:1, :])
                    with contextlib.ExitStack() as gbs:
                        g_fm = vec_fm(gbs, "a_gfm",
                                      aB["ln_g"].rearrange("(c p) -> c p", c=8))
                        b_fm = vec_fm(gbs, "a_bfm",
                                      aB["ln_b"].rearrange("(c p) -> c p", c=8))
                        apply_ln(rb, h, rowMU[0:1, :], rowCS[0:1, :], gcol=g_fm, bcol=b_fm)
                    P.do("sp", "dma_start", out=r_sp, in_=rb[:], inc=dL)
                    P.barrier()

                with contextlib.ExitStack() as s12:
                    kvb = sb(s12, "kvb", [128, DC * 1025], BF16)
                    with contextlib.ExitStack() as s1:
                        kvacc = sb(s1, "kvacc", [128, DC * 1025])
                        wk = sb(s1, "wk", [128, DC * D], BF16)
                        wv = sb(s1, "wv", [128, DC * D], BF16)
                        brk = sb(s1, "brk", [1, D], BF16)
                        brv = sb(s1, "brv", [1, D], BF16)
                        w_fm(wk, aW["wk_lin"])
                        w_fm(wv, aW["wv_lin"])
                        row_bf16(brk[:], aB["bk_lin"])
                        row_bf16(brv[:], aB["bv_lin"])
                        ktok = sb(s1, "ktok", [128, 4 * D], BF16)
                        vtok = sb(s1, "vtok", [128, 4 * 1025], BF16)
                        scrf = sb(s1, "scrfA", [128, 512])
                        P.do("dve", "memset", kvacc[:], 0.0)
                        P.do("dve", "memset",
                             vtok[:].rearrange("p (t c) -> p t c", c=1025)
                             [:, :, 1024:1025], 1.0, inc=sV)
                        P.barrier()
                        with contextlib.ExitStack() as ps1:
                            pk = [psum(ps1, "pk%d" % i, [128, 512], F32)
                                  for i in range(4)]
                            pv2 = [psum(ps1, "pv%d" % i, [128, 512], F32)
                                   for i in range(2)]
                            pks = psum(ps1, "pks", [128, 512], F32)
                            kfree = {0: 0, 1: 0, 2: 0, 3: 0}
                            accdone = 0
                            for stile in range(4):
                                for tl in range(4):
                                    t0 = stile * 512 + tl * 128
                                    for proj in range(2):
                                        wt, brow = (wk, brk) if proj == 0 else (wv, brv)
                                        for hf in range(2):
                                            bank = proj * 2 + hf
                                            P.wait("pe", sV, kfree[bank])
                                            for c in range(DC):
                                                P.do("pe", "matmul", pk[bank][:],
                                                     h[:, c * S + t0:c * S + t0 + 128],
                                                     wt[:, c * D + hf * 512:c * D + (hf + 1) * 512],
                                                     start=(c == 0), stop=False)
                                            P.do("pe", "matmul", pk[bank][:],
                                                 onesb[0:1, 0:128],
                                                 brow[0:1, hf * 512:(hf + 1) * 512],
                                                 start=False, stop=True, inc=sP)
                                            P.wait("dve", sP)
                                            if proj == 0:
                                                dst = ktok[:, tl * D + hf * 512:tl * D + (hf + 1) * 512]
                                                P.do("dve", "tensor_scalar_min",
                                                     scrf[:], pk[bank][:], 0.0,
                                                     inc=sV)
                                                P.wait("act", sV)
                                                P.do("act", "activation", scrf[:],
                                                     scrf[:], AF.Exp, inc=sA)
                                                P.wait("dve", sA)
                                                P.do("dve", "scalar_tensor_tensor",
                                                     dst, pk[bank][:], 0.0, scrf[:],
                                                     ALU.max, ALU.add, inc=sV)
                                            else:
                                                dst = vtok[:, tl * 1025 + hf * 512:tl * 1025 + (hf + 1) * 512]
                                                P.do("dve", "tensor_copy", dst,
                                                     pk[bank][:], inc=sV)
                                            kfree[bank] = sV.n
                                ktv_ready = sV.n
                                for dt in range(DC):
                                    P.wait("pe", sV, ktv_ready)
                                    P.wait("pe", sV, accdone)
                                    for tl in range(4):
                                        st_, sp_ = (tl == 0), (tl == 3)
                                        for hf in range(2):
                                            P.do("pe", "matmul", pv2[hf][:],
                                                 ktok[:, tl * D + dt * 128:tl * D + dt * 128 + 128],
                                                 vtok[:, tl * 1025 + hf * 512:tl * 1025 + (hf + 1) * 512],
                                                 start=st_, stop=sp_)
                                        P.do("pe", "matmul", pks[:, dt:dt + 1],
                                             ktok[:, tl * D + dt * 128:tl * D + dt * 128 + 128],
                                             vtok[:, tl * 1025 + 1024:tl * 1025 + 1025],
                                             start=st_, stop=sp_,
                                             inc=(sP if sp_ else None))
                                    P.wait("dve", sP)
                                    for hf in range(2):
                                        P.do("dve", "tensor_tensor",
                                             kvacc[:, dt * 1025 + hf * 512:dt * 1025 + (hf + 1) * 512],
                                             kvacc[:, dt * 1025 + hf * 512:dt * 1025 + (hf + 1) * 512],
                                             pv2[hf][:], ALU.add)
                                    P.do("dve", "tensor_tensor",
                                         kvacc[:, dt * 1025 + 1024:dt * 1025 + 1025],
                                         kvacc[:, dt * 1025 + 1024:dt * 1025 + 1025],
                                         pks[:, dt:dt + 1], ALU.add, inc=sV)
                                    accdone = sV.n
                        P.barrier()
                        for dt in range(DC):
                            P.do("act", "copy", kvb[:, dt * 1025:(dt + 1) * 1025],
                                 kvacc[:, dt * 1025:(dt + 1) * 1025],
                                 inc=(sA if dt == DC - 1 else None))
                        P.barrier()

                    with contextlib.ExitStack() as s2:
                        linT = sb(s2, "linT", [128, DC * S], BF16)
                        wq = sb(s2, "wq", [128, DC * D], BF16)
                        brq = sb(s2, "brq", [1, D], BF16)
                        w_fm(wq, aW["wq_lin"])
                        row_bf16(brq[:], aB["bq_lin"])
                        qp = sb(s2, "qp", [128, DC * 512], BF16)
                        ltok = sb(s2, "ltok", [128, D], BF16)
                        scrf = sb(s2, "scrfB", [128, 512])
                        rc = sb(s2, "rc", [128, 1])
                        P.barrier()
                        with contextlib.ExitStack() as ps3:
                            pq = [psum(ps3, "pq%d" % i, [128, 512], F32)
                                  for i in range(2)]
                            pn = [psum(ps3, "pn%d" % i, [128, 512], F32)
                                  for i in range(2)]
                            pd = psum(ps3, "pd", [128, 512], F32)
                            pt = psum(ps3, "ptA", [128, 1024], BF16)
                            qfree = [0, 0]
                            nfree = 0
                            tfree = 0
                            for stile in range(4):
                                s0_ = stile * 512
                                for dco in range(DC):
                                    bank = dco % 2
                                    P.wait("pe", sV, qfree[bank])
                                    for c in range(DC):
                                        P.do("pe", "matmul", pq[bank][:],
                                             wq[:, c * D + dco * 128:c * D + dco * 128 + 128],
                                             h[:, c * S + s0_:c * S + s0_ + 512],
                                             start=(c == 0), stop=False)
                                    P.do("pe", "matmul", pq[bank][:],
                                         brq[0:1, dco * 128:dco * 128 + 128],
                                         onesb[0:1, 0:512], start=False,
                                         stop=True, inc=sP)
                                    P.wait("dve", sP)
                                    P.do("dve", "tensor_scalar_min", scrf[:],
                                         pq[bank][:], 0.0, inc=sV)
                                    P.wait("act", sV)
                                    P.do("act", "activation", scrf[:], scrf[:],
                                         AF.Exp, inc=sA)
                                    P.wait("dve", sA)
                                    P.do("dve", "scalar_tensor_tensor",
                                         qp[:, dco * 512:(dco + 1) * 512],
                                         pq[bank][:], 0.0, scrf[:], ALU.max,
                                         ALU.add, inc=sV)
                                    qfree[bank] = sV.n
                                for tl in range(4):
                                    tg = stile * 4 + tl
                                    P.wait("pe", sV)
                                    P.wait("pe", sV, nfree)
                                    for c in range(DC):
                                        st_, sp_ = (c == 0), (c == DC - 1)
                                        for hf in range(2):
                                            P.do("pe", "matmul", pn[hf][:],
                                                 qp[:, c * 512 + tl * 128:c * 512 + tl * 128 + 128],
                                                 kvb[:, c * 1025 + hf * 512:c * 1025 + (hf + 1) * 512],
                                                 start=st_, stop=sp_)
                                        P.do("pe", "matmul", pd[:, tl:tl + 1],
                                             qp[:, c * 512 + tl * 128:c * 512 + tl * 128 + 128],
                                             kvb[:, c * 1025 + 1024:c * 1025 + 1025],
                                             start=st_, stop=sp_,
                                             inc=(sP if sp_ else None))
                                    P.wait("dve", sP)
                                    P.do("dve", "tensor_scalar_add", rc[:],
                                         pd[:, tl:tl + 1], cst[:, 5:6])
                                    P.do("dve", "reciprocal", rc[:], rc[:])
                                    for hf in range(2):
                                        P.do("dve", "tensor_scalar_mul",
                                             ltok[:, hf * 512:(hf + 1) * 512],
                                             pn[hf][:], rc[:],
                                             inc=(sV if hf == 1 else None))
                                    nfree = sV.n
                                    P.wait("pe", sV)
                                    P.wait("pe", sV, tfree)
                                    for dc2 in range(DC):
                                        P.do("pe", "transpose",
                                             pt[:, dc2 * 128:(dc2 + 1) * 128],
                                             ltok[:, dc2 * 128:(dc2 + 1) * 128],
                                             idb[:],
                                             inc=(sP if dc2 == DC - 1 else None))
                                    P.wait("dve", sP)
                                    P.do("dve", "tensor_copy",
                                         linT[:].rearrange("p (c n) -> p c n", c=DC)
                                         [:, :, tg * 128:tg * 128 + 128],
                                         pt[:].rearrange("p (c n) -> p c n", c=DC),
                                         inc=sV)
                                    tfree = sV.n
                        P.barrier()
                        P.do("sp", "dma_start", out=lin_sp, in_=linT[:], inc=dL)
                        P.barrier()

                winT = sb(sa, "winT", [128, DC * S], BF16)
                with contextlib.ExitStack() as s3:
                    qloc = sb(s3, "qloc", [128, DC * S], BF16)
                    klocT = sb(s3, "klocT", [128, DC * 2112], BF16)
                    vloc = sb(s3, "vloc", [128, 17 * 1040], BF16)
                    vstage = sb(s3, "vstage", [96, 1040], BF16)
                    with contextlib.ExitStack() as ps4:
                        pj = [psum(ps4, "pj%d" % i, [128, 512], F32)
                              for i in range(4)]
                        for wname, bname, kind in (("wq_loc", "bq_loc", "q"),
                                                   ("wk_loc", "bk_loc", "k"),
                                                   ("wv_loc", "bv_loc", "v")):
                            with contextlib.ExitStack() as sw:
                                wt = sb(sw, "wloc", [128, DC * D], BF16)
                                brow = sb(sw, "bloc", [1, D], BF16)
                                w_fm(wt, aW[wname])
                                row_bf16(brow[:], aB[bname])
                                if kind == "k":
                                    P.do("dve", "memset",
                                         klocT[:].rearrange("p (c t) -> p c t", t=2112)
                                         [:, :, 0:32], 0.0)
                                    P.do("dve", "memset",
                                         klocT[:].rearrange("p (c t) -> p c t", t=2112)
                                         [:, :, 2080:2112], 0.0, inc=sV)
                                if kind == "v":
                                    P.do("dve", "memset",
                                         vstage[:].rearrange(
                                             "p (h c) -> p h c", c=65)[:, :, 64:65],
                                         1.0)
                                    P.do("dve", "memset", vloc[0:32, 0:1040], 0.0)
                                    P.do("dve", "memset",
                                         vloc[32:64, 16 * 1040:17 * 1040], 0.0)
                                    P.do("dve", "memset",
                                         vloc[64:128, 16 * 1040:17 * 1040], 0.0)
                                    P.do("dve", "memset",
                                         vloc[:].rearrange("p (t h c) -> p t h c",
                                                           h=16, c=65)[:, :, :, 64:65],
                                         1.0, inc=sV)
                                P.barrier()
                                if kind in ("q", "k"):
                                    pjdone = [0, 0, 0, 0]
                                    for stile in range(4):
                                        s0_ = stile * 512
                                        for dco in range(DC):
                                            bank = dco % 4
                                            P.wait("pe", sA, pjdone[bank])
                                            for c in range(DC):
                                                P.do("pe", "matmul", pj[bank][:],
                                                     wt[:, c * D + dco * 128:c * D + dco * 128 + 128],
                                                     h[:, c * S + s0_:c * S + s0_ + 512],
                                                     start=(c == 0), stop=False)
                                            P.do("pe", "matmul", pj[bank][:],
                                                 brow[0:1, dco * 128:dco * 128 + 128],
                                                 onesb[0:1, 0:512], start=False,
                                                 stop=True, inc=sP)
                                            P.wait("act", sP)
                                            if kind == "q":
                                                dd = qloc[:, dco * S + s0_:dco * S + s0_ + 512]
                                            else:
                                                dd = klocT[:, dco * 2112 + 32 + s0_:
                                                           dco * 2112 + 32 + s0_ + 512]
                                            P.do("act", "copy", dd, pj[bank][:],
                                                 inc=sA)
                                            pjdone[bank] = sA.n
                                else:
                                    for tg in range(17):
                                        if tg == 0:
                                            m0, mm_, pofs = 0, 96, 32
                                        elif tg == 16:
                                            m0, mm_, pofs = 2016, 32, 0
                                        else:
                                            m0, mm_, pofs = tg * 128 - 32, 128, 0
                                        P.wait("pe", sA)
                                        for c in range(DC):
                                            st_ = (c == 0)
                                            for hf in range(2):
                                                P.do("pe", "matmul",
                                                     pj[hf][0:mm_, :],
                                                     h[:, c * S + m0:c * S + m0 + mm_],
                                                     wt[:, c * D + hf * 512:c * D + (hf + 1) * 512],
                                                     start=st_, stop=False)
                                        for hf in range(2):
                                            P.do("pe", "matmul", pj[hf][0:mm_, :],
                                                 onesb[0:1, 0:mm_],
                                                 brow[0:1, hf * 512:(hf + 1) * 512],
                                                 start=False, stop=True,
                                                 inc=(sP if hf == 1 else None))
                                        P.wait("act", sP)
                                        for hf in range(2):
                                            srcp = pj[hf][0:mm_, :].rearrange(
                                                "p (h c) -> p h c", c=64)
                                            if pofs == 0:
                                                dsta = vloc[0:mm_,
                                                            tg * 1040:(tg + 1) * 1040].rearrange(
                                                    "p (h c) -> p h c", c=65)[:, 8 * hf:8 * (hf + 1), 0:64]
                                                P.do("act", "copy", dsta, srcp,
                                                     inc=sA)
                                            else:
                                                dsta = vstage[0:mm_, :].rearrange(
                                                    "p (h c) -> p h c", c=65)[:, 8 * hf:8 * (hf + 1), 0:64]
                                                P.do("act", "copy", dsta, srcp,
                                                     inc=sA)
                                        if pofs != 0:
                                            P.wait("sp", sA)
                                            P.do("sp", "dma_start",
                                                 out=vloc[pofs:pofs + mm_,
                                                          tg * 1040:(tg + 1) * 1040],
                                                 in_=vstage[0:mm_, :], inc=dL)
                                P.barrier()
                    mask8 = sb(s3, "mask8", [128, HA * 256])
                    with contextlib.ExitStack() as sm:
                        vbt = sb(sm, "vbt", [128, 256])
                        load_f32(mask8[:], TOEP)
                        load_f32(vbt[:], CN["VBIG"])
                        P.barrier()
                        for hh in range(HA):
                            P.do("dve", "scalar_tensor_tensor",
                                 mask8[:, hh * 256:(hh + 1) * 256],
                                 mask8[:, hh * 256:(hh + 1) * 256], 8.0, vbt[:],
                                 ALU.mult, ALU.add, inc=sV)
                        P.barrier()
                    scm = sb(s3, "scm", [128, 4 * 256])
                    pexp = sb(s3, "pexp", [128, 4 * 256], BF16)
                    win2 = sb(s3, "win2", [128, 256], BF16)
                    rc2 = sb(s3, "rc2", [128, 4])
                    P.do("dve", "memset", scm[:], -1e9, inc=sV)
                    P.barrier()
                    with contextlib.ExitStack() as ps5:
                        psc = [psum(ps5, "psc%d" % i, [128, 256], F32)
                               for i in range(4)]
                        pw = [psum(ps5, "pw%d" % i, [128, 512], F32)
                              for i in range(3)]
                        ptw = psum(ps5, "ptw", [128, 256], BF16)
                        for i in range(4):
                            P.do("dve", "memset", psc[i][64:128, 128:256], 0.0,
                                 inc=sV)
                        P.barrier(dma=False)
                        scfree = [0, 0, 0, 0]
                        scm_free = [0, 0, 0, 0]
                        pexp_done = [0, 0, 0, 0]
                        wfree = {0: 0, 1: 0, 2: 0}
                        wdone = {0: 0, 1: 0, 2: 0}
                        twfree = [0, 0]
                        for B_ in range(TC):
                            qs = B_ * 128
                            for hp in range(8):
                                wb = pw[hp % 3]
                                wcol = 130 * (hp // 3)
                                for k2 in range(2):
                                    hh = 2 * hp + k2
                                    si = hh % 4
                                    sc = psc[si]
                                    pb = 64 * (hh % 2)
                                    so = 256 * si
                                    P.wait("pe", sV, scfree[si])
                                    lT0 = klocT[pb:pb + 64,
                                                hp * 2112 + qs:hp * 2112 + qs + 128]
                                    lT1 = klocT[pb:pb + 64,
                                                hp * 2112 + qs + 128:hp * 2112 + qs + 192]
                                    rQ = qloc[pb:pb + 64, hp * S + qs:hp * S + qs + 128]
                                    P.do("pe", "matmul", sc[:, 0:128], lT0, rQ,
                                         start=True, stop=True)
                                    P.do("pe", "matmul", sc[0:64, 128:256], lT1,
                                         rQ, start=True, stop=True, inc=sP)
                                    P.wait("dve", sP)
                                    P.wait("dve", sA, scm_free[si])
                                    P.do("dve", "tensor_tensor",
                                         scm[:, so:so + 256], sc[:],
                                         mask8[:, hh * 256:hh * 256 + 256],
                                         ALU.add, inc=sV)
                                    scfree[si] = sV.n
                                    if B_ == 0:
                                        P.do("dve", "memset",
                                             scm[0:32, so:so + 128], -1e9,
                                             inc=sV)
                                    if B_ == TC - 1:
                                        P.do("dve", "memset",
                                             scm[32:64, so + 128:so + 256],
                                             -1e9, inc=sV)
                                    P.wait("act", sV)
                                    P.wait("act", sP, pexp_done[si])
                                    P.do("act", "activation",
                                         pexp[:, so:so + 256], scm[:, so:so + 256],
                                         AF.Exp, scale=0.125, inc=sA)
                                    scm_free[si] = sA.n
                                    P.wait("pe", sA)
                                    P.wait("pe", sV, wfree[hp % 3])
                                    c0 = wcol + 65 * k2
                                    P.do("pe", "matmul", wb[:, c0:c0 + 65],
                                         pexp[:, so:so + 128],
                                         vloc[:, B_ * 1040 + hh * 65:B_ * 1040 + hh * 65 + 65],
                                         start=True, stop=False)
                                    P.do("pe", "matmul", wb[:, c0:c0 + 65],
                                         pexp[0:64, so + 128:so + 256],
                                         vloc[0:64, (B_ + 1) * 1040 + hh * 65:(B_ + 1) * 1040 + hh * 65 + 65],
                                         start=False, stop=True, inc=sP)
                                    pexp_done[si] = sP.n
                                    wdone[hp % 3] = sP.n
                                wsl = hp % 2
                                P.wait("dve", sP, wdone[hp % 3])
                                rview = wb[:, wcol:wcol + 130].rearrange(
                                    "p (h c) -> p h c", c=65)[:, :, 64:65].rearrange(
                                    "p h o -> p (h o)")
                                P.do("dve", "reciprocal", rc2[:, 2 * wsl:2 * wsl + 2],
                                     rview)
                                P.wait("dve", sP, twfree[wsl])
                                P.do("dve", "tensor_scalar_mul",
                                     win2[:, 128 * wsl:128 * wsl + 64],
                                     wb[:, wcol:wcol + 64], rc2[:, 2 * wsl:2 * wsl + 1])
                                P.do("dve", "tensor_scalar_mul",
                                     win2[:, 128 * wsl + 64:128 * wsl + 128],
                                     wb[:, wcol + 65:wcol + 129],
                                     rc2[:, 2 * wsl + 1:2 * wsl + 2],
                                     inc=sV)
                                wfree[hp % 3] = sV.n
                                P.wait("pe", sV, wfree[hp % 3])
                                P.do("pe", "transpose", ptw[:, 128 * wsl:128 * wsl + 128],
                                     win2[:, 128 * wsl:128 * wsl + 128], idb[:],
                                     inc=sP)
                                twfree[wsl] = sP.n
                                P.wait("dve", sP)
                                P.do("dve", "tensor_copy",
                                     winT[:, hp * S + qs:hp * S + qs + 128],
                                     ptw[:, 128 * wsl:128 * wsl + 128], inc=sV)
                    P.barrier()

                with contextlib.ExitStack() as s4:
                    lsl = sb(s4, "lsl", [128, DC * 512], BF16)
                    ghid = sb(s4, "ghid", [128, DC * S], BF16)

                    def load_lsl(sl):
                        return load_bf16(
                            lsl[:].rearrange("p (c n) -> p c n", c=DC),
                            lin_sp.rearrange("p (c s) -> p c s", c=DC)
                            [:, :, sl * 512:(sl + 1) * 512])

                    with contextlib.ExitStack() as sg1:
                        wg1 = sb(sg1, "wg1", [128, 24 * D], BF16)
                        brg1 = sb(sg1, "brg1", [1, D], BF16)
                        w_fm(wg1, aW["wg1"])
                        row_bf16(brg1[:], aB["bg1"])
                        P.barrier()
                        with contextlib.ExitStack() as ps6:
                            pg = [psum(ps6, "pg%d" % i, [128, 512], F32)
                                  for i in range(2)]
                            for sl in range(4):
                                P.barrier()
                                load_lsl(sl)
                                P.barrier()
                                for dgt in range(DC):
                                    bank = dgt % 2
                                    P.wait("pe", sA)
                                    first = True
                                    for c in range(DC):
                                        P.do("pe", "matmul", pg[bank][:],
                                             wg1[:, c * D + dgt * 128:c * D + dgt * 128 + 128],
                                             lsl[:, c * 512:(c + 1) * 512],
                                             start=first, stop=False)
                                        first = False
                                    for c in range(DC):
                                        P.do("pe", "matmul", pg[bank][:],
                                             wg1[:, (8 + c) * D + dgt * 128:(8 + c) * D + dgt * 128 + 128],
                                             winT[:, c * S + sl * 512:c * S + (sl + 1) * 512],
                                             start=False, stop=False)
                                    for c in range(DC):
                                        P.do("pe", "matmul", pg[bank][:],
                                             wg1[:, (16 + c) * D + dgt * 128:(16 + c) * D + dgt * 128 + 128],
                                             h[:, c * S + sl * 512:c * S + (sl + 1) * 512],
                                             start=False, stop=False)
                                    P.do("pe", "matmul", pg[bank][:],
                                         brg1[0:1, dgt * 128:dgt * 128 + 128],
                                         onesb[0:1, 0:512], start=False,
                                         stop=True, inc=sP)
                                    P.wait("act", sP)
                                    P.do("act", "activation",
                                         ghid[:, dgt * S + sl * 512:dgt * S + (sl + 1) * 512],
                                         pg[bank][:], AF.Relu, inc=sA)
                        P.barrier()
                    cb = sb(s4, "cb", [128, DC * S], BF16)
                    with contextlib.ExitStack() as sg2:
                        wg2 = sb(sg2, "wg2", [128, DC * D], BF16)
                        brg2 = sb(sg2, "brg2", [1, D], BF16)
                        gsig = sb(sg2, "gsig", [128, 512], BF16)
                        d1 = sb(sg2, "d1", [128, 512], BF16)
                        w_fm(wg2, aW["wg2"])
                        row_bf16(brg2[:], aB["bg2"])
                        P.barrier()
                        with contextlib.ExitStack() as ps7:
                            pg = [psum(ps7, "pg2%d" % i, [128, 512], F32)
                                  for i in range(2)]
                            gdone = 0
                            for sl in range(4):
                                P.barrier()
                                load_lsl(sl)
                                P.barrier()
                                for dt in range(DC):
                                    bank = dt % 2
                                    P.wait("pe", sA)
                                    for c in range(DC):
                                        P.do("pe", "matmul", pg[bank][:],
                                             wg2[:, c * D + dt * 128:c * D + dt * 128 + 128],
                                             ghid[:, c * S + sl * 512:c * S + (sl + 1) * 512],
                                             start=(c == 0), stop=False)
                                    P.do("pe", "matmul", pg[bank][:],
                                         brg2[0:1, dt * 128:dt * 128 + 128],
                                         onesb[0:1, 0:512], start=False,
                                         stop=True, inc=sP)
                                    P.wait("act", sP)
                                    P.wait("act", sV, gdone)
                                    P.do("act", "activation", gsig[:],
                                         pg[bank][:], AF.Sigmoid, inc=sA)
                                    co = dt * S + sl * 512
                                    P.wait("pool", sV, gdone)
                                    P.do("pool", "tensor_tensor", d1[:],
                                         lsl[:, dt * 512:(dt + 1) * 512],
                                         winT[:, co:co + 512], ALU.subtract,
                                         inc=sG)
                                    P.wait("dve", sA)
                                    P.wait("dve", sG)
                                    P.do("dve", "tensor_tensor", gsig[:],
                                         gsig[:], d1[:], ALU.mult)
                                    P.do("dve", "tensor_tensor",
                                         cb[:, co:co + 512], gsig[:],
                                         winT[:, co:co + 512], ALU.add, inc=sV)
                                    gdone = sV.n
                        P.barrier()
                    with contextlib.ExitStack() as s5:
                        wo = sb(s5, "wo", [128, DC * D], BF16)
                        bro_ = sb(s5, "bro", [1, D], BF16)
                        of1 = sb(s5, "of1", [1, S], BF16)
                        rt = sb(s5, "rt", [128, S])
                        w_fm(wo, aW["wo"])
                        row_bf16(bro_[:], aB["bo"])
                        sc1p = sb(s5, "sc1p", [128, S])
                        P.do("dve", "tensor_tensor", rowMU[0:1, :],
                             rowMU[0:1, :], rowSC[0:1, :], ALU.mult)
                        P.do("dve", "tensor_scalar_mul", of1[:],
                             rowMU[0:1, :], -1.0, inc=sV)
                        P.barrier()
                        bcast(rowSC[0:1, :], S, dst32=sc1p, plus1=True)
                        with contextlib.ExitStack() as ps8:
                            po = [psum(ps8, "po%d" % i, [128, 512], F32)
                                  for i in range(2)]
                            for dt in range(DC):
                                P.wait("sp", sV)
                                P.wait("sp", dL)
                                P.do("sp", "dma_start", out=rt[:],
                                     in_=r_sp[:, dt * S:(dt + 1) * S], inc=dL)
                                for sl in range(4):
                                    bank = sl % 2
                                    P.wait("pe", sV)
                                    for c in range(DC):
                                        P.do("pe", "matmul", po[bank][:],
                                             wo[:, c * D + dt * 128:c * D + dt * 128 + 128],
                                             cb[:, c * S + sl * 512:c * S + (sl + 1) * 512],
                                             start=(c == 0), stop=False)
                                    P.do("pe", "matmul", po[bank][:],
                                         bro_[0:1, dt * 128:dt * 128 + 128],
                                         onesb[0:1, 0:512], start=False,
                                         stop=False)
                                    P.do("pe", "matmul", po[bank][:],
                                         onesb[0:1, 0:128],
                                         of1[0:1, sl * 512:(sl + 1) * 512],
                                         start=False, stop=True, inc=sP)
                                    P.wait("dve", dL)
                                    P.wait("dve", sP)
                                    cs_ = slice(sl * 512, (sl + 1) * 512)
                                    P.do("dve", "tensor_tensor", rt[:, cs_],
                                         rt[:, cs_], sc1p[:, cs_], ALU.mult)
                                    P.do("dve", "tensor_tensor", rt[:, cs_],
                                         rt[:, cs_], po[bank][:], ALU.add,
                                         inc=sV)
                                P.wait("sp", sV)
                                P.do("sp", "dma_start",
                                     out=r_sp[:, dt * S:(dt + 1) * S],
                                     in_=rt[:], inc=dL)
                            P.barrier()
            dump_dram("A")

        def stage_ffn(rbuf, tag):
            ln_stats(rbuf, rowMU[0:1, :], rowSC[0:1, :], EPS5)
            with contextlib.ExitStack() as sf:
                xnb = sb(sf, "xnb" + tag, [128, DC * S], BF16)
                apply_ln(rbuf, xnb, rowMU[0:1, :], rowSC[0:1, :])
                wt = sb(sf, "fwt" + tag, [128, DC * D], BF16)
                vr = sb(sf, "fvr" + tag, [128, DC * RK], BF16)
                ct = sb(sf, "fct" + tag, [RK, D], BF16)
                fbr = sb(sf, "fbr" + tag, [1, D], BF16)
                trb = sb(sf, "trb" + tag, [RK, S], BF16)
                w_fm(wt, fWt)
                load_bf16(vr[:].rearrange("p (c n) -> p c n", c=DC),
                          fVr.rearrange("(c p) n -> p c n", p=128))
                load_bf16(ct[:], fCt)
                row_bf16(fbr[:], fB)
                if tag == "B":
                    for li2 in range(2):
                        for cc in range(DC):
                            P.do("pool", "dma_start",
                                 out=w1c[li2].rearrange(
                                     "p (t c f) -> p t c f",
                                     t=32, c=DC)[:, :, cc, :],
                                 in_=bw1[li2].rearrange(
                                     "(c p) (t f) -> c p t f",
                                     p=128, t=32)[cc],
                                 inc=P.dU[0])
                    P.w1c_built = P.dU[0].n
                P.barrier()
                with contextlib.ExitStack() as ps:
                    pf = [psum(ps, "pf%d" % i, [128, 512], F32) for i in range(2)]
                    for sl in range(4):
                        bank = sl % 2
                        P.wait("pe", sA)
                        for c in range(DC):
                            P.do("pe", "matmul", pf[bank][0:RK, :],
                                 vr[:, c * RK:(c + 1) * RK],
                                 xnb[:, c * S + sl * 512:c * S + (sl + 1) * 512],
                                 start=(c == 0), stop=(c == DC - 1),
                                 inc=(sP if c == DC - 1 else None))
                        P.wait("act", sP)
                        P.do("act", "copy", trb[:, sl * 512:(sl + 1) * 512],
                             pf[bank][0:RK, :], inc=sA)
                    P.barrier()
                    bfree = [0, 0]
                    for dt in range(DC):
                        for sl in range(4):
                            bank = sl % 2
                            P.wait("pe", sV, bfree[bank])
                            for c in range(DC):
                                P.do("pe", "matmul", pf[bank][:],
                                     wt[:, c * D + dt * 128:c * D + dt * 128 + 128],
                                     xnb[:, c * S + sl * 512:c * S + (sl + 1) * 512],
                                     start=(c == 0), stop=False)
                            P.do("pe", "matmul", pf[bank][:],
                                 ct[0:RK, dt * 128:dt * 128 + 128],
                                 trb[0:RK, sl * 512:(sl + 1) * 512],
                                 start=False, stop=False)
                            P.do("pe", "matmul", pf[bank][:],
                                 fbr[0:1, dt * 128:dt * 128 + 128],
                                 onesb[0:1, 0:512], start=False, stop=True,
                                 inc=sP)
                            P.wait("dve", sP)
                            co = dt * S + sl * 512
                            P.do("dve", "tensor_tensor", rbuf[:, co:co + 512],
                                 rbuf[:, co:co + 512], pf[bank][:], ALU.add,
                                 inc=sV)
                            bfree[bank] = sV.n
                    P.barrier()

        def stage_C(rbuf, rstack):
            if not hasattr(P, "w1c_built"):
                for li2 in range(2):
                    for cc in range(DC):
                        P.do("pool", "dma_start",
                             out=w1c[li2].rearrange(
                                 "p (t c f) -> p t c f",
                                 t=32, c=DC)[:, :, cc, :],
                             in_=bw1[li2].rearrange(
                                 "(c p) (t f) -> c p t f", p=128, t=32)[cc],
                             inc=P.dU[0])
                P.w1c_built = P.dU[0].n
            ln_stats(rbuf, rowMU[0:1, :], rowSC[0:1, :], EPS5)
            P.do("sp", "dma_start", out=r_sp, in_=rbuf[:], inc=dL)
            P.barrier()
            rstack.close()
            with contextlib.ExitStack() as st:
                hi = sb(st, "tb_h", [128, S])
                lo = sb(st, "tb_l", [128, S])
                tt = sb(st, "tb_t", [128, S], BF16)
                for ix, nm in enumerate(("XCQ", "XSQ", "XCK", "XSK")):
                    load_f32(hi[:], CN[nm + "_h"])
                    load_f32(lo[:], CN[nm + "_l"])
                    P.barrier()
                    P.do("dve", "tensor_tensor", tt[:], hi[:], lo[:],
                         ALU.add, inc=sV)
                    P.barrier()
                    P.do("sp", "dma_start", out=tab_sp[:, ix * S:(ix + 1) * S],
                         in_=tt[:], inc=dL)
                    P.barrier()
            xbst = contextlib.ExitStack()
            xb = sb(xbst, "xb", [128, DC * S], side="right")
            apply_ln(None, xb, rowMU[0:1, :], rowSC[0:1, :], dst_f32=True, src_dram=r_sp)

            for li in range(2):
                L0 = contextlib.ExitStack()
                xnb = sb(L0, "xnbC", [128, DC * S], BF16)
                ygn = sb(L0, "ygn", [128, TC * D], BF16)
                ln_stats(xb, rowMU[0:1, :], rowSC[0:1, :], EPS6)
                with contextlib.ExitStack() as sv_:
                    s1c = vec_fm(sv_, "l1s",
                                 bV["ln1_scale"][li].rearrange("(c p) -> c p", c=8))
                    b1c = vec_fm(sv_, "l1b",
                                 bV["ln1_bias"][li].rearrange("(c p) -> c p", c=8))
                    apply_ln(xb, xnb, rowMU[0:1, :], rowSC[0:1, :], gcol=s1c, bcol=b1c)
                P.do("sp", "dma_start", out=xb_sp, in_=xb[:], inc=dL)
                P.barrier()
                xbst.close()

                L1 = contextlib.ExitStack()
                tabs_t = sb(L1, "tabs_t", [128, 4 * 512], BF16)
                dct = sb(L1, "dct", [128, HB * 128], BF16)
                dct2 = sb(L1, "dct2", [128, HB * 128], BF16)
                gqt = sb(L1, "gqt", [128, HB * 128], BF16)
                gq2t = sb(L1, "gq2t", [128, HB * 128], BF16)
                gkt = sb(L1, "gkt", [128, 2 * HB])
                gct = sb(L1, "gct", [128, HB])
                rott = sb(L1, "rott", [128, 128], BF16)
                gnS = sb(L1, "gnS", [128, D])
                gnB = sb(L1, "gnB", [128, D])
                Sf = sb(L1, "Sf", [128, HB * 128])
                Sb2 = sb(L1, "Sb2", [128, HB * 128], BF16)
                load_bf16(rott[:], CN["ROTT"])
                for ix in range(4):
                    P.do("sp", "dma_start",
                         out=tabs_t[:, ix * 512:(ix + 1) * 512],
                         in_=tab_sp[:, ix * S:ix * S + 512], inc=dL)
                tab_ld = dL.n
                with contextlib.ExitStack() as stx:
                    hi = sb(stx, "ch", [128, HB * 128])
                    lo = sb(stx, "cl", [128, HB * 128])
                    for dst, nm in ((dct, "DCT"), (gqt, "GQ"),
                                    (dct2, "DCT2"), (gq2t, "GQ2")):
                        load_f32(hi[:], CN[nm + "_h"])
                        load_f32(lo[:], CN[nm + "_l"])
                        P.barrier()
                        P.do("dve", "tensor_tensor", dst[:], hi[:], lo[:],
                             ALU.add, inc=sV)
                        P.barrier()
                    for dst, nm in ((gkt[:, 0:HB], "GK"),
                                    (gkt[:, HB:2 * HB], "GK255"),
                                    (gct[:], "GC")):
                        load_f32(hi[:, 0:HB], CN[nm + "_h"])
                        load_f32(lo[:, 0:HB], CN[nm + "_l"])
                        P.barrier()
                        P.do("dve", "tensor_tensor", dst, hi[:, 0:HB],
                             lo[:, 0:HB], ALU.add, inc=sV)
                        P.barrier()
                with contextlib.ExitStack() as sgn:
                    gnr1 = sb(sgn, "gnr1", [1, D])
                    gnr2 = sb(sgn, "gnr2", [1, D])
                    load_f32(gnr1[:], bV["gn_scale"][li:li + 1])
                    load_f32(gnr2[:], bV["gn_bias"][li:li + 1])
                    P.barrier()
                    bcast(gnr1[0:1, :], D, dst32=gnS)
                    bcast(gnr2[0:1, :], D, dst32=gnB)
                P.do("dve", "memset", Sf[:], 0.0)
                P.do("dve", "memset", Sb2[:], 0.0, inc=sV)
                P.barrier()
                with contextlib.ExitStack() as c1:
                    wqq = sb(c1, "wqq", [128, HB * D], BF16)
                    wkk = sb(c1, "wkk", [128, HB * D], BF16)
                    wvv = sb(c1, "wvv", [128, HB * D], BF16)
                    for dst, srcw in ((wqq, bWq), (wkk, bWk), (wvv, bWv)):
                        load_bf16(dst[:].rearrange("p (h c e) -> p h c e",
                                                   h=HB, c=DC),
                                  srcw[li].rearrange("h (c p) e -> p h c e",
                                                     p=128))
                    qf = sb(c1, "qf", [128, 3 * 512], BF16)
                    qx = sb(c1, "qx", [128, HB * 512], BF16)
                    kx = sb(c1, "kx", [128, HB * 512], BF16)
                    vtk = sb(c1, "vtk", [128, 4 * D], BF16)
                    kgt = sb(c1, "kgt", [128, 2 * D], BF16)
                    qg = sb(c1, "qg", [128, 2 * D], BF16)
                    scb = sb(c1, "scb", [128, 3 * D], BF16)
                    ret = sb(c1, "ret", [128, 2 * D], BF16)
                    sqb = sb(c1, "sqb", [128, D], BF16)
                    st8 = sb(c1, "st8", [128, 4 * HB])
                    m1s = sb(c1, "m1s", [128, 512], BF16)
                    m2s_free = [0, 0]
                    P.barrier()
                    TB4 = [tabs_t[:, i * 512:(i + 1) * 512] for i in range(4)]
                    with contextlib.ExitStack() as cp:
                        pbank = [psum(cp, "cb%d" % i, [128, 512], F32)
                                 for i in range(6)]
                        pp = pbank[0:2]
                        prr = psum(cp, "prr", [128, 512], F32)
                        psc2 = pbank[0:2]
                        pou = pbank[2:4]
                        psd = pbank[4:6]
                        ptc = psum(cp, "ptc", [128, 1024], BF16)
                        scb_evac = [[0, 0], [0, 0]]
                        ret_evac = [[0, 0], [0, 0]]
                        scbj_evac = 0
                        supd_done = 0
                        kgt_evac = 0
                        gn_pool_done = 0
                        prj_evac = 0
                        qg_guard = 0
                        pending_gn = []

                        def flush_gn():
                            nonlocal gn_pool_done
                            for fn in pending_gn:
                                fn()
                            pending_gn.clear()
                            gn_pool_done = sG.n
                        for stile in range(4):
                            s0_ = stile * 512
                            P.wait("pool", dL, tab_ld)
                            P.wait("dve", dL, tab_ld)
                            for sel, (ww, dstx, tc_, ts_) in enumerate(
                                    ((wqq, qx, TB4[0], TB4[1]),
                                     (wkk, kx, TB4[2], TB4[3]))):
                                for h_ in range(HB):
                                    bank = h_ % 2
                                    qsl = (sel * 8 + h_) % 3
                                    P.wait("pe", sV, scb_evac[1][bank])
                                    P.wait("pe", sA, prj_evac)
                                    if sel == 0 and h_ == 0:
                                        P.wait("pe", sV, scbj_evac)
                                    for c in range(DC):
                                        P.do("pe", "matmul", pp[bank][:],
                                             ww[:, (h_ * 8 + c) * 128:(h_ * 8 + c) * 128 + 128],
                                             xnb[:, c * S + s0_:c * S + s0_ + 512],
                                             start=(c == 0), stop=(c == DC - 1),
                                             inc=(sP if c == DC - 1 else None))
                                    P.wait("act", sP)
                                    qfs = qf[:, qsl * 512:(qsl + 1) * 512]
                                    P.wait("act", sG)
                                    P.wait("act", sV)
                                    P.do("act", "copy", qfs, pp[bank][:], inc=sA)
                                    prj_evac = sA.n
                                    rb_ = prr if (h_ % 2 == 0) else pou[1]
                                    P.wait("pe", sA)
                                    if h_ % 2 == 1:
                                        P.wait("pe", sA, ret_evac[1][1])
                                    P.do("pe", "matmul", rb_[:], rott[:], qfs,
                                         start=True, stop=True, inc=sP)
                                    P.wait("pool", sA)
                                    P.do("pool", "tensor_tensor", m1s[:], qfs,
                                         tc_, ALU.mult)
                                    slot = (sel * 8 + h_) % 2
                                    m2sl = sqb[:, slot * 512:(slot + 1) * 512]
                                    P.wait("dve", sP)
                                    P.wait("dve", sG, m2s_free[slot])
                                    P.do("dve", "tensor_tensor", m2sl, rb_[:],
                                         ts_, ALU.mult, inc=sV)
                                    P.wait("pool", sV)
                                    P.do("pool", "tensor_tensor",
                                         dstx[:, h_ * 512:(h_ + 1) * 512],
                                         m1s[:], m2sl, ALU.add, inc=sG)
                                    m2s_free[slot] = sG.n
                            qk_done = sG.n
                            if stile < 3:
                                P.wait("sp", sG, qk_done)
                                for ix in range(4):
                                    P.do("sp", "dma_start",
                                         out=TB4[ix],
                                         in_=tab_sp[:, ix * S + s0_ + 512:
                                                    ix * S + s0_ + 1024], inc=dL)
                                tab_ld = dL.n
                            for tl in range(4):
                                t0 = s0_ + tl * 128
                                P.wait("pe", sV, scb_evac[1][0])
                                P.wait("pe", sV, scb_evac[1][1])
                                P.wait("pe", sA, prj_evac)
                                for c in range(DC):
                                    for hf in range(2):
                                        P.do("pe", "matmul", pp[hf][:],
                                             xnb[:, c * S + t0:c * S + t0 + 128],
                                             wvv[:].rearrange("p (h c e) -> p h c e",
                                                              h=HB, c=DC)[:, 4 * hf:4 * (hf + 1), c, :],
                                             start=(c == 0), stop=(c == DC - 1),
                                             inc=(sP if (c == DC - 1 and hf == 1) else None))
                                P.wait("act", sP)
                                for hf in range(2):
                                    P.do("act", "copy",
                                         vtk[:, tl * D + hf * 512:tl * D + (hf + 1) * 512],
                                         pp[hf][:], inc=sA)
                                prj_evac = sA.n
                            for pl in range(2):
                                ta, tb = 2 * pl, 2 * pl + 1
                                P.wait("pe", sV, kgt_evac)
                                P.wait("pe", sG, qk_done)
                                for h_ in range(HB):
                                    P.do("pe", "transpose",
                                         ptc[:, h_ * 128:(h_ + 1) * 128],
                                         kx[:, h_ * 512 + ta * 128:h_ * 512 + ta * 128 + 128],
                                         idb[:], inc=(sP if h_ == HB - 1 else None))
                                P.wait("dve", sP)
                                for h_ in range(HB):
                                    P.do("dve", "tensor_scalar_mul",
                                         kgt[:, h_ * 128:(h_ + 1) * 128],
                                         ptc[:, h_ * 128:(h_ + 1) * 128],
                                         gkt[:, HB + h_:HB + h_ + 1],
                                         inc=(sV if h_ == HB - 1 else None))
                                kgt_a_done = sV.n
                                P.wait("pool", sP, qg_guard)
                                for ch_, tt_, gt_ in ((0, ta, gqt), (1, tb, gq2t)):
                                    P.do("pool", "tensor_tensor",
                                         qg[:, ch_ * D:(ch_ + 1) * D].rearrange(
                                             "p (h e) -> p h e", h=HB),
                                         qx[:].rearrange("p (h c e) -> p h c e",
                                                         h=HB, c=4)[:, :, tt_, :],
                                         gt_[:].rearrange("p (h e) -> p h e", h=HB),
                                         ALU.mult,
                                         inc=(sG if ch_ == 1 else None))
                                qg_done = sG.n
                                flush_gn()
                                for grp in range(2):
                                    P.wait("pe", sV, scb_evac[0][grp])
                                    P.wait("pe", sA, prj_evac)
                                    for h_ in range(4 * grp, 4 * grp + 4):
                                        P.do("pe", "matmul",
                                             psc2[grp][:, (h_ % 4) * 128:(h_ % 4) * 128 + 128],
                                             kx[:, h_ * 512 + ta * 128:h_ * 512 + ta * 128 + 128],
                                             qx[:, h_ * 512 + ta * 128:h_ * 512 + ta * 128 + 128],
                                             start=True, stop=True,
                                             inc=(sP if h_ % 4 == 3 else None))
                                    P.wait("dve", sP)
                                    P.do("dve", "tensor_tensor",
                                         scb[:, grp * 512:(grp + 1) * 512],
                                         psc2[grp][:],
                                         dct[:, grp * 512:(grp + 1) * 512],
                                         ALU.mult, inc=sV)
                                    scb_evac[0][grp] = sV.n
                                P.wait("pe", sV, scbj_evac)
                                for h_ in range(4):
                                    P.do("pe", "matmul",
                                         prr[:, h_ * 128:(h_ + 1) * 128],
                                         kx[:, h_ * 512 + ta * 128:h_ * 512 + ta * 128 + 128],
                                         qx[:, h_ * 512 + tb * 128:h_ * 512 + tb * 128 + 128],
                                         start=True, stop=True,
                                         inc=(sP if h_ == 3 else None))
                                P.wait("dve", sP)
                                P.do("dve", "tensor_tensor",
                                     scb[:, 2 * D:2 * D + 512], prr[:],
                                     dct2[:, 0:512], ALU.mult, inc=sV)
                                scbj0 = sV.n
                                P.wait("pe", sV, kgt_a_done)
                                for h_ in range(HB):
                                    P.do("pe", "transpose",
                                         ptc[:, h_ * 128:(h_ + 1) * 128],
                                         kx[:, h_ * 512 + tb * 128:h_ * 512 + tb * 128 + 128],
                                         idb[:], inc=(sP if h_ == HB - 1 else None))
                                P.wait("dve", sP)
                                for h_ in range(HB):
                                    P.do("dve", "tensor_scalar_mul",
                                         kgt[:, D + h_ * 128:D + (h_ + 1) * 128],
                                         ptc[:, h_ * 128:(h_ + 1) * 128],
                                         gkt[:, h_:h_ + 1],
                                         inc=(sV if h_ == HB - 1 else None))
                                kgt_evac = sV.n
                                for grp in range(2):
                                    P.wait("pe", sV, scb_evac[0][grp])
                                    for h_ in range(4 * grp, 4 * grp + 4):
                                        P.do("pe", "matmul",
                                             psc2[grp][:, (h_ % 4) * 128:(h_ % 4) * 128 + 128],
                                             kx[:, h_ * 512 + tb * 128:h_ * 512 + tb * 128 + 128],
                                             qx[:, h_ * 512 + tb * 128:h_ * 512 + tb * 128 + 128],
                                             start=True, stop=True,
                                             inc=(sP if h_ % 4 == 3 else None))
                                    P.wait("dve", sP)
                                    P.do("dve", "tensor_tensor",
                                         scb[:, D + grp * 512:D + (grp + 1) * 512],
                                         psc2[grp][:],
                                         dct[:, grp * 512:(grp + 1) * 512],
                                         ALU.mult, inc=sV)
                                    scb_evac[1][grp] = sV.n
                                P.wait("pe", sV, scbj0)
                                for h_ in range(4, 8):
                                    P.do("pe", "matmul",
                                         prr[:, (h_ % 4) * 128:(h_ % 4) * 128 + 128],
                                         kx[:, h_ * 512 + ta * 128:h_ * 512 + ta * 128 + 128],
                                         qx[:, h_ * 512 + tb * 128:h_ * 512 + tb * 128 + 128],
                                         start=True, stop=True,
                                         inc=(sP if h_ == 7 else None))
                                P.wait("dve", sP)
                                P.do("dve", "tensor_tensor",
                                     scb[:, 2 * D + 512:3 * D], prr[:],
                                     dct2[:, 512:1024], ALU.mult, inc=sV)
                                scbj_evac = sV.n
                                P.wait("pe", sG, qg_done)
                                for grp in range(2):
                                    P.wait("pe", sA, ret_evac[1][grp])
                                    P.wait("pe", sG, supd_done)
                                    P.wait("pe", sV, scb_evac[0][grp])
                                    for h_ in range(4 * grp, 4 * grp + 4):
                                        cc0 = (h_ % 4) * 128
                                        P.do("pe", "matmul",
                                             pou[grp][:, cc0:cc0 + 128],
                                             scb[:, h_ * 128:(h_ + 1) * 128],
                                             vtk[:, ta * D + h_ * 128:ta * D + (h_ + 1) * 128],
                                             start=True, stop=False)
                                        P.do("pe", "matmul",
                                             pou[grp][:, cc0:cc0 + 128],
                                             qg[:, h_ * 128:(h_ + 1) * 128],
                                             Sb2[:, h_ * 128:(h_ + 1) * 128],
                                             start=False, stop=True,
                                             inc=(sP if h_ % 4 == 3 else None))
                                    P.wait("act", sP)
                                    P.wait("act", sG, gn_pool_done)
                                    P.do("act", "copy",
                                         ret[:, grp * 512:(grp + 1) * 512],
                                         pou[grp][:], inc=sA)
                                    ret_evac[0][grp] = sA.n
                                for grp in range(2):
                                    P.wait("pe", sA, ret_evac[0][grp])
                                    P.wait("pe", sV, scb_evac[1][grp])
                                    P.wait("pe", sV, scbj_evac)
                                    P.wait("pe", sV, kgt_evac)
                                    for h_ in range(4 * grp, 4 * grp + 4):
                                        cc0 = (h_ % 4) * 128
                                        P.do("pe", "matmul",
                                             pou[grp][:, cc0:cc0 + 128],
                                             scb[:, D + h_ * 128:D + (h_ + 1) * 128],
                                             vtk[:, tb * D + h_ * 128:tb * D + (h_ + 1) * 128],
                                             start=True, stop=False)
                                        P.do("pe", "matmul",
                                             pou[grp][:, cc0:cc0 + 128],
                                             scb[:, 2 * D + h_ * 128:2 * D + (h_ + 1) * 128],
                                             vtk[:, ta * D + h_ * 128:ta * D + (h_ + 1) * 128],
                                             start=False, stop=False)
                                        P.do("pe", "matmul",
                                             pou[grp][:, cc0:cc0 + 128],
                                             qg[:, D + h_ * 128:D + (h_ + 1) * 128],
                                             Sb2[:, h_ * 128:(h_ + 1) * 128],
                                             start=False, stop=True)
                                        P.do("pe", "matmul",
                                             psd[grp][:, cc0:cc0 + 128],
                                             kgt[:, h_ * 128:(h_ + 1) * 128],
                                             vtk[:, ta * D + h_ * 128:ta * D + (h_ + 1) * 128],
                                             start=True, stop=False)
                                        P.do("pe", "matmul",
                                             psd[grp][:, cc0:cc0 + 128],
                                             kgt[:, D + h_ * 128:D + (h_ + 1) * 128],
                                             vtk[:, tb * D + h_ * 128:tb * D + (h_ + 1) * 128],
                                             start=False, stop=True,
                                             inc=(sP if h_ % 4 == 3 else None))
                                    P.wait("act", sP)
                                    P.wait("act", sG, gn_pool_done)
                                    P.do("act", "copy",
                                         ret[:, D + grp * 512:D + (grp + 1) * 512],
                                         pou[grp][:], inc=sA)
                                    ret_evac[1][grp] = sA.n
                                qg_guard = sP.n
                                P.wait("dve", sP)
                                for h_ in range(HB):
                                    grp, cc0 = h_ // 4, (h_ % 4) * 128
                                    P.do("dve", "scalar_tensor_tensor",
                                         Sf[:, h_ * 128:(h_ + 1) * 128],
                                         Sf[:, h_ * 128:(h_ + 1) * 128],
                                         gct[:, h_:h_ + 1],
                                         psd[grp][:, cc0:cc0 + 128],
                                         ALU.mult, ALU.add)
                                P.inc("dve", sV)
                                P.wait("pool", sV)
                                P.do("pool", "tensor_copy", Sb2[:], Sf[:],
                                     inc=sG)
                                supd_done = sG.n
                                for ch_ in range(2):
                                    tg = stile * 4 + 2 * pl + ch_
                                    rv = ret[:, ch_ * D:(ch_ + 1) * D]
                                    P.wait("dve", sA, ret_evac[ch_][1])
                                    P.do("dve", "reduce_sum", st8[:, 0:HB],
                                         rv.rearrange("p (h e) -> p h e", h=HB),
                                         axis=mybir.AxisListType.X)
                                    P.do("dve", "tensor_tensor", sqb[:], rv, rv,
                                         ALU.mult)
                                    P.do("dve", "reduce_sum", st8[:, HB:2 * HB],
                                         sqb[:].rearrange("p (h e) -> p h e", h=HB),
                                         axis=mybir.AxisListType.X)
                                    P.wait("dve", sA)
                                    P.do("dve", "tensor_scalar_mul",
                                         st8[:, 2 * HB:3 * HB], st8[:, 0:HB],
                                         cst[:, 4:5])
                                    P.do("dve", "tensor_scalar_mul",
                                         st8[:, HB:2 * HB], st8[:, HB:2 * HB],
                                         cst[:, 4:5])
                                    P.do("dve", "tensor_tensor", st8[:, 0:HB],
                                         st8[:, 2 * HB:3 * HB],
                                         st8[:, 2 * HB:3 * HB], ALU.mult)
                                    P.do("dve", "tensor_tensor", st8[:, HB:2 * HB],
                                         st8[:, HB:2 * HB], st8[:, 0:HB],
                                         ALU.subtract, inc=sV)
                                    P.wait("act", sV)
                                    P.do("act", "activation", st8[:, 3 * HB:4 * HB],
                                         st8[:, HB:2 * HB], AF.Ln, bias=EPS6)
                                    P.do("act", "activation", st8[:, 3 * HB:4 * HB],
                                         st8[:, 3 * HB:4 * HB], AF.Exp, scale=-0.5,
                                         inc=sA)
                                    P.wait("dve", sA)
                                    for h_ in range(HB):
                                        P.do("dve", "tensor_scalar",
                                             rv[:, h_ * 128:(h_ + 1) * 128],
                                             rv[:, h_ * 128:(h_ + 1) * 128],
                                             st8[:, 2 * HB + h_:2 * HB + h_ + 1],
                                             st8[:, 3 * HB + h_:3 * HB + h_ + 1],
                                             ALU.subtract, ALU.mult, inc=sV)
                                    svn = sV.n

                                    def em(rv=rv, tg=tg, svn=svn):
                                        P.wait("pool", sV, svn)
                                        P.do("pool", "tensor_tensor", rv, rv,
                                             gnS[:], ALU.mult)
                                        P.do("pool", "tensor_tensor",
                                             ygn[:, tg * D:(tg + 1) * D], rv,
                                             gnB[:], ALU.add, inc=sG)
                                    pending_gn.append(em)
                        flush_gn()
                    P.barrier()
                L1.close()

                xbst = contextlib.ExitStack()
                xb = sb(xbst, "xb", [128, DC * S], side="right")
                P.do("sp", "dma_start", out=xb[:], in_=xb_sp, inc=dL)
                with contextlib.ExitStack() as c2:
                    wgg = sb(c2, "wgg", [128, DC * D], BF16)
                    woo = sb(c2, "woo", [128, DC * D], BF16)
                    load_bf16(wgg[:].rearrange("p (c n) -> p c n", c=DC),
                              bWg[li].rearrange("(c p) n -> p c n", p=128))
                    load_bf16(woo[:].rearrange("p (c n) -> p c n", c=DC),
                              bWo[li].rearrange("(c p) n -> p c n", p=128))
                    swt = sb(c2, "swt", [128, D], BF16)
                    ptok = sb(c2, "ptok", [128, 2 * D], BF16)
                    prT = sb(c2, "prT", [128, DC * 512], BF16)
                    P.barrier()
                    with contextlib.ExitStack() as ps9:
                        pg = [psum(ps9, "pg3%d" % i, [128, 512], F32)
                              for i in range(2)]
                        ptc = psum(ps9, "ptc2", [128, 1024], BF16)
                        pt_cons = 0
                        prT_evac = 0
                        for stile in range(4):
                            s0_ = stile * 512
                            for tl in range(4):
                                tg = stile * 4 + tl
                                t0 = s0_ + tl * 128
                                P.wait("pe", sG)
                                P.wait("pe", sA)
                                P.wait("pe", sV)
                                for c in range(DC):
                                    for hf in range(2):
                                        P.do("pe", "matmul", pg[hf][:],
                                             xnb[:, c * S + t0:c * S + t0 + 128],
                                             wgg[:, c * D + hf * 512:c * D + (hf + 1) * 512],
                                             start=(c == 0), stop=(c == DC - 1),
                                             inc=(sP if (c == DC - 1 and hf == 1) else None))
                                P.wait("act", sP)
                                for hf in range(2):
                                    P.do("act", "activation",
                                         swt[:, hf * 512:(hf + 1) * 512],
                                         pg[hf][:], AF.Silu,
                                         inc=(sA if hf == 1 else None))
                                psl = (tl % 2) * D
                                P.wait("pool", sA)
                                P.wait("pool", sP, pt_cons)
                                P.do("pool", "tensor_tensor", ptok[:, psl:psl + D],
                                     ygn[:, tg * D:(tg + 1) * D], swt[:],
                                     ALU.mult, inc=sG)
                                P.wait("pe", sG)
                                P.wait("pe", sV, prT_evac)
                                for dc2 in range(DC):
                                    P.do("pe", "transpose",
                                         ptc[:, dc2 * 128:(dc2 + 1) * 128],
                                         ptok[:, psl + dc2 * 128:psl + (dc2 + 1) * 128],
                                         idb[:],
                                         inc=(sP if dc2 == DC - 1 else None))
                                pt_cons = sP.n
                                P.wait("dve", sP)
                                P.do("dve", "tensor_copy",
                                     prT[:].rearrange("p (c n) -> p c n", c=DC)
                                     [:, :, tl * 128:tl * 128 + 128],
                                     ptc[:].rearrange("p (c n) -> p c n", c=DC),
                                     inc=sV)
                                prT_evac = sV.n
                            prdone = sV.n
                            P.wait("dve", dL)
                            for dt in range(DC):
                                bank = dt % 2
                                P.wait("pe", sV, prdone)
                                P.wait("pe", sV)
                                P.wait("pe", sA)
                                for c in range(DC):
                                    P.do("pe", "matmul", pg[bank][:],
                                         woo[:, c * D + dt * 128:c * D + dt * 128 + 128],
                                         prT[:, c * 512:(c + 1) * 512],
                                         start=(c == 0), stop=(c == DC - 1),
                                         inc=(sP if c == DC - 1 else None))
                                P.wait("dve", sP)
                                co = dt * S + s0_
                                P.do("dve", "tensor_tensor", xb[:, co:co + 512],
                                     xb[:, co:co + 512], pg[bank][:], ALU.add,
                                     inc=sV)
                    P.barrier()
                L0.close()

                with contextlib.ExitStack() as sv2:
                    w2r = sb(sv2, "w2r", [128, 32 * D], BF16)
                    load_bf16(w2r[:].rearrange("p (c n) -> p c n", c=32),
                              bw2[li].rearrange("(c p) n -> p c n", p=128))
                    ln_stats(xb, rowMU[0:1, :], rowSC[0:1, :], EPS6)
                    s2c = vec_fm(sv2, "l2s",
                                 bV["ln2_scale"][li].rearrange("(c p) -> c p", c=8))
                    b2c = vec_fm(sv2, "l2b",
                                 bV["ln2_bias"][li].rearrange("(c p) -> c p", c=8))
                    P.barrier()
                    for st2 in range(4):
                        with contextlib.ExitStack() as c3:
                            xn2 = sb(c3, "xn2", [128, DC * 512], BF16)
                            h1 = sb(c3, "h1", [128, 32 * 512], BF16)
                            w1s = [sb(c3, "w1s%d" % i, [128, DC * 128], BF16)
                                   for i in range(4)]
                            apply_ln(xb, xn2, rowMU[0:1, :], rowSC[0:1, :],
                                     gcol=s2c, bcol=b2c, s0=st2 * 512, ns=512)
                            with contextlib.ExitStack() as pc3:
                                p1 = [psum(pc3, "p31%d" % i, [128, 512], F32)
                                      for i in range(2)]
                                use_done = [0, 0, 0, 0]
                                ldd = {}

                                def w1_load(ft):
                                    sl_ = ft % 4
                                    P.wait("sp", P.dU[0], P.w1c_built)
                                    P.wait("pool", P.dU[0], P.w1c_built)
                                    P.wait("sp", sP, use_done[sl_])
                                    P.wait("pool", sP, use_done[sl_])
                                    P.do("sp", "dma_start",
                                         out=w1s[sl_][:, 0:512],
                                         in_=w1c[li][:, ft * 1024:ft * 1024 + 512],
                                         inc=P.dH[sl_])
                                    P.do("pool", "dma_start",
                                         out=w1s[sl_][:, 512:1024],
                                         in_=w1c[li][:, ft * 1024 + 512:(ft + 1) * 1024],
                                         inc=P.dU[sl_])
                                    ldd[ft] = (P.dH[sl_].n, P.dU[sl_].n)
                                w1_load(0)
                                w1_load(1)
                                w1_load(2)
                                for ft in range(32):
                                    sl_ = ft % 4
                                    if ft + 3 < 32:
                                        w1_load(ft + 3)
                                    P.wait("pe", P.dH[sl_], ldd[ft][0])
                                    P.wait("pe", P.dU[sl_], ldd[ft][1])
                                    P.wait("pe", sA)
                                    bank = ft % 2
                                    for c in range(DC):
                                        P.do("pe", "matmul", p1[bank][:],
                                             w1s[sl_][:, c * 128:(c + 1) * 128],
                                             xn2[:, c * 512:(c + 1) * 512],
                                             start=(c == 0), stop=(c == DC - 1),
                                             inc=(sP if c == DC - 1 else None))
                                    use_done[sl_] = sP.n
                                    P.wait("act", sP)
                                    P.do("act", "activation",
                                         h1[:, ft * 512:(ft + 1) * 512],
                                         p1[bank][:], AF.Gelu_apprx_tanh,
                                         inc=sA)
                            with contextlib.ExitStack() as pc4:
                                p8 = [psum(pc4, "p8%d" % i, [128, 512], F32)
                                      for i in range(8)]
                                P.wait("pe", sA)
                                P.wait("pe", sV)
                                for fc in range(32):
                                    for do in range(8):
                                        P.do("pe", "matmul", p8[do][:],
                                             w2r[:, fc * D + do * 128:fc * D + (do + 1) * 128],
                                             h1[:, fc * 512:(fc + 1) * 512],
                                             start=(fc == 0), stop=(fc == 31),
                                             inc=(sP if (fc == 31 and do == 7) else None))
                                P.wait("dve", sP)
                                for do in range(8):
                                    co = do * S + st2 * 512
                                    P.do("dve", "tensor_tensor",
                                         xb[:, co:co + 512],
                                         xb[:, co:co + 512], p8[do][:],
                                         ALU.add, inc=(sV if do == 7 else None))
                            P.barrier()
            with contextlib.ExitStack() as ce:
                t = sb(ce, "cet", [128, S])
                for c in range(DC):
                    P.wait("sp", sV)
                    P.wait("sp", dL)
                    P.do("sp", "dma_start", out=t[:],
                         in_=r_sp[:, c * S:(c + 1) * S], inc=dL)
                    P.wait("dve", dL)
                    P.do("dve", "tensor_tensor", t[:], t[:],
                         xb[:, c * S:(c + 1) * S], ALU.add, inc=sV)
                    P.wait("sp", sV)
                    P.do("sp", "dma_start", out=r_sp[:, c * S:(c + 1) * S],
                         in_=t[:], inc=dL)
                P.barrier()
            xbst.close()
            dump_dram("C")

        if "A" in stages:
            stage_A()
        else:
            with contextlib.ExitStack() as st:
                t = sb(st, "xcp", [128, DC * S])
                load_f32(t[:].rearrange("p (c s) -> p c s", c=DC),
                         xT.rearrange("(c p) s -> p c s", p=128))
                P.barrier()
                P.do("sp", "dma_start", out=r_sp, in_=t[:], inc=dL)
                P.barrier()

        rstack = contextlib.ExitStack()
        rbuf = None
        if "B" in stages or "C" in stages or "D" in stages:
            rbuf = sb(rstack, "rbB", [128, DC * S])
            P.do("sp", "dma_start", out=rbuf[:], in_=r_sp, inc=dL)
            P.barrier()
        if "B" in stages:
            stage_ffn(rbuf, "B")
            dump_sbuf("B", rbuf)
        if "C" in stages:
            stage_C(rbuf, rstack)
            rstack = contextlib.ExitStack()
            rbuf = sb(rstack, "rbD", [128, DC * S])
            P.do("sp", "dma_start", out=rbuf[:], in_=r_sp, inc=dL)
            P.barrier()
        if "D" in stages:
            stage_ffn(rbuf, "D")
        if rbuf is not None:
            P.barrier()
            P.do("sp", "dma_start", out=yT, in_=rbuf[:], inc=dL)
            P.barrier()
            rstack.close()
        else:
            with contextlib.ExitStack() as st:
                t = sb(st, "ycp", [128, S])
                for c in range(DC):
                    P.do("sp", "dma_start", out=t[:],
                         in_=r_sp[:, c * S:(c + 1) * S], inc=dL)
                    P.wait("sp", dL)
                    P.do("sp", "dma_start", out=yT[:, c * S:(c + 1) * S],
                         in_=t[:], inc=dL)
                    P.wait("sp", dL)
                P.barrier()

        with nc.Block() as block:
            P.finalize(block)
    return nc


def kernel(**inputs):
    n = 8
    consts = _host_consts()
    toep = _toep(inputs["L0_attn_a_rel_bias"])
    x = np.ascontiguousarray(inputs["L0_attn_a_x"])

    nc = bass.Bass("TRN2", target_bir_lowering=False)
    _build(nc, _CFG)

    common = {}
    for k in ["wq_lin", "bq_lin", "wk_lin", "bk_lin", "wv_lin", "bv_lin",
              "wq_loc", "bq_loc", "wk_loc", "bk_loc", "wv_loc", "bv_loc",
              "wo", "bo", "wg1", "bg1", "wg2", "bg2", "ln_g", "ln_b"]:
        common["a_" + k] = np.ascontiguousarray(
            np.asarray(inputs["L0_attn_a_" + k], np.float32))
    common["TOEP"] = toep
    common["f_Wt"] = np.ascontiguousarray(
        np.asarray(inputs["L0_ffn_W"], np.float32).T)
    common["f_b"] = np.ascontiguousarray(np.asarray(inputs["L0_ffn_b"], np.float32))
    common["f_Vr"] = np.ascontiguousarray(np.asarray(inputs["L0_ffn_V_r"], np.float32))
    common["f_Ct"] = np.ascontiguousarray(
        np.asarray(inputs["L0_ffn_C"], np.float32).T)
    for k in ["Wq", "Wk", "Wv", "Wg", "Wo", "gn_scale", "gn_bias",
              "ln1_scale", "ln1_bias", "ln2_scale", "ln2_bias"]:
        common["b_" + k] = np.ascontiguousarray(
            np.asarray(inputs["L0_attn_b_" + k], np.float32))
    common["b_w1"] = np.ascontiguousarray(
        np.asarray(inputs["L0_attn_b_ffn_w1"], np.float32))
    common["b_w2"] = np.ascontiguousarray(
        np.asarray(inputs["L0_attn_b_ffn_w2"], np.float32))
    for k, v in consts.items():
        common["C_" + k] = np.ascontiguousarray(v)

    in_maps = []
    for b in range(n):
        m = dict(common)
        m["xT"] = np.ascontiguousarray(np.asarray(x[b], np.float32).T)
        in_maps.append(m)

    res = run_bass_kernel_spmd(nc, in_maps, core_ids=list(range(n)))
    out = np.empty((n, S, D), np.float32)
    for b in range(n):
        yt = np.asarray(res.results[b]["yT"])
        out[b] = yt.reshape(128, DC, S).transpose(2, 1, 0).reshape(S, D)
    return out


# version 19
# speedup vs baseline: 1.0330x; 1.0330x over previous
import contextlib
import numpy as np

import concourse.bass as bass
import concourse.mybir as mybir
from concourse.bass_utils import run_bass_kernel_spmd

F32 = mybir.dt.float32
BF16 = mybir.dt.bfloat16
AF = mybir.ActivationFunctionType
ALU = mybir.AluOpType

S = 2048
D = 1024
DC = 8
TC = 16
HA = 16
HB = 8
FF = 4096
RK = 32
CSH = 8.0

_CFG = {"stages": ("A", "B", "C", "D"), "dump": ()}


def _hilo(a):
    a = np.asarray(a, np.float64)
    hi = a.astype(np.float32)
    lo = (a - hi.astype(np.float64)).astype(np.float32)
    return hi, lo


def _host_consts():
    import jax
    import jax.numpy as jnp
    cpu_dev = jax.devices("cpu")[0]
    with jax.default_device(cpu_dev):
        f = jnp.float32
        c = {}
        c["IDF"] = np.eye(128, dtype=np.float32)
        P = np.zeros((128, 128), np.float32)
        for i in range(64):
            P[2 * i, 2 * i + 1] = -1.0
            P[2 * i + 1, 2 * i] = 1.0
        c["ROTT"] = np.ascontiguousarray(P.T)
        d, half = 128, 64
        base = (jnp.arange(0, d, 2, dtype=f) + 0.4 * d) / (1.4 * d)
        pos = jnp.arange(S, dtype=f)
        inv_freq = 1.0 / 10000.0 ** (jnp.arange(half, dtype=f) / half)
        sinusoid = pos[:, None] * inv_freq[None, :]
        sin, cos = jnp.sin(sinusoid), jnp.cos(sinusoid)
        scale = base[None, :] ** (pos[:, None] / 512.0)
        for tag, sc in (("Q", scale), ("K", 1.0 / scale)):
            sin2 = jnp.repeat(sin * sc, 2, axis=-1)
            cos2 = jnp.repeat(cos * sc, 2, axis=-1)
            c["XC%s_h" % tag] = np.ascontiguousarray(np.asarray(cos2, np.float32).T)
            c["XC%s_l" % tag] = np.zeros((128, S), np.float32)
            c["XS%s_h" % tag] = np.ascontiguousarray(np.asarray(sin2, np.float32).T)
            c["XS%s_l" % tag] = np.zeros((128, S), np.float32)
        gamj = 1.0 - 2.0 ** (-5.0 - jnp.arange(HB, dtype=f))
        ii32 = jnp.arange(128, dtype=f)
        diff = ii32[None, :] - ii32[:, None]
        decj = np.asarray(gamj[:, None, None] ** diff[None], np.float32)
        diff2 = (ii32[None, :] + 128.0) - ii32[:, None]
        decj2 = np.asarray(gamj[:, None, None] ** diff2[None], np.float32)
        dct = np.zeros((128, HB * 128), np.float32)
        dct2 = np.zeros((128, HB * 128), np.float32)
        for h in range(HB):
            dct[:, h * 128:(h + 1) * 128] = np.where(
                np.arange(128)[None, :] >= np.arange(128)[:, None], decj[h], 0.0)
            dct2[:, h * 128:(h + 1) * 128] = decj2[h]
        c["DCT_h"] = dct
        c["DCT_l"] = np.zeros_like(dct)
        c["DCT2_h"] = dct2
        c["DCT2_l"] = np.zeros_like(dct2)
    gam = 1.0 - 2.0 ** (-5.0 - np.arange(HB, dtype=np.float64))
    ii = np.arange(128, dtype=np.float64)
    gq = np.zeros((128, HB * 128), np.float64)
    gq2 = np.zeros((128, HB * 128), np.float64)
    for h in range(HB):
        gq[:, h * 128:(h + 1) * 128] = (gam[h] ** (ii + 1.0))[None, :]
        gq2[:, h * 128:(h + 1) * 128] = (gam[h] ** (ii + 129.0))[None, :]
    c["GQ_h"], c["GQ_l"] = _hilo(gq)
    c["GQ2_h"], c["GQ2_l"] = _hilo(gq2)
    gk = np.stack([gam[h] ** (127.0 - ii) for h in range(HB)], axis=1)
    c["GK_h"], c["GK_l"] = _hilo(gk)
    gk2 = np.stack([gam[h] ** (255.0 - ii) for h in range(HB)], axis=1)
    c["GK255_h"], c["GK255_l"] = _hilo(gk2)
    gc = np.tile((gam ** 256.0)[None, :], (128, 1))
    c["GC_h"], c["GC_l"] = _hilo(gc)
    vb = np.full((128, 256), -1e9, np.float32)
    for t in range(2):
        for p in range(128):
            j = 128 * t + p
            lo, hi = max(0, j - 64), min(128, j + 1)
            if lo < hi:
                vb[p, 128 * t + lo:128 * t + hi] = -8.0 * CSH
    c["VBIG"] = vb
    cst = np.zeros((128, 8), np.float32)
    cst[:, 0] = 1e-5
    cst[:, 1] = 1e-6
    cst[:, 2] = 1.0
    cst[:, 3] = 1.0 / 1024.0
    cst[:, 4] = 1.0 / 128.0
    cst[:, 5] = 1e-6
    c["CST"] = cst
    c["ONES32"] = np.ones((1, 512), np.float32)
    c["ONESB"] = np.ones((1, 512), np.float32)
    return c


def _toep(rel_bias):
    rb = np.asarray(rel_bias, np.float32)
    out = np.zeros((128, HA * 256), np.float32)
    i = np.arange(128)
    for h in range(HA):
        for t in range(2):
            for p in range(128):
                off = (128 * t + p) - i
                valid = (off >= 0) & (off <= 64)
                out[p, 256 * h + 128 * t:256 * h + 128 * t + 128] = np.where(
                    valid, rb[np.clip(off + 32, 0, 128), h], 0.0)
    return out


class Sem:
    __slots__ = ("h", "step", "n")

    def __init__(self, handle, step):
        self.h, self.step, self.n = handle, step, 0


class Prog:
    ENGS = ("sp", "pe", "act", "dve", "pool")

    def __init__(self, nc, es):
        self.nc = nc
        self.q = {e: [] for e in self.ENGS}
        self.last_wait = {e: {} for e in self.ENGS}
        mk = lambda n, s: Sem(es.enter_context(nc.semaphore(n)), s)
        self.dL = mk("dL", 16)
        self.dC = mk("dC", 16)
        self.dU = [mk("dU%d" % i, 16) for i in range(4)]
        self.dH = [mk("dH%d" % i, 16) for i in range(4)]
        self.own = {"sp": mk("sS", 1), "pe": mk("sP", 1), "act": mk("sA", 1),
                    "dve": mk("sV", 1), "pool": mk("sG", 1)}
        self.lastc = {e: None for e in self.ENGS}
        self.uncapped = {e: False for e in self.ENGS}

    def do(self, eng, meth, *args, inc=None, **kw):
        if eng in ("act", "dve", "pool") and meth != "dma_start":
            self.wait(eng, self.own[eng])
            if inc is None:
                inc = self.own[eng]
        if inc is not None:
            inc.n += inc.step
        cell = [inc]
        def thunk(h, meth=meth, args=args, kw=kw, cell=cell):
            ins = getattr(h, meth)(*args, **kw)
            if cell[0] is not None:
                ins.then_inc(cell[0].h, cell[0].step)
        self.q[eng].append(thunk)
        if meth != "dma_start":
            if meth == "matmul" and kw.get("stop") is False:
                self.lastc[eng] = None
                self.uncapped[eng] = True
            else:
                self.lastc[eng] = cell
                self.uncapped[eng] = inc is None
        return inc.n if inc is not None else None

    def inc(self, eng, sem):
        sem.n += 1
        c = self.lastc[eng]
        if c is not None and c[0] is None:
            c[0] = sem
        else:
            def thunk(h, sem=sem):
                h.nop().then_inc(sem.h, 1)
            self.q[eng].append(thunk)
            self.lastc[eng] = None
        self.uncapped[eng] = False
        return sem.n

    def wait(self, eng, sem, val=None):
        if val is None:
            val = sem.n
        if val <= 0 or self.last_wait[eng].get(sem, 0) >= val:
            return
        self.last_wait[eng][sem] = val
        def thunk(h, sem=sem, val=val):
            h.wait_ge(sem.h, val)
        self.q[eng].append(thunk)

    def barrier(self, dma=True):
        for e in ("pe", "act", "dve", "pool"):
            if self.uncapped[e]:
                self.inc(e, self.own[e])
        for e in self.ENGS:
            for o in ("pe", "act", "dve", "pool"):
                self.wait(e, self.own[o])
            if dma:
                self.wait(e, self.dL)
                self.wait(e, self.dC)

    def finalize(self, block):
        block.sync(lambda h: [f(h) for f in self.q["sp"]])
        block.tensor(lambda h: [f(h) for f in self.q["pe"]])
        block.scalar(lambda h: [f(h) for f in self.q["act"]])
        block.vector(lambda h: [f(h) for f in self.q["dve"]])
        block.gpsimd(lambda h: [f(h) for f in self.q["pool"]])


def _build(nc, cfg):
    stages = list(cfg["stages"])
    dump = set(cfg.get("dump", ()))

    def din(name, shape):
        return nc.dram_tensor(name, list(shape), F32, kind="ExternalInput").ap()

    xT = din("xT", [D, S])
    aW = {k: din("a_" + k, [3 * D, D] if k == "wg1" else [D, D]) for k in
          ["wq_lin", "wk_lin", "wv_lin", "wq_loc", "wk_loc", "wv_loc",
           "wo", "wg1", "wg2"]}
    aB = {k: din("a_" + k, [D]) for k in
          ["bq_lin", "bk_lin", "bv_lin", "bq_loc", "bk_loc", "bv_loc",
           "bo", "bg1", "bg2", "ln_g", "ln_b"]}
    TOEP = din("TOEP", [128, HA * 256])
    fWt = din("f_Wt", [D, D])
    fB = din("f_b", [D])
    fVr = din("f_Vr", [D, RK])
    fCt = din("f_Ct", [RK, D])
    bWq = din("b_Wq", [2, HB, D, 128])
    bWk = din("b_Wk", [2, HB, D, 128])
    bWv = din("b_Wv", [2, HB, D, 128])
    bWg = din("b_Wg", [2, D, D])
    bWo = din("b_Wo", [2, D, D])
    bw1 = din("b_w1", [2, D, FF])
    bw2 = din("b_w2", [2, FF, D])
    bV = {k: din("b_" + k, [2, D]) for k in
          ["gn_scale", "gn_bias", "ln1_scale", "ln1_bias",
           "ln2_scale", "ln2_bias"]}
    CN = {k: din("C_" + k, v.shape) for k, v in _host_consts().items()}
    yT = nc.dram_tensor("yT", [128, DC * S], F32, kind="ExternalOutput").ap()
    r_sp = nc.dram_tensor("r_spill", [128, DC * S], F32, kind="Internal").ap()
    xb_sp = nc.dram_tensor("xb_spill", [128, DC * S], F32, kind="Internal").ap()
    lin_sp = nc.dram_tensor("lin_spill", [128, DC * S], BF16, kind="Internal").ap()
    tab_sp = nc.dram_tensor("tab_cache", [128, 4 * S], BF16, kind="Internal").ap()
    w1c = nc.dram_tensor("w1_cache", [2, 128, 32 * 1024], BF16, kind="Internal").ap()
    dbg_t = {t: nc.dram_tensor("dbg_" + t, [128, DC * S], F32,
                               kind="ExternalOutput").ap() for t in dump}

    es = contextlib.ExitStack()
    with es:
        P = Prog(nc, es)
        dL, dC = P.dL, P.dC
        sP, sA, sV, sG = P.own["pe"], P.own["act"], P.own["dve"], P.own["pool"]

        _uid = [0]

        def _un(name):
            _uid[0] += 1
            return "%s_%d" % (name, _uid[0])

        def sb(stack, name, shape, dt=F32, side=None):
            return stack.enter_context(
                nc.sbuf_tensor(_un(name), shape, dt, side=side))

        def psum(stack, name, shape, dt=F32):
            return stack.enter_context(nc.psum_tensor(_un(name), shape, dt))

        rowMU = sb(es, "rowMU", [1, S])
        rowSC = sb(es, "rowSC", [1, S])
        cst = sb(es, "cst", [128, 8])
        ones32 = sb(es, "ones32", [1, 512])
        onesb = sb(es, "onesb", [1, 512], BF16)
        kcol32 = sb(es, "kcol32", [128, 1])
        kcolb = sb(es, "kcolb", [128, 1], BF16)
        idb = sb(es, "idb", [128, 128], BF16)
        idf8 = sb(es, "idf8", [8, 8])

        P.do("sp", "dma_start", out=cst[:], in_=CN["CST"], inc=dL)
        P.do("sp", "dma_start", out=ones32[:], in_=CN["ONES32"], inc=dL)
        P.do("sp", "dma_start", out=idf8[:], in_=CN["IDF"][0:8, 0:8], inc=dL)
        P.do("pool", "dma_start", out=onesb[:], in_=CN["ONESB"], inc=dC)
        P.do("pool", "dma_start", out=idb[:], in_=CN["IDF"], inc=dC)
        P.barrier()
        P.do("dve", "memset", kcol32[:], 1.0 / 1024.0)
        P.do("dve", "memset", kcolb[:], 1.0 / 1024.0, inc=sV)
        P.barrier()

        EPS5 = cst[:, 0:1]
        EPS6 = cst[:, 1:2]



        def load_bf16(dst_ap, src_ap):
            return P.do("pool", "dma_start", out=dst_ap, in_=src_ap, inc=dC)

        def load_f32(dst_ap, src_ap):
            return P.do("sp", "dma_start", out=dst_ap, in_=src_ap, inc=dL)

        def w_fm(dst, src2d):
            nch = src2d.shape[0] // 128
            return load_bf16(dst[:].rearrange("p (c n) -> p c n", c=nch),
                             src2d.rearrange("(c p) n -> p c n", p=128))

        def row_bf16(dst, vec):
            return load_bf16(dst, vec.rearrange("(o d) -> o d", o=1))

        def ln_stats(src_fm, mu_row, sc_row, eps_col, dbl=False, sc2_row=None):
            P.barrier(dma=False)
            with contextlib.ExitStack() as st:
                pr = [psum(st, "lnp%d" % i, [1, 512], F32) for i in range(4)]
                sq = sb(st, "lnsq", [128, DC * S], BF16)
                msq = sb(st, "lnmsq", [1, S])
                t1 = sb(st, "lnt1", [1, S])
                uu = sb(st, "lnu", [1, S])
                cast_done = {}
                for c in range(DC):
                    if c % 2 == 0:
                        P.do("dve", "tensor_copy", sq[:, c * S:(c + 1) * S],
                             src_fm[:, c * S:(c + 1) * S], inc=sV)
                        cast_done[c] = (sV, sV.n)
                    else:
                        P.do("act", "copy", sq[:, c * S:(c + 1) * S],
                             src_fm[:, c * S:(c + 1) * S], inc=sA)
                        cast_done[c] = (sA, sA.n)
                for c in range(DC):
                    P.wait("pe", cast_done[c][0], cast_done[c][1])
                    for sl in range(4):
                        P.do("pe", "matmul", pr[sl][0:1, :], kcolb[:],
                             sq[:, c * S + 512 * sl:c * S + 512 * (sl + 1)],
                             start=(c == 0), stop=(c == DC - 1),
                             inc=(sP if (c == DC - 1 and sl == 3) else None))
                P.wait("act", sP)
                for sl in range(4):
                    P.do("act", "copy", mu_row[:, 512 * sl:512 * (sl + 1)],
                         pr[sl][0:1, :], inc=(sA if sl == 3 else None))
                mu_done = sA.n
                P.wait("dve", sP)
                sq_done = {}
                for c in range(DC):
                    P.do("dve", "tensor_tensor", sq[:, c * S:(c + 1) * S],
                         sq[:, c * S:(c + 1) * S], sq[:, c * S:(c + 1) * S],
                         ALU.mult, inc=sV)
                    sq_done[c] = sV.n
                    if c == 1:
                        P.wait("dve", sA, mu_done)
                        P.do("dve", "tensor_tensor", t1[0:1, :], mu_row,
                             mu_row, ALU.mult, inc=sV)
                P.wait("pe", sA, mu_done)
                for c in range(DC):
                    P.wait("pe", sV, sq_done[c])
                    for sl in range(4):
                        P.do("pe", "matmul", pr[sl][0:1, :], kcolb[:],
                             sq[:, c * S + 512 * sl:c * S + 512 * (sl + 1)],
                             start=(c == 0), stop=(c == DC - 1),
                             inc=(sP if (c == DC - 1 and sl == 3) else None))
                P.wait("act", sP)
                for sl in range(4):
                    P.do("act", "copy", msq[0:1, 512 * sl:512 * (sl + 1)],
                         pr[sl][0:1, :], inc=(sA if sl == 3 else None))
                P.wait("dve", sA)
                P.do("dve", "tensor_tensor", t1[0:1, :], msq[0:1, :], t1[0:1, :],
                     ALU.subtract, inc=sV)
                P.wait("act", sV)
                P.do("act", "activation", uu[0:1, :], t1[0:1, :], AF.Ln,
                     bias=eps_col[0:1, :])
                P.do("act", "activation", sc_row, uu[0:1, :],
                     AF.Exp, scale=-0.5, inc=sA)
                if dbl:
                    P.wait("dve", sA)
                    P.do("dve", "tensor_tensor", uu[0:1, :], sc_row, sc_row,
                         ALU.mult)
                    P.do("dve", "tensor_tensor", uu[0:1, :], t1[0:1, :],
                         uu[0:1, :], ALU.mult, inc=sV)
                    P.wait("act", sV)
                    P.do("act", "activation", uu[0:1, :], uu[0:1, :], AF.Ln,
                         bias=eps_col[0:1, :])
                    P.do("act", "activation", uu[0:1, :], uu[0:1, :], AF.Exp,
                         scale=-0.5, inc=sA)
                    P.wait("dve", sA)
                    P.do("dve", "tensor_tensor", sc2_row, sc_row, uu[0:1, :],
                         ALU.mult, inc=sV)
            P.barrier(dma=False)

        def bcast(src_row, width, dst32=None, dstb=None, plus1=False):
            P.barrier(dma=False)
            with contextlib.ExitStack() as st:
                pb = [psum(st, "bcp%d" % i, [128, 512], F32) for i in range(2)]
                rb16 = sb(st, "bcr", [1, width], BF16)
                P.do("act", "copy", rb16[0:1, :], src_row[:, 0:width], inc=sA)
                P.wait("pe", sA)
                nsl = (width + 511) // 512
                edone = [0, 0]
                for sl in range(nsl):
                    w = min(512, width - 512 * sl)
                    bank = sl % 2
                    P.wait("pe", sA, edone[bank])
                    P.do("pe", "matmul", pb[bank][:, 0:w], onesb[0:1, 0:128],
                         rb16[0:1, 512 * sl:512 * sl + w], start=True,
                         stop=not plus1)
                    if plus1:
                        P.do("pe", "matmul", pb[bank][:, 0:w], onesb[0:1, 0:128],
                             onesb[0:1, 0:w], start=False, stop=True)
                    P.inc("pe", sP)
                    P.wait("act", sP)
                    if dst32 is not None:
                        P.do("act", "copy", dst32[:, 512 * sl:512 * sl + w],
                             pb[bank][:, 0:w],
                             inc=(None if dstb is not None else sA))
                    if dstb is not None:
                        P.do("act", "copy", dstb[:, 512 * sl:512 * sl + w],
                             pb[bank][:, 0:w], inc=sA)
                    edone[bank] = sA.n
            P.barrier(dma=False)

        def bcast2(mu_src, sc_src, ns2, bcA, bcBb, bcB=None):
            P.barrier(dma=False)
            with contextlib.ExitStack() as st:
                pb = [psum(st, "bcp%d" % i, [128, 512], F32) for i in range(2)]
                rb16 = sb(st, "bcr2", [1, 2 * ns2], BF16)
                P.do("act", "copy", rb16[0:1, 0:ns2], mu_src, inc=sA)
                mu_cast = sA.n
                P.do("dve", "tensor_copy", rb16[0:1, ns2:2 * ns2], sc_src,
                     inc=sV)
                sc_cast = sV.n
                nsl = (ns2 + 511) // 512
                edA, edV = 0, 0
                for sl in range(nsl):
                    w = min(512, ns2 - 512 * sl)
                    P.wait("pe", sA, mu_cast)
                    P.wait("pe", sA, edA)
                    P.do("pe", "matmul", pb[0][:, 0:w], onesb[0:1, 0:128],
                         rb16[0:1, 512 * sl:512 * sl + w],
                         start=True, stop=True, inc=sP)
                    P.wait("act", sP)
                    P.do("act", "copy", bcA[:, 512 * sl:512 * sl + w],
                         pb[0][:, 0:w], inc=sA)
                    edA = sA.n
                    P.wait("pe", sV, sc_cast)
                    P.wait("pe", sV, edV)
                    if bcB is not None:
                        P.wait("pe", sA, edA)
                    P.do("pe", "matmul", pb[1][:, 0:w], onesb[0:1, 0:128],
                         rb16[0:1, ns2 + 512 * sl:ns2 + 512 * sl + w],
                         start=True, stop=True, inc=sP)
                    P.wait("dve", sP)
                    P.do("dve", "tensor_copy",
                         bcBb[:, 512 * sl:512 * sl + w], pb[1][:, 0:w],
                         inc=sV)
                    edV = sV.n
                    if bcB is not None:
                        P.wait("act", sP)
                        P.do("act", "copy", bcB[:, 512 * sl:512 * sl + w],
                             pb[1][:, 0:w], inc=sA)
                        edA = sA.n
            P.barrier(dma=False)

        def vec_fm(stack, name, src8x128):
            t8 = sb(stack, name + "_r", [8, 128])
            out = sb(stack, name, [128, 8])
            load_f32(t8[:], src8x128)
            P.barrier(dma=False)
            P.wait("pe", P.dL)
            with contextlib.ExitStack() as st:
                pp = psum(st, name + "_p", [128, 8], F32)
                P.do("pe", "transpose", pp[:], t8[:], idf8[:], inc=sP)
                P.wait("dve", sP)
                P.do("dve", "tensor_copy", out[:], pp[:], inc=sV)
            P.barrier(dma=False)
            return out

        def apply_ln(src_fm, dst, mu_row, sc_row, gcol=None, bcol=None,
                     s0=0, ns=S, dst_f32=False, src_dram=None,
                     tail_barrier=True):
            with contextlib.ExitStack() as st:
                bcA = sb(st, "bcAl", [128, ns])
                bcB = sb(st, "bcBl", [128, ns]) if dst_f32 else None
                bcBb = sb(st, "bcBbl", [128, ns], BF16)
                if dst_f32:
                    bcast(mu_row[:, s0:s0 + ns], ns, dst32=bcA)
                    bcast(sc_row[:, s0:s0 + ns], ns, dstb=bcBb, dst32=bcB)
                else:
                    bcast2(mu_row[:, s0:s0 + ns], sc_row[:, s0:s0 + ns], ns,
                           bcA, bcBb)
                t = sb(st, "apl_t", [128, ns], F32 if dst_f32 else BF16)
                tmp = sb(st, "apl_s", [128, ns]) if src_dram is not None else None
                for c in range(DC):
                    if src_dram is not None:
                        P.wait("sp", sG)
                        P.wait("sp", sV)
                        P.do("sp", "dma_start", out=tmp[:],
                             in_=src_dram[:, c * S + s0:c * S + s0 + ns], inc=dL)
                        P.wait("dve", dL)
                        srcap = tmp[:]
                    else:
                        srcap = src_fm[:, c * S + s0:c * S + s0 + ns]
                    P.wait("dve", sG)
                    P.do("dve", "tensor_tensor", t[:], srcap,
                         bcA[:], ALU.subtract, inc=sV)
                    P.wait("pool", sV)
                    dd = dst[:, c * ns:(c + 1) * ns]
                    if dst_f32:
                        P.do("pool", "tensor_tensor", dd, t[:],
                             bcB[:], ALU.mult, inc=sG)
                    elif gcol is None:
                        P.do("pool", "tensor_tensor", dd, t[:],
                             bcBb[:], ALU.mult, inc=sG)
                    else:
                        P.do("pool", "tensor_tensor", t[:], t[:],
                             bcBb[:], ALU.mult, inc=sG)
                        P.wait("dve", sG)
                        P.do("dve", "tensor_scalar", dd, t[:],
                             gcol[:, c:c + 1], bcol[:, c:c + 1],
                             ALU.mult, ALU.add, inc=sV)
            done = sV.n
            if tail_barrier:
                P.barrier(dma=False)
            return done

        def dump_sbuf(tag, src_fm):
            if tag in dump:
                P.barrier()
                P.do("sp", "dma_start", out=dbg_t[tag], in_=src_fm[:], inc=dL)
                P.barrier()

        def dump_dram(tag):
            if tag in dump:
                P.barrier()
                with contextlib.ExitStack() as st:
                    t = sb(st, "dmp", [128, S])
                    for c in range(DC):
                        P.do("sp", "dma_start", out=t[:],
                             in_=r_sp[:, c * S:(c + 1) * S], inc=dL)
                        P.wait("sp", dL)
                        P.do("sp", "dma_start",
                             out=dbg_t[tag][:, c * S:(c + 1) * S], in_=t[:],
                             inc=dL)
                        P.wait("sp", dL)
                P.barrier()

        def stage_A():
            with contextlib.ExitStack() as sa:
                h = sb(sa, "h", [128, DC * S], BF16)
                with contextlib.ExitStack() as s0:
                    rb = sb(s0, "rbA", [128, DC * S])
                    load_f32(rb[:].rearrange("p (c s) -> p c s", c=DC),
                             xT.rearrange("(c p) s -> p c s", p=128))
                    P.barrier()
                    rowCS = sb(s0, "rowCS", [1, S])
                    ln_stats(rb, rowMU[0:1, :], rowSC[0:1, :], EPS5, dbl=True, sc2_row=rowCS[0:1, :])
                    with contextlib.ExitStack() as gbs:
                        g_fm = vec_fm(gbs, "a_gfm",
                                      aB["ln_g"].rearrange("(c p) -> c p", c=8))
                        b_fm = vec_fm(gbs, "a_bfm",
                                      aB["ln_b"].rearrange("(c p) -> c p", c=8))
                        apply_ln(rb, h, rowMU[0:1, :], rowCS[0:1, :], gcol=g_fm, bcol=b_fm)
                    P.do("sp", "dma_start", out=r_sp, in_=rb[:], inc=dL)
                    P.barrier()

                with contextlib.ExitStack() as s12:
                    kvb = sb(s12, "kvb", [128, DC * 1025], BF16)
                    with contextlib.ExitStack() as s1:
                        kvacc = sb(s1, "kvacc", [128, DC * 1025])
                        wk = sb(s1, "wk", [128, DC * D], BF16)
                        wv = sb(s1, "wv", [128, DC * D], BF16)
                        brk = sb(s1, "brk", [1, D], BF16)
                        brv = sb(s1, "brv", [1, D], BF16)
                        w_fm(wk, aW["wk_lin"])
                        w_fm(wv, aW["wv_lin"])
                        row_bf16(brk[:], aB["bk_lin"])
                        row_bf16(brv[:], aB["bv_lin"])
                        ktok = sb(s1, "ktok", [128, 4 * D], BF16)
                        vtok = sb(s1, "vtok", [128, 4 * 1025], BF16)
                        scrf = sb(s1, "scrfA", [128, 512])
                        P.do("dve", "memset", kvacc[:], 0.0)
                        P.do("dve", "memset",
                             vtok[:].rearrange("p (t c) -> p t c", c=1025)
                             [:, :, 1024:1025], 1.0, inc=sV)
                        P.barrier()
                        with contextlib.ExitStack() as ps1:
                            pk = [psum(ps1, "pk%d" % i, [128, 512], F32)
                                  for i in range(4)]
                            pv2 = [psum(ps1, "pv%d" % i, [128, 512], F32)
                                   for i in range(2)]
                            pks = psum(ps1, "pks", [128, 512], F32)
                            kfree = {0: 0, 1: 0, 2: 0, 3: 0}
                            accdone = 0
                            for stile in range(4):
                                for tl in range(4):
                                    t0 = stile * 512 + tl * 128
                                    for proj in range(2):
                                        wt, brow = (wk, brk) if proj == 0 else (wv, brv)
                                        for hf in range(2):
                                            bank = proj * 2 + hf
                                            P.wait("pe", sV, kfree[bank])
                                            for c in range(DC):
                                                P.do("pe", "matmul", pk[bank][:],
                                                     h[:, c * S + t0:c * S + t0 + 128],
                                                     wt[:, c * D + hf * 512:c * D + (hf + 1) * 512],
                                                     start=(c == 0), stop=False)
                                            P.do("pe", "matmul", pk[bank][:],
                                                 onesb[0:1, 0:128],
                                                 brow[0:1, hf * 512:(hf + 1) * 512],
                                                 start=False, stop=True, inc=sP)
                                            P.wait("dve", sP)
                                            if proj == 0:
                                                dst = ktok[:, tl * D + hf * 512:tl * D + (hf + 1) * 512]
                                                P.do("dve", "tensor_scalar_min",
                                                     scrf[:], pk[bank][:], 0.0,
                                                     inc=sV)
                                                P.wait("act", sV)
                                                P.do("act", "activation", scrf[:],
                                                     scrf[:], AF.Exp, inc=sA)
                                                P.wait("dve", sA)
                                                P.do("dve", "scalar_tensor_tensor",
                                                     dst, pk[bank][:], 0.0, scrf[:],
                                                     ALU.max, ALU.add, inc=sV)
                                            else:
                                                dst = vtok[:, tl * 1025 + hf * 512:tl * 1025 + (hf + 1) * 512]
                                                P.do("dve", "tensor_copy", dst,
                                                     pk[bank][:], inc=sV)
                                            kfree[bank] = sV.n
                                ktv_ready = sV.n
                                for dt in range(DC):
                                    P.wait("pe", sV, ktv_ready)
                                    P.wait("pe", sV, accdone)
                                    for tl in range(4):
                                        st_, sp_ = (tl == 0), (tl == 3)
                                        for hf in range(2):
                                            P.do("pe", "matmul", pv2[hf][:],
                                                 ktok[:, tl * D + dt * 128:tl * D + dt * 128 + 128],
                                                 vtok[:, tl * 1025 + hf * 512:tl * 1025 + (hf + 1) * 512],
                                                 start=st_, stop=sp_)
                                        P.do("pe", "matmul", pks[:, dt:dt + 1],
                                             ktok[:, tl * D + dt * 128:tl * D + dt * 128 + 128],
                                             vtok[:, tl * 1025 + 1024:tl * 1025 + 1025],
                                             start=st_, stop=sp_,
                                             inc=(sP if sp_ else None))
                                    P.wait("dve", sP)
                                    for hf in range(2):
                                        P.do("dve", "tensor_tensor",
                                             kvacc[:, dt * 1025 + hf * 512:dt * 1025 + (hf + 1) * 512],
                                             kvacc[:, dt * 1025 + hf * 512:dt * 1025 + (hf + 1) * 512],
                                             pv2[hf][:], ALU.add)
                                    P.do("dve", "tensor_tensor",
                                         kvacc[:, dt * 1025 + 1024:dt * 1025 + 1025],
                                         kvacc[:, dt * 1025 + 1024:dt * 1025 + 1025],
                                         pks[:, dt:dt + 1], ALU.add, inc=sV)
                                    accdone = sV.n
                        P.barrier()
                        for dt in range(DC):
                            P.do("act", "copy", kvb[:, dt * 1025:(dt + 1) * 1025],
                                 kvacc[:, dt * 1025:(dt + 1) * 1025],
                                 inc=(sA if dt == DC - 1 else None))
                        P.barrier()

                    with contextlib.ExitStack() as s2:
                        linT = sb(s2, "linT", [128, DC * S], BF16)
                        wq = sb(s2, "wq", [128, DC * D], BF16)
                        brq = sb(s2, "brq", [1, D], BF16)
                        w_fm(wq, aW["wq_lin"])
                        row_bf16(brq[:], aB["bq_lin"])
                        qp = sb(s2, "qp", [128, DC * 512], BF16)
                        ltok = sb(s2, "ltok", [128, D], BF16)
                        scrf = sb(s2, "scrfB", [128, 512])
                        rc = sb(s2, "rc", [128, 1])
                        P.barrier()
                        with contextlib.ExitStack() as ps3:
                            pq = [psum(ps3, "pq%d" % i, [128, 512], F32)
                                  for i in range(2)]
                            pn = [psum(ps3, "pn%d" % i, [128, 512], F32)
                                  for i in range(2)]
                            pd = psum(ps3, "pd", [128, 512], F32)
                            pt = psum(ps3, "ptA", [128, 1024], BF16)
                            qfree = [0, 0]
                            nfree = 0
                            tfree = 0
                            for stile in range(4):
                                s0_ = stile * 512
                                for dco in range(DC):
                                    bank = dco % 2
                                    P.wait("pe", sV, qfree[bank])
                                    for c in range(DC):
                                        P.do("pe", "matmul", pq[bank][:],
                                             wq[:, c * D + dco * 128:c * D + dco * 128 + 128],
                                             h[:, c * S + s0_:c * S + s0_ + 512],
                                             start=(c == 0), stop=False)
                                    P.do("pe", "matmul", pq[bank][:],
                                         brq[0:1, dco * 128:dco * 128 + 128],
                                         onesb[0:1, 0:512], start=False,
                                         stop=True, inc=sP)
                                    P.wait("dve", sP)
                                    P.do("dve", "tensor_scalar_min", scrf[:],
                                         pq[bank][:], 0.0, inc=sV)
                                    P.wait("act", sV)
                                    P.do("act", "activation", scrf[:], scrf[:],
                                         AF.Exp, inc=sA)
                                    P.wait("dve", sA)
                                    P.do("dve", "scalar_tensor_tensor",
                                         qp[:, dco * 512:(dco + 1) * 512],
                                         pq[bank][:], 0.0, scrf[:], ALU.max,
                                         ALU.add, inc=sV)
                                    qfree[bank] = sV.n
                                for tl in range(4):
                                    tg = stile * 4 + tl
                                    P.wait("pe", sV)
                                    P.wait("pe", sV, nfree)
                                    for c in range(DC):
                                        st_, sp_ = (c == 0), (c == DC - 1)
                                        for hf in range(2):
                                            P.do("pe", "matmul", pn[hf][:],
                                                 qp[:, c * 512 + tl * 128:c * 512 + tl * 128 + 128],
                                                 kvb[:, c * 1025 + hf * 512:c * 1025 + (hf + 1) * 512],
                                                 start=st_, stop=sp_)
                                        P.do("pe", "matmul", pd[:, tl:tl + 1],
                                             qp[:, c * 512 + tl * 128:c * 512 + tl * 128 + 128],
                                             kvb[:, c * 1025 + 1024:c * 1025 + 1025],
                                             start=st_, stop=sp_,
                                             inc=(sP if sp_ else None))
                                    P.wait("dve", sP)
                                    P.do("dve", "tensor_scalar_add", rc[:],
                                         pd[:, tl:tl + 1], cst[:, 5:6])
                                    P.do("dve", "reciprocal", rc[:], rc[:])
                                    for hf in range(2):
                                        P.do("dve", "tensor_scalar_mul",
                                             ltok[:, hf * 512:(hf + 1) * 512],
                                             pn[hf][:], rc[:],
                                             inc=(sV if hf == 1 else None))
                                    nfree = sV.n
                                    P.wait("pe", sV)
                                    P.wait("pe", sV, tfree)
                                    for dc2 in range(DC):
                                        P.do("pe", "transpose",
                                             pt[:, dc2 * 128:(dc2 + 1) * 128],
                                             ltok[:, dc2 * 128:(dc2 + 1) * 128],
                                             idb[:],
                                             inc=(sP if dc2 == DC - 1 else None))
                                    P.wait("dve", sP)
                                    P.do("dve", "tensor_copy",
                                         linT[:].rearrange("p (c n) -> p c n", c=DC)
                                         [:, :, tg * 128:tg * 128 + 128],
                                         pt[:].rearrange("p (c n) -> p c n", c=DC),
                                         inc=sV)
                                    tfree = sV.n
                        P.barrier()
                        P.do("sp", "dma_start", out=lin_sp, in_=linT[:], inc=dL)
                        P.barrier()

                winT = sb(sa, "winT", [128, DC * S], BF16)
                with contextlib.ExitStack() as s3:
                    qloc = sb(s3, "qloc", [128, DC * S], BF16)
                    klocT = sb(s3, "klocT", [128, DC * 2112], BF16)
                    vloc = sb(s3, "vloc", [128, 17 * 1040], BF16)
                    vstage = sb(s3, "vstage", [96, 1040], BF16)
                    with contextlib.ExitStack() as ps4:
                        pj = [psum(ps4, "pj%d" % i, [128, 512], F32)
                              for i in range(4)]
                        for wname, bname, kind in (("wq_loc", "bq_loc", "q"),
                                                   ("wk_loc", "bk_loc", "k"),
                                                   ("wv_loc", "bv_loc", "v")):
                            with contextlib.ExitStack() as sw:
                                wt = sb(sw, "wloc", [128, DC * D], BF16)
                                brow = sb(sw, "bloc", [1, D], BF16)
                                w_fm(wt, aW[wname])
                                row_bf16(brow[:], aB[bname])
                                if kind == "k":
                                    P.do("dve", "memset",
                                         klocT[:].rearrange("p (c t) -> p c t", t=2112)
                                         [:, :, 0:32], 0.0)
                                    P.do("dve", "memset",
                                         klocT[:].rearrange("p (c t) -> p c t", t=2112)
                                         [:, :, 2080:2112], 0.0, inc=sV)
                                if kind == "v":
                                    P.do("dve", "memset",
                                         vstage[:].rearrange(
                                             "p (h c) -> p h c", c=65)[:, :, 64:65],
                                         1.0)
                                    P.do("dve", "memset", vloc[0:32, 0:1040], 0.0)
                                    P.do("dve", "memset",
                                         vloc[32:64, 16 * 1040:17 * 1040], 0.0)
                                    P.do("dve", "memset",
                                         vloc[64:128, 16 * 1040:17 * 1040], 0.0)
                                    P.do("dve", "memset",
                                         vloc[:].rearrange("p (t h c) -> p t h c",
                                                           h=16, c=65)[:, :, :, 64:65],
                                         1.0, inc=sV)
                                P.barrier()
                                if kind in ("q", "k"):
                                    pjdone = [0, 0, 0, 0]
                                    for stile in range(4):
                                        s0_ = stile * 512
                                        for dco in range(DC):
                                            bank = dco % 4
                                            P.wait("pe", sA, pjdone[bank])
                                            for c in range(DC):
                                                P.do("pe", "matmul", pj[bank][:],
                                                     wt[:, c * D + dco * 128:c * D + dco * 128 + 128],
                                                     h[:, c * S + s0_:c * S + s0_ + 512],
                                                     start=(c == 0), stop=False)
                                            P.do("pe", "matmul", pj[bank][:],
                                                 brow[0:1, dco * 128:dco * 128 + 128],
                                                 onesb[0:1, 0:512], start=False,
                                                 stop=True, inc=sP)
                                            P.wait("act", sP)
                                            if kind == "q":
                                                dd = qloc[:, dco * S + s0_:dco * S + s0_ + 512]
                                            else:
                                                dd = klocT[:, dco * 2112 + 32 + s0_:
                                                           dco * 2112 + 32 + s0_ + 512]
                                            P.do("act", "copy", dd, pj[bank][:],
                                                 inc=sA)
                                            pjdone[bank] = sA.n
                                else:
                                    for tg in range(17):
                                        if tg == 0:
                                            m0, mm_, pofs = 0, 96, 32
                                        elif tg == 16:
                                            m0, mm_, pofs = 2016, 32, 0
                                        else:
                                            m0, mm_, pofs = tg * 128 - 32, 128, 0
                                        P.wait("pe", sA)
                                        for c in range(DC):
                                            st_ = (c == 0)
                                            for hf in range(2):
                                                P.do("pe", "matmul",
                                                     pj[hf][0:mm_, :],
                                                     h[:, c * S + m0:c * S + m0 + mm_],
                                                     wt[:, c * D + hf * 512:c * D + (hf + 1) * 512],
                                                     start=st_, stop=False)
                                        for hf in range(2):
                                            P.do("pe", "matmul", pj[hf][0:mm_, :],
                                                 onesb[0:1, 0:mm_],
                                                 brow[0:1, hf * 512:(hf + 1) * 512],
                                                 start=False, stop=True,
                                                 inc=(sP if hf == 1 else None))
                                        P.wait("act", sP)
                                        for hf in range(2):
                                            srcp = pj[hf][0:mm_, :].rearrange(
                                                "p (h c) -> p h c", c=64)
                                            if pofs == 0:
                                                dsta = vloc[0:mm_,
                                                            tg * 1040:(tg + 1) * 1040].rearrange(
                                                    "p (h c) -> p h c", c=65)[:, 8 * hf:8 * (hf + 1), 0:64]
                                                P.do("act", "copy", dsta, srcp,
                                                     inc=sA)
                                            else:
                                                dsta = vstage[0:mm_, :].rearrange(
                                                    "p (h c) -> p h c", c=65)[:, 8 * hf:8 * (hf + 1), 0:64]
                                                P.do("act", "copy", dsta, srcp,
                                                     inc=sA)
                                        if pofs != 0:
                                            P.wait("sp", sA)
                                            P.do("sp", "dma_start",
                                                 out=vloc[pofs:pofs + mm_,
                                                          tg * 1040:(tg + 1) * 1040],
                                                 in_=vstage[0:mm_, :], inc=dL)
                                P.barrier()
                    mask8 = sb(s3, "mask8", [128, HA * 256])
                    with contextlib.ExitStack() as sm:
                        vbt = sb(sm, "vbt", [128, 256])
                        load_f32(mask8[:], TOEP)
                        load_f32(vbt[:], CN["VBIG"])
                        P.barrier()
                        for hh in range(HA):
                            P.do("dve", "scalar_tensor_tensor",
                                 mask8[:, hh * 256:(hh + 1) * 256],
                                 mask8[:, hh * 256:(hh + 1) * 256], 8.0, vbt[:],
                                 ALU.mult, ALU.add, inc=sV)
                        P.barrier()
                    scm = sb(s3, "scm", [128, 4 * 256])
                    pexp = sb(s3, "pexp", [128, 4 * 256], BF16)
                    win2 = sb(s3, "win2", [128, 256], BF16)
                    rc2 = sb(s3, "rc2", [128, 4])
                    P.do("dve", "memset", scm[:], -1e9, inc=sV)
                    P.barrier()
                    with contextlib.ExitStack() as ps5:
                        psc = [psum(ps5, "psc%d" % i, [128, 256], F32)
                               for i in range(4)]
                        pw = [psum(ps5, "pw%d" % i, [128, 512], F32)
                              for i in range(3)]
                        ptw = psum(ps5, "ptw", [128, 256], BF16)
                        for i in range(4):
                            P.do("dve", "memset", psc[i][64:128, 128:256], 0.0,
                                 inc=sV)
                        P.barrier(dma=False)
                        scfree = [0, 0, 0, 0]
                        scm_free = [0, 0, 0, 0]
                        pexp_done = [0, 0, 0, 0]
                        wfree = {0: 0, 1: 0, 2: 0}
                        wdone = {0: 0, 1: 0, 2: 0}
                        twfree = [0, 0]
                        for B_ in range(TC):
                            qs = B_ * 128
                            for hp in range(8):
                                wb = pw[hp % 3]
                                wcol = 130 * (hp // 3)
                                for k2 in range(2):
                                    hh = 2 * hp + k2
                                    si = hh % 4
                                    sc = psc[si]
                                    pb = 64 * (hh % 2)
                                    so = 256 * si
                                    P.wait("pe", sV, scfree[si])
                                    lT0 = klocT[pb:pb + 64,
                                                hp * 2112 + qs:hp * 2112 + qs + 128]
                                    lT1 = klocT[pb:pb + 64,
                                                hp * 2112 + qs + 128:hp * 2112 + qs + 192]
                                    rQ = qloc[pb:pb + 64, hp * S + qs:hp * S + qs + 128]
                                    P.do("pe", "matmul", sc[:, 0:128], lT0, rQ,
                                         start=True, stop=True)
                                    P.do("pe", "matmul", sc[0:64, 128:256], lT1,
                                         rQ, start=True, stop=True, inc=sP)
                                    P.wait("dve", sP)
                                    P.wait("dve", sA, scm_free[si])
                                    P.do("dve", "tensor_tensor",
                                         scm[:, so:so + 256], sc[:],
                                         mask8[:, hh * 256:hh * 256 + 256],
                                         ALU.add, inc=sV)
                                    scfree[si] = sV.n
                                    if B_ == 0:
                                        P.do("dve", "memset",
                                             scm[0:32, so:so + 128], -1e9,
                                             inc=sV)
                                    if B_ == TC - 1:
                                        P.do("dve", "memset",
                                             scm[32:64, so + 128:so + 256],
                                             -1e9, inc=sV)
                                    P.wait("act", sV)
                                    P.wait("act", sP, pexp_done[si])
                                    P.do("act", "activation",
                                         pexp[:, so:so + 256], scm[:, so:so + 256],
                                         AF.Exp, scale=0.125, inc=sA)
                                    scm_free[si] = sA.n
                                    P.wait("pe", sA)
                                    P.wait("pe", sV, wfree[hp % 3])
                                    c0 = wcol + 65 * k2
                                    P.do("pe", "matmul", wb[:, c0:c0 + 65],
                                         pexp[:, so:so + 128],
                                         vloc[:, B_ * 1040 + hh * 65:B_ * 1040 + hh * 65 + 65],
                                         start=True, stop=False)
                                    P.do("pe", "matmul", wb[:, c0:c0 + 65],
                                         pexp[0:64, so + 128:so + 256],
                                         vloc[0:64, (B_ + 1) * 1040 + hh * 65:(B_ + 1) * 1040 + hh * 65 + 65],
                                         start=False, stop=True, inc=sP)
                                    pexp_done[si] = sP.n
                                    wdone[hp % 3] = sP.n
                                wsl = hp % 2
                                P.wait("dve", sP, wdone[hp % 3])
                                rview = wb[:, wcol:wcol + 130].rearrange(
                                    "p (h c) -> p h c", c=65)[:, :, 64:65].rearrange(
                                    "p h o -> p (h o)")
                                P.do("dve", "reciprocal", rc2[:, 2 * wsl:2 * wsl + 2],
                                     rview)
                                P.wait("dve", sP, twfree[wsl])
                                P.do("dve", "tensor_scalar_mul",
                                     win2[:, 128 * wsl:128 * wsl + 64],
                                     wb[:, wcol:wcol + 64], rc2[:, 2 * wsl:2 * wsl + 1])
                                P.do("dve", "tensor_scalar_mul",
                                     win2[:, 128 * wsl + 64:128 * wsl + 128],
                                     wb[:, wcol + 65:wcol + 129],
                                     rc2[:, 2 * wsl + 1:2 * wsl + 2],
                                     inc=sV)
                                wfree[hp % 3] = sV.n
                                P.wait("pe", sV, wfree[hp % 3])
                                P.do("pe", "transpose", ptw[:, 128 * wsl:128 * wsl + 128],
                                     win2[:, 128 * wsl:128 * wsl + 128], idb[:],
                                     inc=sP)
                                twfree[wsl] = sP.n
                                P.wait("dve", sP)
                                P.do("dve", "tensor_copy",
                                     winT[:, hp * S + qs:hp * S + qs + 128],
                                     ptw[:, 128 * wsl:128 * wsl + 128], inc=sV)
                    P.barrier()

                with contextlib.ExitStack() as s4:
                    lsl = sb(s4, "lsl", [128, DC * 512], BF16)
                    ghid = sb(s4, "ghid", [128, DC * S], BF16)

                    def load_lsl(sl):
                        return load_bf16(
                            lsl[:].rearrange("p (c n) -> p c n", c=DC),
                            lin_sp.rearrange("p (c s) -> p c s", c=DC)
                            [:, :, sl * 512:(sl + 1) * 512])

                    with contextlib.ExitStack() as sg1:
                        wg1 = sb(sg1, "wg1", [128, 24 * D], BF16)
                        brg1 = sb(sg1, "brg1", [1, D], BF16)
                        w_fm(wg1, aW["wg1"])
                        row_bf16(brg1[:], aB["bg1"])
                        P.barrier()
                        with contextlib.ExitStack() as ps6:
                            pg = [psum(ps6, "pg%d" % i, [128, 512], F32)
                                  for i in range(2)]
                            for sl in range(4):
                                P.barrier()
                                load_lsl(sl)
                                P.barrier()
                                for dgt in range(DC):
                                    bank = dgt % 2
                                    P.wait("pe", sA)
                                    first = True
                                    for c in range(DC):
                                        P.do("pe", "matmul", pg[bank][:],
                                             wg1[:, c * D + dgt * 128:c * D + dgt * 128 + 128],
                                             lsl[:, c * 512:(c + 1) * 512],
                                             start=first, stop=False)
                                        first = False
                                    for c in range(DC):
                                        P.do("pe", "matmul", pg[bank][:],
                                             wg1[:, (8 + c) * D + dgt * 128:(8 + c) * D + dgt * 128 + 128],
                                             winT[:, c * S + sl * 512:c * S + (sl + 1) * 512],
                                             start=False, stop=False)
                                    for c in range(DC):
                                        P.do("pe", "matmul", pg[bank][:],
                                             wg1[:, (16 + c) * D + dgt * 128:(16 + c) * D + dgt * 128 + 128],
                                             h[:, c * S + sl * 512:c * S + (sl + 1) * 512],
                                             start=False, stop=False)
                                    P.do("pe", "matmul", pg[bank][:],
                                         brg1[0:1, dgt * 128:dgt * 128 + 128],
                                         onesb[0:1, 0:512], start=False,
                                         stop=True, inc=sP)
                                    P.wait("act", sP)
                                    P.do("act", "activation",
                                         ghid[:, dgt * S + sl * 512:dgt * S + (sl + 1) * 512],
                                         pg[bank][:], AF.Relu, inc=sA)
                        P.barrier()
                    cb = sb(s4, "cb", [128, DC * S], BF16)
                    with contextlib.ExitStack() as sg2:
                        wg2 = sb(sg2, "wg2", [128, DC * D], BF16)
                        brg2 = sb(sg2, "brg2", [1, D], BF16)
                        gsig = sb(sg2, "gsig", [128, 512], BF16)
                        d1 = sb(sg2, "d1", [128, 512], BF16)
                        w_fm(wg2, aW["wg2"])
                        row_bf16(brg2[:], aB["bg2"])
                        P.barrier()
                        with contextlib.ExitStack() as ps7:
                            pg = [psum(ps7, "pg2%d" % i, [128, 512], F32)
                                  for i in range(2)]
                            gdone = 0
                            for sl in range(4):
                                P.barrier()
                                load_lsl(sl)
                                P.barrier()
                                for dt in range(DC):
                                    bank = dt % 2
                                    P.wait("pe", sA)
                                    for c in range(DC):
                                        P.do("pe", "matmul", pg[bank][:],
                                             wg2[:, c * D + dt * 128:c * D + dt * 128 + 128],
                                             ghid[:, c * S + sl * 512:c * S + (sl + 1) * 512],
                                             start=(c == 0), stop=False)
                                    P.do("pe", "matmul", pg[bank][:],
                                         brg2[0:1, dt * 128:dt * 128 + 128],
                                         onesb[0:1, 0:512], start=False,
                                         stop=True, inc=sP)
                                    P.wait("act", sP)
                                    P.wait("act", sV, gdone)
                                    P.do("act", "activation", gsig[:],
                                         pg[bank][:], AF.Sigmoid, inc=sA)
                                    co = dt * S + sl * 512
                                    P.wait("pool", sV, gdone)
                                    P.do("pool", "tensor_tensor", d1[:],
                                         lsl[:, dt * 512:(dt + 1) * 512],
                                         winT[:, co:co + 512], ALU.subtract,
                                         inc=sG)
                                    P.wait("dve", sA)
                                    P.wait("dve", sG)
                                    P.do("dve", "tensor_tensor", gsig[:],
                                         gsig[:], d1[:], ALU.mult)
                                    P.do("dve", "tensor_tensor",
                                         cb[:, co:co + 512], gsig[:],
                                         winT[:, co:co + 512], ALU.add, inc=sV)
                                    gdone = sV.n
                        P.barrier()
                    with contextlib.ExitStack() as s5:
                        wo = sb(s5, "wo", [128, DC * D], BF16)
                        bro_ = sb(s5, "bro", [1, D], BF16)
                        of1 = sb(s5, "of1", [1, S], BF16)
                        rt = sb(s5, "rt", [128, S])
                        w_fm(wo, aW["wo"])
                        row_bf16(bro_[:], aB["bo"])
                        sc1p = sb(s5, "sc1p", [128, S])
                        P.do("dve", "tensor_tensor", rowMU[0:1, :],
                             rowMU[0:1, :], rowSC[0:1, :], ALU.mult)
                        P.do("dve", "tensor_scalar_mul", of1[:],
                             rowMU[0:1, :], -1.0, inc=sV)
                        P.barrier()
                        bcast(rowSC[0:1, :], S, dst32=sc1p, plus1=True)
                        with contextlib.ExitStack() as ps8:
                            po = [psum(ps8, "po%d" % i, [128, 512], F32)
                                  for i in range(2)]
                            for dt in range(DC):
                                P.wait("sp", sV)
                                P.wait("sp", dL)
                                P.do("sp", "dma_start", out=rt[:],
                                     in_=r_sp[:, dt * S:(dt + 1) * S], inc=dL)
                                for sl in range(4):
                                    bank = sl % 2
                                    P.wait("pe", sV)
                                    for c in range(DC):
                                        P.do("pe", "matmul", po[bank][:],
                                             wo[:, c * D + dt * 128:c * D + dt * 128 + 128],
                                             cb[:, c * S + sl * 512:c * S + (sl + 1) * 512],
                                             start=(c == 0), stop=False)
                                    P.do("pe", "matmul", po[bank][:],
                                         bro_[0:1, dt * 128:dt * 128 + 128],
                                         onesb[0:1, 0:512], start=False,
                                         stop=False)
                                    P.do("pe", "matmul", po[bank][:],
                                         onesb[0:1, 0:128],
                                         of1[0:1, sl * 512:(sl + 1) * 512],
                                         start=False, stop=True, inc=sP)
                                    P.wait("dve", dL)
                                    P.wait("dve", sP)
                                    cs_ = slice(sl * 512, (sl + 1) * 512)
                                    P.do("dve", "tensor_tensor", rt[:, cs_],
                                         rt[:, cs_], sc1p[:, cs_], ALU.mult)
                                    P.do("dve", "tensor_tensor", rt[:, cs_],
                                         rt[:, cs_], po[bank][:], ALU.add,
                                         inc=sV)
                                P.wait("sp", sV)
                                P.do("sp", "dma_start",
                                     out=r_sp[:, dt * S:(dt + 1) * S],
                                     in_=rt[:], inc=dL)
                            P.barrier()
            dump_dram("A")

        def stage_ffn(rbuf, tag):
            ln_stats(rbuf, rowMU[0:1, :], rowSC[0:1, :], EPS5)
            with contextlib.ExitStack() as sf:
                xnb = sb(sf, "xnb" + tag, [128, DC * S], BF16)
                apply_ln(rbuf, xnb, rowMU[0:1, :], rowSC[0:1, :])
                wt = sb(sf, "fwt" + tag, [128, DC * D], BF16)
                vr = sb(sf, "fvr" + tag, [128, DC * RK], BF16)
                ct = sb(sf, "fct" + tag, [RK, D], BF16)
                fbr = sb(sf, "fbr" + tag, [1, D], BF16)
                trb = sb(sf, "trb" + tag, [RK, S], BF16)
                w_fm(wt, fWt)
                load_bf16(vr[:].rearrange("p (c n) -> p c n", c=DC),
                          fVr.rearrange("(c p) n -> p c n", p=128))
                load_bf16(ct[:], fCt)
                row_bf16(fbr[:], fB)
                if tag == "B":
                    for li2 in range(2):
                        for cc in range(DC):
                            P.do("pool", "dma_start",
                                 out=w1c[li2].rearrange(
                                     "p (t c f) -> p t c f",
                                     t=32, c=DC)[:, :, cc, :],
                                 in_=bw1[li2].rearrange(
                                     "(c p) (t f) -> c p t f",
                                     p=128, t=32)[cc],
                                 inc=P.dU[0])
                    P.w1c_built = P.dU[0].n
                P.barrier()
                with contextlib.ExitStack() as ps:
                    pf = [psum(ps, "pf%d" % i, [128, 512], F32) for i in range(2)]
                    for sl in range(4):
                        bank = sl % 2
                        P.wait("pe", sA)
                        for c in range(DC):
                            P.do("pe", "matmul", pf[bank][0:RK, :],
                                 vr[:, c * RK:(c + 1) * RK],
                                 xnb[:, c * S + sl * 512:c * S + (sl + 1) * 512],
                                 start=(c == 0), stop=(c == DC - 1),
                                 inc=(sP if c == DC - 1 else None))
                        P.wait("act", sP)
                        P.do("act", "copy", trb[:, sl * 512:(sl + 1) * 512],
                             pf[bank][0:RK, :], inc=sA)
                    P.barrier()
                    bfree = [0, 0]
                    for dt in range(DC):
                        for sl in range(4):
                            bank = sl % 2
                            P.wait("pe", sV, bfree[bank])
                            for c in range(DC):
                                P.do("pe", "matmul", pf[bank][:],
                                     wt[:, c * D + dt * 128:c * D + dt * 128 + 128],
                                     xnb[:, c * S + sl * 512:c * S + (sl + 1) * 512],
                                     start=(c == 0), stop=False)
                            P.do("pe", "matmul", pf[bank][:],
                                 ct[0:RK, dt * 128:dt * 128 + 128],
                                 trb[0:RK, sl * 512:(sl + 1) * 512],
                                 start=False, stop=False)
                            P.do("pe", "matmul", pf[bank][:],
                                 fbr[0:1, dt * 128:dt * 128 + 128],
                                 onesb[0:1, 0:512], start=False, stop=True,
                                 inc=sP)
                            P.wait("dve", sP)
                            co = dt * S + sl * 512
                            P.do("dve", "tensor_tensor", rbuf[:, co:co + 512],
                                 rbuf[:, co:co + 512], pf[bank][:], ALU.add,
                                 inc=sV)
                            bfree[bank] = sV.n
                    P.barrier()

        def stage_C(rbuf, rstack):
            if not hasattr(P, "w1c_built"):
                for li2 in range(2):
                    for cc in range(DC):
                        P.do("pool", "dma_start",
                             out=w1c[li2].rearrange(
                                 "p (t c f) -> p t c f",
                                 t=32, c=DC)[:, :, cc, :],
                             in_=bw1[li2].rearrange(
                                 "(c p) (t f) -> c p t f", p=128, t=32)[cc],
                             inc=P.dU[0])
                P.w1c_built = P.dU[0].n
            ln_stats(rbuf, rowMU[0:1, :], rowSC[0:1, :], EPS5)
            P.do("sp", "dma_start", out=r_sp, in_=rbuf[:], inc=dL)
            P.barrier()
            rstack.close()
            with contextlib.ExitStack() as st:
                hi = sb(st, "tb_h", [128, S])
                lo = sb(st, "tb_l", [128, S])
                tt = sb(st, "tb_t", [128, S], BF16)
                for ix, nm in enumerate(("XCQ", "XSQ", "XCK", "XSK")):
                    load_f32(hi[:], CN[nm + "_h"])
                    load_f32(lo[:], CN[nm + "_l"])
                    P.barrier()
                    P.do("dve", "tensor_tensor", tt[:], hi[:], lo[:],
                         ALU.add, inc=sV)
                    P.barrier()
                    P.do("sp", "dma_start", out=tab_sp[:, ix * S:(ix + 1) * S],
                         in_=tt[:], inc=dL)
                    P.barrier()
            xbst = contextlib.ExitStack()
            xb = sb(xbst, "xb", [128, DC * S], side="right")
            apply_ln(None, xb, rowMU[0:1, :], rowSC[0:1, :], dst_f32=True, src_dram=r_sp)

            for li in range(2):
                L0 = contextlib.ExitStack()
                xnb = sb(L0, "xnbC", [128, DC * S], BF16)
                ygn = sb(L0, "ygn", [128, TC * D], BF16)
                ln_stats(xb, rowMU[0:1, :], rowSC[0:1, :], EPS6)
                with contextlib.ExitStack() as sv_:
                    s1c = vec_fm(sv_, "l1s",
                                 bV["ln1_scale"][li].rearrange("(c p) -> c p", c=8))
                    b1c = vec_fm(sv_, "l1b",
                                 bV["ln1_bias"][li].rearrange("(c p) -> c p", c=8))
                    apply_ln(xb, xnb, rowMU[0:1, :], rowSC[0:1, :], gcol=s1c, bcol=b1c)
                P.do("sp", "dma_start", out=xb_sp, in_=xb[:], inc=dL)
                P.barrier()
                xbst.close()

                L1 = contextlib.ExitStack()
                tabs_t = sb(L1, "tabs_t", [128, 4 * 512], BF16)
                dct = sb(L1, "dct", [128, HB * 128], BF16)
                dct2 = sb(L1, "dct2", [128, HB * 128], BF16)
                gqt = sb(L1, "gqt", [128, HB * 128], BF16)
                gq2t = sb(L1, "gq2t", [128, HB * 128], BF16)
                gkt = sb(L1, "gkt", [128, 2 * HB])
                gct = sb(L1, "gct", [128, HB])
                rott = sb(L1, "rott", [128, 128], BF16)
                gnS = sb(L1, "gnS", [128, D])
                gnB = sb(L1, "gnB", [128, D])
                Sf = sb(L1, "Sf", [128, HB * 128])
                Sb2 = sb(L1, "Sb2", [128, HB * 128], BF16)
                load_bf16(rott[:], CN["ROTT"])
                for ix in range(4):
                    P.do("sp", "dma_start",
                         out=tabs_t[:, ix * 512:(ix + 1) * 512],
                         in_=tab_sp[:, ix * S:ix * S + 512], inc=dL)
                tab_ld = dL.n
                with contextlib.ExitStack() as stx:
                    hi = sb(stx, "ch", [128, HB * 128])
                    lo = sb(stx, "cl", [128, HB * 128])
                    for dst, nm in ((dct, "DCT"), (gqt, "GQ"),
                                    (dct2, "DCT2"), (gq2t, "GQ2")):
                        load_f32(hi[:], CN[nm + "_h"])
                        load_f32(lo[:], CN[nm + "_l"])
                        P.barrier()
                        P.do("dve", "tensor_tensor", dst[:], hi[:], lo[:],
                             ALU.add, inc=sV)
                        P.barrier()
                    for dst, nm in ((gkt[:, 0:HB], "GK"),
                                    (gkt[:, HB:2 * HB], "GK255"),
                                    (gct[:], "GC")):
                        load_f32(hi[:, 0:HB], CN[nm + "_h"])
                        load_f32(lo[:, 0:HB], CN[nm + "_l"])
                        P.barrier()
                        P.do("dve", "tensor_tensor", dst, hi[:, 0:HB],
                             lo[:, 0:HB], ALU.add, inc=sV)
                        P.barrier()
                with contextlib.ExitStack() as sgn:
                    gnr1 = sb(sgn, "gnr1", [1, D])
                    gnr2 = sb(sgn, "gnr2", [1, D])
                    load_f32(gnr1[:], bV["gn_scale"][li:li + 1])
                    load_f32(gnr2[:], bV["gn_bias"][li:li + 1])
                    P.barrier()
                    bcast(gnr1[0:1, :], D, dst32=gnS)
                    bcast(gnr2[0:1, :], D, dst32=gnB)
                P.do("dve", "memset", Sf[:], 0.0)
                P.do("dve", "memset", Sb2[:], 0.0, inc=sV)
                P.barrier()
                with contextlib.ExitStack() as c1:
                    wqq = sb(c1, "wqq", [128, HB * D], BF16)
                    wkk = sb(c1, "wkk", [128, HB * D], BF16)
                    wvv = sb(c1, "wvv", [128, HB * D], BF16)
                    for dst, srcw in ((wqq, bWq), (wkk, bWk), (wvv, bWv)):
                        load_bf16(dst[:].rearrange("p (h c e) -> p h c e",
                                                   h=HB, c=DC),
                                  srcw[li].rearrange("h (c p) e -> p h c e",
                                                     p=128))
                    qf = sb(c1, "qf", [128, 3 * 512], BF16)
                    qx = sb(c1, "qx", [128, HB * 512], BF16)
                    kx = sb(c1, "kx", [128, HB * 512], BF16)
                    vtk = sb(c1, "vtk", [128, 4 * D], BF16)
                    kgt = sb(c1, "kgt", [128, 2 * D], BF16)
                    qg = sb(c1, "qg", [128, 2 * D], BF16)
                    scb = sb(c1, "scb", [128, 3 * D], BF16)
                    ret = sb(c1, "ret", [128, 2 * D], BF16)
                    sqb = sb(c1, "sqb", [128, D], BF16)
                    st8 = sb(c1, "st8", [128, 4 * HB])
                    m1s = sb(c1, "m1s", [128, 512], BF16)
                    m2s = sqb[:, 0:512]
                    P.barrier()
                    TB4 = [tabs_t[:, i * 512:(i + 1) * 512] for i in range(4)]
                    with contextlib.ExitStack() as cp:
                        pbank = [psum(cp, "cb%d" % i, [128, 512], F32)
                                 for i in range(6)]
                        pp = pbank[0:2]
                        prr = psum(cp, "prr", [128, 512], F32)
                        psc2 = pbank[0:2]
                        pou = pbank[2:4]
                        psd = pbank[4:6]
                        ptc = psum(cp, "ptc", [128, 1024], BF16)
                        scb_evac = [[0, 0], [0, 0]]
                        ret_evac = [[0, 0], [0, 0]]
                        scbj_evac = 0
                        supd_done = 0
                        kgt_evac = 0
                        gn_pool_done = 0
                        prj_evac = 0
                        qg_guard = 0
                        pending_gn = []

                        def flush_gn():
                            nonlocal gn_pool_done
                            for fn in pending_gn:
                                fn()
                            pending_gn.clear()
                            gn_pool_done = sG.n
                        for stile in range(4):
                            s0_ = stile * 512
                            P.wait("pool", dL, tab_ld)
                            P.wait("dve", dL, tab_ld)
                            for sel, (ww, dstx, tc_, ts_) in enumerate(
                                    ((wqq, qx, TB4[0], TB4[1]),
                                     (wkk, kx, TB4[2], TB4[3]))):
                                for h_ in range(HB):
                                    bank = h_ % 2
                                    qsl = (sel * 8 + h_) % 3
                                    P.wait("pe", sV, scb_evac[1][bank])
                                    P.wait("pe", sA, prj_evac)
                                    if sel == 0 and h_ == 0:
                                        P.wait("pe", sV, scbj_evac)
                                    for c in range(DC):
                                        P.do("pe", "matmul", pp[bank][:],
                                             ww[:, (h_ * 8 + c) * 128:(h_ * 8 + c) * 128 + 128],
                                             xnb[:, c * S + s0_:c * S + s0_ + 512],
                                             start=(c == 0), stop=(c == DC - 1),
                                             inc=(sP if c == DC - 1 else None))
                                    P.wait("act", sP)
                                    qfs = qf[:, qsl * 512:(qsl + 1) * 512]
                                    P.wait("act", sG)
                                    P.wait("act", sV)
                                    P.do("act", "copy", qfs, pp[bank][:], inc=sA)
                                    prj_evac = sA.n
                                    rb_ = prr if (h_ % 2 == 0) else pou[1]
                                    P.wait("pe", sA)
                                    if h_ % 2 == 1:
                                        P.wait("pe", sA, ret_evac[1][1])
                                    P.do("pe", "matmul", rb_[:], rott[:], qfs,
                                         start=True, stop=True, inc=sP)
                                    P.wait("pool", sA)
                                    P.do("pool", "tensor_tensor", m1s[:], qfs,
                                         tc_, ALU.mult, inc=sG)
                                    P.wait("dve", sP)
                                    P.do("dve", "tensor_tensor", m2s, rb_[:],
                                         ts_, ALU.mult, inc=sV)
                                    P.wait("dve", sG)
                                    P.do("dve", "tensor_tensor",
                                         dstx[:, h_ * 512:(h_ + 1) * 512],
                                         m1s[:], m2s, ALU.add, inc=sV)
                            qk_done = sV.n
                            if stile < 3:
                                P.wait("sp", sV, qk_done)
                                for ix in range(4):
                                    P.do("sp", "dma_start",
                                         out=TB4[ix],
                                         in_=tab_sp[:, ix * S + s0_ + 512:
                                                    ix * S + s0_ + 1024], inc=dL)
                                tab_ld = dL.n
                            for tl in range(4):
                                t0 = s0_ + tl * 128
                                P.wait("pe", sV, scb_evac[1][0])
                                P.wait("pe", sV, scb_evac[1][1])
                                P.wait("pe", sA, prj_evac)
                                for c in range(DC):
                                    for hf in range(2):
                                        P.do("pe", "matmul", pp[hf][:],
                                             xnb[:, c * S + t0:c * S + t0 + 128],
                                             wvv[:].rearrange("p (h c e) -> p h c e",
                                                              h=HB, c=DC)[:, 4 * hf:4 * (hf + 1), c, :],
                                             start=(c == 0), stop=(c == DC - 1),
                                             inc=(sP if (c == DC - 1 and hf == 1) else None))
                                P.wait("act", sP)
                                for hf in range(2):
                                    P.do("act", "copy",
                                         vtk[:, tl * D + hf * 512:tl * D + (hf + 1) * 512],
                                         pp[hf][:], inc=sA)
                                prj_evac = sA.n
                            for pl in range(2):
                                ta, tb = 2 * pl, 2 * pl + 1
                                P.wait("pe", sV, kgt_evac)
                                P.wait("pe", sV, qk_done)
                                for h_ in range(HB):
                                    P.do("pe", "transpose",
                                         ptc[:, h_ * 128:(h_ + 1) * 128],
                                         kx[:, h_ * 512 + ta * 128:h_ * 512 + ta * 128 + 128],
                                         idb[:], inc=(sP if h_ == HB - 1 else None))
                                P.wait("dve", sP)
                                for h_ in range(HB):
                                    P.do("dve", "tensor_scalar_mul",
                                         kgt[:, h_ * 128:(h_ + 1) * 128],
                                         ptc[:, h_ * 128:(h_ + 1) * 128],
                                         gkt[:, HB + h_:HB + h_ + 1],
                                         inc=(sV if h_ == HB - 1 else None))
                                kgt_a_done = sV.n
                                P.wait("pool", sV, qk_done)
                                P.wait("pool", sP, qg_guard)
                                for ch_, tt_, gt_ in ((0, ta, gqt), (1, tb, gq2t)):
                                    P.do("pool", "tensor_tensor",
                                         qg[:, ch_ * D:(ch_ + 1) * D].rearrange(
                                             "p (h e) -> p h e", h=HB),
                                         qx[:].rearrange("p (h c e) -> p h c e",
                                                         h=HB, c=4)[:, :, tt_, :],
                                         gt_[:].rearrange("p (h e) -> p h e", h=HB),
                                         ALU.mult,
                                         inc=(sG if ch_ == 1 else None))
                                qg_done = sG.n
                                flush_gn()
                                for grp in range(2):
                                    P.wait("pe", sV, scb_evac[0][grp])
                                    P.wait("pe", sA, prj_evac)
                                    for h_ in range(4 * grp, 4 * grp + 4):
                                        P.do("pe", "matmul",
                                             psc2[grp][:, (h_ % 4) * 128:(h_ % 4) * 128 + 128],
                                             kx[:, h_ * 512 + ta * 128:h_ * 512 + ta * 128 + 128],
                                             qx[:, h_ * 512 + ta * 128:h_ * 512 + ta * 128 + 128],
                                             start=True, stop=True,
                                             inc=(sP if h_ % 4 == 3 else None))
                                    P.wait("dve", sP)
                                    P.do("dve", "tensor_tensor",
                                         scb[:, grp * 512:(grp + 1) * 512],
                                         psc2[grp][:],
                                         dct[:, grp * 512:(grp + 1) * 512],
                                         ALU.mult, inc=sV)
                                    scb_evac[0][grp] = sV.n
                                P.wait("pe", sV, scbj_evac)
                                for h_ in range(4):
                                    P.do("pe", "matmul",
                                         prr[:, h_ * 128:(h_ + 1) * 128],
                                         kx[:, h_ * 512 + ta * 128:h_ * 512 + ta * 128 + 128],
                                         qx[:, h_ * 512 + tb * 128:h_ * 512 + tb * 128 + 128],
                                         start=True, stop=True,
                                         inc=(sP if h_ == 3 else None))
                                P.wait("dve", sP)
                                P.do("dve", "tensor_tensor",
                                     scb[:, 2 * D:2 * D + 512], prr[:],
                                     dct2[:, 0:512], ALU.mult, inc=sV)
                                scbj0 = sV.n
                                P.wait("pe", sV, kgt_a_done)
                                for h_ in range(HB):
                                    P.do("pe", "transpose",
                                         ptc[:, h_ * 128:(h_ + 1) * 128],
                                         kx[:, h_ * 512 + tb * 128:h_ * 512 + tb * 128 + 128],
                                         idb[:], inc=(sP if h_ == HB - 1 else None))
                                P.wait("dve", sP)
                                for h_ in range(HB):
                                    P.do("dve", "tensor_scalar_mul",
                                         kgt[:, D + h_ * 128:D + (h_ + 1) * 128],
                                         ptc[:, h_ * 128:(h_ + 1) * 128],
                                         gkt[:, h_:h_ + 1],
                                         inc=(sV if h_ == HB - 1 else None))
                                kgt_evac = sV.n
                                for grp in range(2):
                                    P.wait("pe", sV, scb_evac[0][grp])
                                    for h_ in range(4 * grp, 4 * grp + 4):
                                        P.do("pe", "matmul",
                                             psc2[grp][:, (h_ % 4) * 128:(h_ % 4) * 128 + 128],
                                             kx[:, h_ * 512 + tb * 128:h_ * 512 + tb * 128 + 128],
                                             qx[:, h_ * 512 + tb * 128:h_ * 512 + tb * 128 + 128],
                                             start=True, stop=True,
                                             inc=(sP if h_ % 4 == 3 else None))
                                    P.wait("dve", sP)
                                    P.do("dve", "tensor_tensor",
                                         scb[:, D + grp * 512:D + (grp + 1) * 512],
                                         psc2[grp][:],
                                         dct[:, grp * 512:(grp + 1) * 512],
                                         ALU.mult, inc=sV)
                                    scb_evac[1][grp] = sV.n
                                P.wait("pe", sV, scbj0)
                                for h_ in range(4, 8):
                                    P.do("pe", "matmul",
                                         prr[:, (h_ % 4) * 128:(h_ % 4) * 128 + 128],
                                         kx[:, h_ * 512 + ta * 128:h_ * 512 + ta * 128 + 128],
                                         qx[:, h_ * 512 + tb * 128:h_ * 512 + tb * 128 + 128],
                                         start=True, stop=True,
                                         inc=(sP if h_ == 7 else None))
                                P.wait("dve", sP)
                                P.do("dve", "tensor_tensor",
                                     scb[:, 2 * D + 512:3 * D], prr[:],
                                     dct2[:, 512:1024], ALU.mult, inc=sV)
                                scbj_evac = sV.n
                                P.wait("pe", sG, qg_done)
                                for grp in range(2):
                                    P.wait("pe", sA, ret_evac[1][grp])
                                    P.wait("pe", sV, supd_done)
                                    P.wait("pe", sV, scb_evac[0][grp])
                                    for h_ in range(4 * grp, 4 * grp + 4):
                                        cc0 = (h_ % 4) * 128
                                        P.do("pe", "matmul",
                                             pou[grp][:, cc0:cc0 + 128],
                                             scb[:, h_ * 128:(h_ + 1) * 128],
                                             vtk[:, ta * D + h_ * 128:ta * D + (h_ + 1) * 128],
                                             start=True, stop=False)
                                        P.do("pe", "matmul",
                                             pou[grp][:, cc0:cc0 + 128],
                                             qg[:, h_ * 128:(h_ + 1) * 128],
                                             Sb2[:, h_ * 128:(h_ + 1) * 128],
                                             start=False, stop=True,
                                             inc=(sP if h_ % 4 == 3 else None))
                                    P.wait("act", sP)
                                    P.wait("act", sG, gn_pool_done)
                                    P.do("act", "copy",
                                         ret[:, grp * 512:(grp + 1) * 512],
                                         pou[grp][:], inc=sA)
                                    ret_evac[0][grp] = sA.n
                                for grp in range(2):
                                    P.wait("pe", sA, ret_evac[0][grp])
                                    P.wait("pe", sV, scb_evac[1][grp])
                                    P.wait("pe", sV, scbj_evac)
                                    P.wait("pe", sV, kgt_evac)
                                    for h_ in range(4 * grp, 4 * grp + 4):
                                        cc0 = (h_ % 4) * 128
                                        P.do("pe", "matmul",
                                             pou[grp][:, cc0:cc0 + 128],
                                             scb[:, D + h_ * 128:D + (h_ + 1) * 128],
                                             vtk[:, tb * D + h_ * 128:tb * D + (h_ + 1) * 128],
                                             start=True, stop=False)
                                        P.do("pe", "matmul",
                                             pou[grp][:, cc0:cc0 + 128],
                                             scb[:, 2 * D + h_ * 128:2 * D + (h_ + 1) * 128],
                                             vtk[:, ta * D + h_ * 128:ta * D + (h_ + 1) * 128],
                                             start=False, stop=False)
                                        P.do("pe", "matmul",
                                             pou[grp][:, cc0:cc0 + 128],
                                             qg[:, D + h_ * 128:D + (h_ + 1) * 128],
                                             Sb2[:, h_ * 128:(h_ + 1) * 128],
                                             start=False, stop=True)
                                        P.do("pe", "matmul",
                                             psd[grp][:, cc0:cc0 + 128],
                                             kgt[:, h_ * 128:(h_ + 1) * 128],
                                             vtk[:, ta * D + h_ * 128:ta * D + (h_ + 1) * 128],
                                             start=True, stop=False)
                                        P.do("pe", "matmul",
                                             psd[grp][:, cc0:cc0 + 128],
                                             kgt[:, D + h_ * 128:D + (h_ + 1) * 128],
                                             vtk[:, tb * D + h_ * 128:tb * D + (h_ + 1) * 128],
                                             start=False, stop=True,
                                             inc=(sP if h_ % 4 == 3 else None))
                                    P.wait("act", sP)
                                    P.wait("act", sG, gn_pool_done)
                                    P.do("act", "copy",
                                         ret[:, D + grp * 512:D + (grp + 1) * 512],
                                         pou[grp][:], inc=sA)
                                    ret_evac[1][grp] = sA.n
                                qg_guard = sP.n
                                P.wait("dve", sP)
                                for h_ in range(HB):
                                    grp, cc0 = h_ // 4, (h_ % 4) * 128
                                    P.do("dve", "scalar_tensor_tensor",
                                         Sf[:, h_ * 128:(h_ + 1) * 128],
                                         Sf[:, h_ * 128:(h_ + 1) * 128],
                                         gct[:, h_:h_ + 1],
                                         psd[grp][:, cc0:cc0 + 128],
                                         ALU.mult, ALU.add)
                                P.do("dve", "tensor_copy", Sb2[:], Sf[:], inc=sV)
                                supd_done = sV.n
                                for ch_ in range(2):
                                    tg = stile * 4 + 2 * pl + ch_
                                    rv = ret[:, ch_ * D:(ch_ + 1) * D]
                                    P.wait("dve", sA, ret_evac[ch_][1])
                                    P.do("dve", "reduce_sum", st8[:, 0:HB],
                                         rv.rearrange("p (h e) -> p h e", h=HB),
                                         axis=mybir.AxisListType.X)
                                    P.do("dve", "tensor_tensor", sqb[:], rv, rv,
                                         ALU.mult)
                                    P.do("dve", "reduce_sum", st8[:, HB:2 * HB],
                                         sqb[:].rearrange("p (h e) -> p h e", h=HB),
                                         axis=mybir.AxisListType.X)
                                    P.wait("dve", sA)
                                    P.do("dve", "tensor_scalar_mul",
                                         st8[:, 2 * HB:3 * HB], st8[:, 0:HB],
                                         cst[:, 4:5])
                                    P.do("dve", "tensor_scalar_mul",
                                         st8[:, HB:2 * HB], st8[:, HB:2 * HB],
                                         cst[:, 4:5])
                                    P.do("dve", "tensor_tensor", st8[:, 0:HB],
                                         st8[:, 2 * HB:3 * HB],
                                         st8[:, 2 * HB:3 * HB], ALU.mult)
                                    P.do("dve", "tensor_tensor", st8[:, HB:2 * HB],
                                         st8[:, HB:2 * HB], st8[:, 0:HB],
                                         ALU.subtract, inc=sV)
                                    P.wait("act", sV)
                                    P.do("act", "activation", st8[:, 3 * HB:4 * HB],
                                         st8[:, HB:2 * HB], AF.Ln, bias=EPS6)
                                    P.do("act", "activation", st8[:, 3 * HB:4 * HB],
                                         st8[:, 3 * HB:4 * HB], AF.Exp, scale=-0.5,
                                         inc=sA)
                                    P.wait("dve", sA)
                                    for h_ in range(HB):
                                        P.do("dve", "tensor_scalar",
                                             rv[:, h_ * 128:(h_ + 1) * 128],
                                             rv[:, h_ * 128:(h_ + 1) * 128],
                                             st8[:, 2 * HB + h_:2 * HB + h_ + 1],
                                             st8[:, 3 * HB + h_:3 * HB + h_ + 1],
                                             ALU.subtract, ALU.mult, inc=sV)
                                    svn = sV.n

                                    def em(rv=rv, tg=tg, svn=svn):
                                        P.wait("pool", sV, svn)
                                        P.do("pool", "tensor_tensor", rv, rv,
                                             gnS[:], ALU.mult)
                                        P.do("pool", "tensor_tensor",
                                             ygn[:, tg * D:(tg + 1) * D], rv,
                                             gnB[:], ALU.add, inc=sG)
                                    pending_gn.append(em)
                        flush_gn()
                    P.barrier()
                L1.close()

                xbst = contextlib.ExitStack()
                xb = sb(xbst, "xb", [128, DC * S], side="right")
                P.do("sp", "dma_start", out=xb[:], in_=xb_sp, inc=dL)
                with contextlib.ExitStack() as c2:
                    wgg = sb(c2, "wgg", [128, DC * D], BF16)
                    woo = sb(c2, "woo", [128, DC * D], BF16)
                    load_bf16(wgg[:].rearrange("p (c n) -> p c n", c=DC),
                              bWg[li].rearrange("(c p) n -> p c n", p=128))
                    load_bf16(woo[:].rearrange("p (c n) -> p c n", c=DC),
                              bWo[li].rearrange("(c p) n -> p c n", p=128))
                    swt = sb(c2, "swt", [128, D], BF16)
                    ptok = sb(c2, "ptok", [128, 2 * D], BF16)
                    prT = sb(c2, "prT", [128, DC * 512], BF16)
                    P.barrier()
                    with contextlib.ExitStack() as ps9:
                        pg = [psum(ps9, "pg3%d" % i, [128, 512], F32)
                              for i in range(2)]
                        ptc = psum(ps9, "ptc2", [128, 1024], BF16)
                        pt_cons = 0
                        prT_evac = 0
                        for stile in range(4):
                            s0_ = stile * 512
                            for tl in range(4):
                                tg = stile * 4 + tl
                                t0 = s0_ + tl * 128
                                P.wait("pe", sG)
                                P.wait("pe", sA)
                                P.wait("pe", sV)
                                for c in range(DC):
                                    for hf in range(2):
                                        P.do("pe", "matmul", pg[hf][:],
                                             xnb[:, c * S + t0:c * S + t0 + 128],
                                             wgg[:, c * D + hf * 512:c * D + (hf + 1) * 512],
                                             start=(c == 0), stop=(c == DC - 1),
                                             inc=(sP if (c == DC - 1 and hf == 1) else None))
                                P.wait("act", sP)
                                for hf in range(2):
                                    P.do("act", "activation",
                                         swt[:, hf * 512:(hf + 1) * 512],
                                         pg[hf][:], AF.Silu,
                                         inc=(sA if hf == 1 else None))
                                psl = (tl % 2) * D
                                P.wait("pool", sA)
                                P.wait("pool", sP, pt_cons)
                                P.do("pool", "tensor_tensor", ptok[:, psl:psl + D],
                                     ygn[:, tg * D:(tg + 1) * D], swt[:],
                                     ALU.mult, inc=sG)
                                P.wait("pe", sG)
                                P.wait("pe", sV, prT_evac)
                                for dc2 in range(DC):
                                    P.do("pe", "transpose",
                                         ptc[:, dc2 * 128:(dc2 + 1) * 128],
                                         ptok[:, psl + dc2 * 128:psl + (dc2 + 1) * 128],
                                         idb[:],
                                         inc=(sP if dc2 == DC - 1 else None))
                                pt_cons = sP.n
                                P.wait("dve", sP)
                                P.do("dve", "tensor_copy",
                                     prT[:].rearrange("p (c n) -> p c n", c=DC)
                                     [:, :, tl * 128:tl * 128 + 128],
                                     ptc[:].rearrange("p (c n) -> p c n", c=DC),
                                     inc=sV)
                                prT_evac = sV.n
                            prdone = sV.n
                            P.wait("dve", dL)
                            for dt in range(DC):
                                bank = dt % 2
                                P.wait("pe", sV, prdone)
                                P.wait("pe", sV)
                                P.wait("pe", sA)
                                for c in range(DC):
                                    P.do("pe", "matmul", pg[bank][:],
                                         woo[:, c * D + dt * 128:c * D + dt * 128 + 128],
                                         prT[:, c * 512:(c + 1) * 512],
                                         start=(c == 0), stop=(c == DC - 1),
                                         inc=(sP if c == DC - 1 else None))
                                P.wait("dve", sP)
                                co = dt * S + s0_
                                P.do("dve", "tensor_tensor", xb[:, co:co + 512],
                                     xb[:, co:co + 512], pg[bank][:], ALU.add,
                                     inc=sV)
                    P.barrier()
                L0.close()

                with contextlib.ExitStack() as sv2:
                    w2r = sb(sv2, "w2r", [128, 32 * D], BF16)
                    load_bf16(w2r[:].rearrange("p (c n) -> p c n", c=32),
                              bw2[li].rearrange("(c p) n -> p c n", p=128))
                    ln_stats(xb, rowMU[0:1, :], rowSC[0:1, :], EPS6)
                    s2c = vec_fm(sv2, "l2s",
                                 bV["ln2_scale"][li].rearrange("(c p) -> c p", c=8))
                    b2c = vec_fm(sv2, "l2b",
                                 bV["ln2_bias"][li].rearrange("(c p) -> c p", c=8))
                    P.barrier()
                    for st2 in range(4):
                        with contextlib.ExitStack() as c3:
                            xn2 = sb(c3, "xn2", [128, DC * 512], BF16)
                            h1 = sb(c3, "h1", [128, 32 * 512], BF16)
                            w1s = [sb(c3, "w1s%d" % i, [128, DC * 128], BF16)
                                   for i in range(4)]
                            xn2_done = apply_ln(
                                xb, xn2, rowMU[0:1, :], rowSC[0:1, :],
                                gcol=s2c, bcol=b2c, s0=st2 * 512, ns=512,
                                tail_barrier=False)
                            with contextlib.ExitStack() as pc3:
                                p1 = [psum(pc3, "p31%d" % i, [128, 512], F32)
                                      for i in range(2)]
                                use_done = [0, 0, 0, 0]
                                ldd = {}

                                def w1_load(ft):
                                    sl_ = ft % 4
                                    P.wait("sp", P.dU[0], P.w1c_built)
                                    P.wait("pool", P.dU[0], P.w1c_built)
                                    P.wait("sp", sP, use_done[sl_])
                                    P.wait("pool", sP, use_done[sl_])
                                    P.do("sp", "dma_start",
                                         out=w1s[sl_][:, 0:512],
                                         in_=w1c[li][:, ft * 1024:ft * 1024 + 512],
                                         inc=P.dH[sl_])
                                    P.do("pool", "dma_start",
                                         out=w1s[sl_][:, 512:1024],
                                         in_=w1c[li][:, ft * 1024 + 512:(ft + 1) * 1024],
                                         inc=P.dU[sl_])
                                    ldd[ft] = (P.dH[sl_].n, P.dU[sl_].n)
                                w1_load(0)
                                w1_load(1)
                                w1_load(2)
                                P.wait("pe", sV, xn2_done)
                                for ft in range(32):
                                    sl_ = ft % 4
                                    if ft + 3 < 32:
                                        w1_load(ft + 3)
                                    P.wait("pe", P.dH[sl_], ldd[ft][0])
                                    P.wait("pe", P.dU[sl_], ldd[ft][1])
                                    P.wait("pe", sA)
                                    bank = ft % 2
                                    for c in range(DC):
                                        P.do("pe", "matmul", p1[bank][:],
                                             w1s[sl_][:, c * 128:(c + 1) * 128],
                                             xn2[:, c * 512:(c + 1) * 512],
                                             start=(c == 0), stop=(c == DC - 1),
                                             inc=(sP if c == DC - 1 else None))
                                    use_done[sl_] = sP.n
                                    P.wait("act", sP)
                                    P.do("act", "activation",
                                         h1[:, ft * 512:(ft + 1) * 512],
                                         p1[bank][:], AF.Gelu_apprx_tanh,
                                         inc=sA)
                            with contextlib.ExitStack() as pc4:
                                p8 = [psum(pc4, "p8%d" % i, [128, 512], F32)
                                      for i in range(8)]
                                P.wait("pe", sA)
                                P.wait("pe", sV)
                                for fc in range(32):
                                    for do in range(8):
                                        P.do("pe", "matmul", p8[do][:],
                                             w2r[:, fc * D + do * 128:fc * D + (do + 1) * 128],
                                             h1[:, fc * 512:(fc + 1) * 512],
                                             start=(fc == 0), stop=(fc == 31),
                                             inc=(sP if (fc == 31 and do == 7) else None))
                                P.wait("dve", sP)
                                for do in range(8):
                                    co = do * S + st2 * 512
                                    P.do("dve", "tensor_tensor",
                                         xb[:, co:co + 512],
                                         xb[:, co:co + 512], p8[do][:],
                                         ALU.add, inc=(sV if do == 7 else None))
                            P.barrier()
            with contextlib.ExitStack() as ce:
                t = sb(ce, "cet", [128, S])
                for c in range(DC):
                    P.wait("sp", sV)
                    P.wait("sp", dL)
                    P.do("sp", "dma_start", out=t[:],
                         in_=r_sp[:, c * S:(c + 1) * S], inc=dL)
                    P.wait("dve", dL)
                    P.do("dve", "tensor_tensor", t[:], t[:],
                         xb[:, c * S:(c + 1) * S], ALU.add, inc=sV)
                    P.wait("sp", sV)
                    P.do("sp", "dma_start", out=r_sp[:, c * S:(c + 1) * S],
                         in_=t[:], inc=dL)
                P.barrier()
            xbst.close()
            dump_dram("C")

        if "A" in stages:
            stage_A()
        else:
            with contextlib.ExitStack() as st:
                t = sb(st, "xcp", [128, DC * S])
                load_f32(t[:].rearrange("p (c s) -> p c s", c=DC),
                         xT.rearrange("(c p) s -> p c s", p=128))
                P.barrier()
                P.do("sp", "dma_start", out=r_sp, in_=t[:], inc=dL)
                P.barrier()

        rstack = contextlib.ExitStack()
        rbuf = None
        if "B" in stages or "C" in stages or "D" in stages:
            rbuf = sb(rstack, "rbB", [128, DC * S])
            P.do("sp", "dma_start", out=rbuf[:], in_=r_sp, inc=dL)
            P.barrier()
        if "B" in stages:
            stage_ffn(rbuf, "B")
            dump_sbuf("B", rbuf)
        if "C" in stages:
            stage_C(rbuf, rstack)
            rstack = contextlib.ExitStack()
            rbuf = sb(rstack, "rbD", [128, DC * S])
            P.do("sp", "dma_start", out=rbuf[:], in_=r_sp, inc=dL)
            P.barrier()
        if "D" in stages:
            stage_ffn(rbuf, "D")
        if rbuf is not None:
            P.barrier()
            P.do("sp", "dma_start", out=yT, in_=rbuf[:], inc=dL)
            P.barrier()
            rstack.close()
        else:
            with contextlib.ExitStack() as st:
                t = sb(st, "ycp", [128, S])
                for c in range(DC):
                    P.do("sp", "dma_start", out=t[:],
                         in_=r_sp[:, c * S:(c + 1) * S], inc=dL)
                    P.wait("sp", dL)
                    P.do("sp", "dma_start", out=yT[:, c * S:(c + 1) * S],
                         in_=t[:], inc=dL)
                    P.wait("sp", dL)
                P.barrier()

        with nc.Block() as block:
            P.finalize(block)
    return nc


def kernel(**inputs):
    n = 8
    consts = _host_consts()
    toep = _toep(inputs["L0_attn_a_rel_bias"])
    x = np.ascontiguousarray(inputs["L0_attn_a_x"])

    nc = bass.Bass("TRN2", target_bir_lowering=False)
    _build(nc, _CFG)

    common = {}
    for k in ["wq_lin", "bq_lin", "wk_lin", "bk_lin", "wv_lin", "bv_lin",
              "wq_loc", "bq_loc", "wk_loc", "bk_loc", "wv_loc", "bv_loc",
              "wo", "bo", "wg1", "bg1", "wg2", "bg2", "ln_g", "ln_b"]:
        common["a_" + k] = np.ascontiguousarray(
            np.asarray(inputs["L0_attn_a_" + k], np.float32))
    common["TOEP"] = toep
    common["f_Wt"] = np.ascontiguousarray(
        np.asarray(inputs["L0_ffn_W"], np.float32).T)
    common["f_b"] = np.ascontiguousarray(np.asarray(inputs["L0_ffn_b"], np.float32))
    common["f_Vr"] = np.ascontiguousarray(np.asarray(inputs["L0_ffn_V_r"], np.float32))
    common["f_Ct"] = np.ascontiguousarray(
        np.asarray(inputs["L0_ffn_C"], np.float32).T)
    for k in ["Wq", "Wk", "Wv", "Wg", "Wo", "gn_scale", "gn_bias",
              "ln1_scale", "ln1_bias", "ln2_scale", "ln2_bias"]:
        common["b_" + k] = np.ascontiguousarray(
            np.asarray(inputs["L0_attn_b_" + k], np.float32))
    common["b_w1"] = np.ascontiguousarray(
        np.asarray(inputs["L0_attn_b_ffn_w1"], np.float32))
    common["b_w2"] = np.ascontiguousarray(
        np.asarray(inputs["L0_attn_b_ffn_w2"], np.float32))
    for k, v in consts.items():
        common["C_" + k] = np.ascontiguousarray(v)

    in_maps = []
    for b in range(n):
        m = dict(common)
        m["xT"] = np.ascontiguousarray(np.asarray(x[b], np.float32).T)
        in_maps.append(m)

    res = run_bass_kernel_spmd(nc, in_maps, core_ids=list(range(n)))
    out = np.empty((n, S, D), np.float32)
    for b in range(n):
        yt = np.asarray(res.results[b]["yT"])
        out[b] = yt.reshape(128, DC, S).transpose(2, 1, 0).reshape(S, D)
    return out


# version 20
# speedup vs baseline: 1.0395x; 1.0062x over previous
import contextlib
import numpy as np

import concourse.bass as bass
import concourse.mybir as mybir
from concourse.bass_utils import run_bass_kernel_spmd

F32 = mybir.dt.float32
BF16 = mybir.dt.bfloat16
AF = mybir.ActivationFunctionType
ALU = mybir.AluOpType

S = 2048
D = 1024
DC = 8
TC = 16
HA = 16
HB = 8
FF = 4096
RK = 32
CSH = 8.0

_CFG = {"stages": ("A", "B", "C", "D"), "dump": ()}


def _hilo(a):
    a = np.asarray(a, np.float64)
    hi = a.astype(np.float32)
    lo = (a - hi.astype(np.float64)).astype(np.float32)
    return hi, lo


def _host_consts():
    import jax
    import jax.numpy as jnp
    cpu_dev = jax.devices("cpu")[0]
    with jax.default_device(cpu_dev):
        f = jnp.float32
        c = {}
        c["IDF"] = np.eye(128, dtype=np.float32)
        P = np.zeros((128, 128), np.float32)
        for i in range(64):
            P[2 * i, 2 * i + 1] = -1.0
            P[2 * i + 1, 2 * i] = 1.0
        c["ROTT"] = np.ascontiguousarray(P.T)
        d, half = 128, 64
        base = (jnp.arange(0, d, 2, dtype=f) + 0.4 * d) / (1.4 * d)
        pos = jnp.arange(S, dtype=f)
        inv_freq = 1.0 / 10000.0 ** (jnp.arange(half, dtype=f) / half)
        sinusoid = pos[:, None] * inv_freq[None, :]
        sin, cos = jnp.sin(sinusoid), jnp.cos(sinusoid)
        scale = base[None, :] ** (pos[:, None] / 512.0)
        for tag, sc in (("Q", scale), ("K", 1.0 / scale)):
            sin2 = jnp.repeat(sin * sc, 2, axis=-1)
            cos2 = jnp.repeat(cos * sc, 2, axis=-1)
            c["XC%s_h" % tag] = np.ascontiguousarray(np.asarray(cos2, np.float32).T)
            c["XC%s_l" % tag] = np.zeros((128, S), np.float32)
            c["XS%s_h" % tag] = np.ascontiguousarray(np.asarray(sin2, np.float32).T)
            c["XS%s_l" % tag] = np.zeros((128, S), np.float32)
        gamj = 1.0 - 2.0 ** (-5.0 - jnp.arange(HB, dtype=f))
        ii32 = jnp.arange(128, dtype=f)
        diff = ii32[None, :] - ii32[:, None]
        decj = np.asarray(gamj[:, None, None] ** diff[None], np.float32)
        diff2 = (ii32[None, :] + 128.0) - ii32[:, None]
        decj2 = np.asarray(gamj[:, None, None] ** diff2[None], np.float32)
        dct = np.zeros((128, HB * 128), np.float32)
        dct2 = np.zeros((128, HB * 128), np.float32)
        for h in range(HB):
            dct[:, h * 128:(h + 1) * 128] = np.where(
                np.arange(128)[None, :] >= np.arange(128)[:, None], decj[h], 0.0)
            dct2[:, h * 128:(h + 1) * 128] = decj2[h]
        c["DCT_h"] = dct
        c["DCT_l"] = np.zeros_like(dct)
        c["DCT2_h"] = dct2
        c["DCT2_l"] = np.zeros_like(dct2)
    gam = 1.0 - 2.0 ** (-5.0 - np.arange(HB, dtype=np.float64))
    ii = np.arange(128, dtype=np.float64)
    gq = np.zeros((128, HB * 128), np.float64)
    gq2 = np.zeros((128, HB * 128), np.float64)
    for h in range(HB):
        gq[:, h * 128:(h + 1) * 128] = (gam[h] ** (ii + 1.0))[None, :]
        gq2[:, h * 128:(h + 1) * 128] = (gam[h] ** (ii + 129.0))[None, :]
    c["GQ_h"], c["GQ_l"] = _hilo(gq)
    c["GQ2_h"], c["GQ2_l"] = _hilo(gq2)
    gk = np.stack([gam[h] ** (127.0 - ii) for h in range(HB)], axis=1)
    c["GK_h"], c["GK_l"] = _hilo(gk)
    gk2 = np.stack([gam[h] ** (255.0 - ii) for h in range(HB)], axis=1)
    c["GK255_h"], c["GK255_l"] = _hilo(gk2)
    gc = np.tile((gam ** 256.0)[None, :], (128, 1))
    c["GC_h"], c["GC_l"] = _hilo(gc)
    vb = np.full((128, 256), -1e9, np.float32)
    for t in range(2):
        for p in range(128):
            j = 128 * t + p
            lo, hi = max(0, j - 64), min(128, j + 1)
            if lo < hi:
                vb[p, 128 * t + lo:128 * t + hi] = -8.0 * CSH
    c["VBIG"] = vb
    cst = np.zeros((128, 8), np.float32)
    cst[:, 0] = 1e-5
    cst[:, 1] = 1e-6
    cst[:, 2] = 1.0
    cst[:, 3] = 1.0 / 1024.0
    cst[:, 4] = 1.0 / 128.0
    cst[:, 5] = 1e-6
    c["CST"] = cst
    c["ONES32"] = np.ones((1, 512), np.float32)
    c["ONESB"] = np.ones((1, 512), np.float32)
    return c


def _toep(rel_bias):
    rb = np.asarray(rel_bias, np.float32)
    out = np.zeros((128, HA * 256), np.float32)
    i = np.arange(128)
    for h in range(HA):
        for t in range(2):
            for p in range(128):
                off = (128 * t + p) - i
                valid = (off >= 0) & (off <= 64)
                out[p, 256 * h + 128 * t:256 * h + 128 * t + 128] = np.where(
                    valid, rb[np.clip(off + 32, 0, 128), h], 0.0)
    return out


class Sem:
    __slots__ = ("h", "step", "n")

    def __init__(self, handle, step):
        self.h, self.step, self.n = handle, step, 0


class Prog:
    ENGS = ("sp", "pe", "act", "dve", "pool")

    def __init__(self, nc, es):
        self.nc = nc
        self.q = {e: [] for e in self.ENGS}
        self.last_wait = {e: {} for e in self.ENGS}
        mk = lambda n, s: Sem(es.enter_context(nc.semaphore(n)), s)
        self.dL = mk("dL", 16)
        self.dC = mk("dC", 16)
        self.dU = [mk("dU%d" % i, 16) for i in range(4)]
        self.dH = [mk("dH%d" % i, 16) for i in range(4)]
        self.own = {"sp": mk("sS", 1), "pe": mk("sP", 1), "act": mk("sA", 1),
                    "dve": mk("sV", 1), "pool": mk("sG", 1)}
        self.lastc = {e: None for e in self.ENGS}
        self.uncapped = {e: False for e in self.ENGS}

    def do(self, eng, meth, *args, inc=None, **kw):
        if eng in ("act", "dve", "pool") and meth != "dma_start":
            self.wait(eng, self.own[eng])
            if inc is None:
                inc = self.own[eng]
        if inc is not None:
            inc.n += inc.step
        cell = [inc]
        def thunk(h, meth=meth, args=args, kw=kw, cell=cell):
            ins = getattr(h, meth)(*args, **kw)
            if cell[0] is not None:
                ins.then_inc(cell[0].h, cell[0].step)
        self.q[eng].append(thunk)
        if meth != "dma_start":
            if meth == "matmul" and kw.get("stop") is False:
                self.lastc[eng] = None
                self.uncapped[eng] = True
            else:
                self.lastc[eng] = cell
                self.uncapped[eng] = inc is None
        return inc.n if inc is not None else None

    def inc(self, eng, sem):
        sem.n += 1
        c = self.lastc[eng]
        if c is not None and c[0] is None:
            c[0] = sem
        else:
            def thunk(h, sem=sem):
                h.nop().then_inc(sem.h, 1)
            self.q[eng].append(thunk)
            self.lastc[eng] = None
        self.uncapped[eng] = False
        return sem.n

    def wait(self, eng, sem, val=None):
        if val is None:
            val = sem.n
        if val <= 0 or self.last_wait[eng].get(sem, 0) >= val:
            return
        self.last_wait[eng][sem] = val
        def thunk(h, sem=sem, val=val):
            h.wait_ge(sem.h, val)
        self.q[eng].append(thunk)

    def barrier(self, dma=True):
        for e in ("pe", "act", "dve", "pool"):
            if self.uncapped[e]:
                self.inc(e, self.own[e])
        for e in self.ENGS:
            for o in ("pe", "act", "dve", "pool"):
                self.wait(e, self.own[o])
            if dma:
                self.wait(e, self.dL)
                self.wait(e, self.dC)

    def finalize(self, block):
        block.sync(lambda h: [f(h) for f in self.q["sp"]])
        block.tensor(lambda h: [f(h) for f in self.q["pe"]])
        block.scalar(lambda h: [f(h) for f in self.q["act"]])
        block.vector(lambda h: [f(h) for f in self.q["dve"]])
        block.gpsimd(lambda h: [f(h) for f in self.q["pool"]])


def _build(nc, cfg):
    stages = list(cfg["stages"])
    dump = set(cfg.get("dump", ()))

    def din(name, shape):
        return nc.dram_tensor(name, list(shape), F32, kind="ExternalInput").ap()

    xT = din("xT", [D, S])
    aW = {k: din("a_" + k, [3 * D, D] if k == "wg1" else [D, D]) for k in
          ["wq_lin", "wk_lin", "wv_lin", "wq_loc", "wk_loc", "wv_loc",
           "wo", "wg1", "wg2"]}
    aB = {k: din("a_" + k, [D]) for k in
          ["bq_lin", "bk_lin", "bv_lin", "bq_loc", "bk_loc", "bv_loc",
           "bo", "bg1", "bg2", "ln_g", "ln_b"]}
    TOEP = din("TOEP", [128, HA * 256])
    fWt = din("f_Wt", [D, D])
    fB = din("f_b", [D])
    fVr = din("f_Vr", [D, RK])
    fCt = din("f_Ct", [RK, D])
    bWq = din("b_Wq", [2, HB, D, 128])
    bWk = din("b_Wk", [2, HB, D, 128])
    bWv = din("b_Wv", [2, HB, D, 128])
    bWg = din("b_Wg", [2, D, D])
    bWo = din("b_Wo", [2, D, D])
    bw1 = din("b_w1", [2, D, FF])
    bw2 = din("b_w2", [2, FF, D])
    bV = {k: din("b_" + k, [2, D]) for k in
          ["gn_scale", "gn_bias", "ln1_scale", "ln1_bias",
           "ln2_scale", "ln2_bias"]}
    CN = {k: din("C_" + k, v.shape) for k, v in _host_consts().items()}
    yT = nc.dram_tensor("yT", [128, DC * S], F32, kind="ExternalOutput").ap()
    r_sp = nc.dram_tensor("r_spill", [128, DC * S], F32, kind="Internal").ap()
    xb_sp = nc.dram_tensor("xb_spill", [128, DC * S], F32, kind="Internal").ap()
    lin_sp = nc.dram_tensor("lin_spill", [128, DC * S], BF16, kind="Internal").ap()
    tab_sp = nc.dram_tensor("tab_cache", [128, 4 * S], BF16, kind="Internal").ap()
    w1c = nc.dram_tensor("w1_cache", [2, 128, 32 * 1024], BF16, kind="Internal").ap()
    dbg_t = {t: nc.dram_tensor("dbg_" + t, [128, DC * S], F32,
                               kind="ExternalOutput").ap() for t in dump}

    es = contextlib.ExitStack()
    with es:
        P = Prog(nc, es)
        dL, dC = P.dL, P.dC
        sP, sA, sV, sG = P.own["pe"], P.own["act"], P.own["dve"], P.own["pool"]

        _uid = [0]

        def _un(name):
            _uid[0] += 1
            return "%s_%d" % (name, _uid[0])

        def sb(stack, name, shape, dt=F32, side=None):
            return stack.enter_context(
                nc.sbuf_tensor(_un(name), shape, dt, side=side))

        def psum(stack, name, shape, dt=F32):
            return stack.enter_context(nc.psum_tensor(_un(name), shape, dt))

        rowMU = sb(es, "rowMU", [1, S])
        rowSC = sb(es, "rowSC", [1, S])
        cst = sb(es, "cst", [128, 8])
        ones32 = sb(es, "ones32", [1, 512])
        onesb = sb(es, "onesb", [1, 512], BF16)
        kcol32 = sb(es, "kcol32", [128, 1])
        kcolb = sb(es, "kcolb", [128, 1], BF16)
        idb = sb(es, "idb", [128, 128], BF16)
        idf8 = sb(es, "idf8", [8, 8])

        P.do("sp", "dma_start", out=cst[:], in_=CN["CST"], inc=dL)
        P.do("sp", "dma_start", out=ones32[:], in_=CN["ONES32"], inc=dL)
        P.do("sp", "dma_start", out=idf8[:], in_=CN["IDF"][0:8, 0:8], inc=dL)
        P.do("pool", "dma_start", out=onesb[:], in_=CN["ONESB"], inc=dC)
        P.do("pool", "dma_start", out=idb[:], in_=CN["IDF"], inc=dC)
        P.barrier()
        P.do("dve", "memset", kcol32[:], 1.0 / 1024.0)
        P.do("dve", "memset", kcolb[:], 1.0 / 1024.0, inc=sV)
        P.barrier()

        EPS5 = cst[:, 0:1]
        EPS6 = cst[:, 1:2]



        def load_bf16(dst_ap, src_ap):
            return P.do("pool", "dma_start", out=dst_ap, in_=src_ap, inc=dC)

        def load_f32(dst_ap, src_ap):
            return P.do("sp", "dma_start", out=dst_ap, in_=src_ap, inc=dL)

        def w_fm(dst, src2d):
            nch = src2d.shape[0] // 128
            return load_bf16(dst[:].rearrange("p (c n) -> p c n", c=nch),
                             src2d.rearrange("(c p) n -> p c n", p=128))

        def row_bf16(dst, vec):
            return load_bf16(dst, vec.rearrange("(o d) -> o d", o=1))

        def ln_stats(src_fm, mu_row, sc_row, eps_col, dbl=False, sc2_row=None):
            P.barrier(dma=False)
            with contextlib.ExitStack() as st:
                pr = [psum(st, "lnp%d" % i, [1, 512], F32) for i in range(4)]
                sq = sb(st, "lnsq", [128, DC * S], BF16)
                msq = sb(st, "lnmsq", [1, S])
                t1 = sb(st, "lnt1", [1, S])
                uu = sb(st, "lnu", [1, S])
                cast_done = {}
                for c in range(DC):
                    if c % 2 == 0:
                        P.do("dve", "tensor_copy", sq[:, c * S:(c + 1) * S],
                             src_fm[:, c * S:(c + 1) * S], inc=sV)
                        cast_done[c] = (sV, sV.n)
                    else:
                        P.do("act", "copy", sq[:, c * S:(c + 1) * S],
                             src_fm[:, c * S:(c + 1) * S], inc=sA)
                        cast_done[c] = (sA, sA.n)
                for c in range(DC):
                    P.wait("pe", cast_done[c][0], cast_done[c][1])
                    for sl in range(4):
                        P.do("pe", "matmul", pr[sl][0:1, :], kcolb[:],
                             sq[:, c * S + 512 * sl:c * S + 512 * (sl + 1)],
                             start=(c == 0), stop=(c == DC - 1),
                             inc=(sP if (c == DC - 1 and sl == 3) else None))
                P.wait("act", sP)
                for sl in range(4):
                    P.do("act", "copy", mu_row[:, 512 * sl:512 * (sl + 1)],
                         pr[sl][0:1, :], inc=(sA if sl == 3 else None))
                mu_done = sA.n
                P.wait("dve", sP)
                sq_done = {}
                for c in range(DC):
                    P.do("dve", "tensor_tensor", sq[:, c * S:(c + 1) * S],
                         sq[:, c * S:(c + 1) * S], sq[:, c * S:(c + 1) * S],
                         ALU.mult, inc=sV)
                    sq_done[c] = sV.n
                    if c == 1:
                        P.wait("dve", sA, mu_done)
                        P.do("dve", "tensor_tensor", t1[0:1, :], mu_row,
                             mu_row, ALU.mult, inc=sV)
                P.wait("pe", sA, mu_done)
                for c in range(DC):
                    P.wait("pe", sV, sq_done[c])
                    for sl in range(4):
                        P.do("pe", "matmul", pr[sl][0:1, :], kcolb[:],
                             sq[:, c * S + 512 * sl:c * S + 512 * (sl + 1)],
                             start=(c == 0), stop=(c == DC - 1),
                             inc=(sP if (c == DC - 1 and sl == 3) else None))
                P.wait("act", sP)
                for sl in range(4):
                    P.do("act", "copy", msq[0:1, 512 * sl:512 * (sl + 1)],
                         pr[sl][0:1, :], inc=(sA if sl == 3 else None))
                P.wait("dve", sA)
                P.do("dve", "tensor_tensor", t1[0:1, :], msq[0:1, :], t1[0:1, :],
                     ALU.subtract, inc=sV)
                P.wait("act", sV)
                P.do("act", "activation", uu[0:1, :], t1[0:1, :], AF.Ln,
                     bias=eps_col[0:1, :])
                P.do("act", "activation", sc_row, uu[0:1, :],
                     AF.Exp, scale=-0.5, inc=sA)
                if dbl:
                    P.wait("dve", sA)
                    P.do("dve", "tensor_tensor", uu[0:1, :], sc_row, sc_row,
                         ALU.mult)
                    P.do("dve", "tensor_tensor", uu[0:1, :], t1[0:1, :],
                         uu[0:1, :], ALU.mult, inc=sV)
                    P.wait("act", sV)
                    P.do("act", "activation", uu[0:1, :], uu[0:1, :], AF.Ln,
                         bias=eps_col[0:1, :])
                    P.do("act", "activation", uu[0:1, :], uu[0:1, :], AF.Exp,
                         scale=-0.5, inc=sA)
                    P.wait("dve", sA)
                    P.do("dve", "tensor_tensor", sc2_row, sc_row, uu[0:1, :],
                         ALU.mult, inc=sV)
            P.barrier(dma=False)

        def bcast(src_row, width, dst32=None, dstb=None, plus1=False):
            P.barrier(dma=False)
            with contextlib.ExitStack() as st:
                pb = [psum(st, "bcp%d" % i, [128, 512], F32) for i in range(2)]
                rb16 = sb(st, "bcr", [1, width], BF16)
                P.do("act", "copy", rb16[0:1, :], src_row[:, 0:width], inc=sA)
                P.wait("pe", sA)
                nsl = (width + 511) // 512
                edone = [0, 0]
                for sl in range(nsl):
                    w = min(512, width - 512 * sl)
                    bank = sl % 2
                    P.wait("pe", sA, edone[bank])
                    P.do("pe", "matmul", pb[bank][:, 0:w], onesb[0:1, 0:128],
                         rb16[0:1, 512 * sl:512 * sl + w], start=True,
                         stop=not plus1)
                    if plus1:
                        P.do("pe", "matmul", pb[bank][:, 0:w], onesb[0:1, 0:128],
                             onesb[0:1, 0:w], start=False, stop=True)
                    P.inc("pe", sP)
                    P.wait("act", sP)
                    if dst32 is not None:
                        P.do("act", "copy", dst32[:, 512 * sl:512 * sl + w],
                             pb[bank][:, 0:w],
                             inc=(None if dstb is not None else sA))
                    if dstb is not None:
                        P.do("act", "copy", dstb[:, 512 * sl:512 * sl + w],
                             pb[bank][:, 0:w], inc=sA)
                    edone[bank] = sA.n
            P.barrier(dma=False)

        def bcast2(mu_src, sc_src, ns2, bcA, bcBb, bcB=None):
            P.barrier(dma=False)
            with contextlib.ExitStack() as st:
                pb = [psum(st, "bcp%d" % i, [128, 512], F32) for i in range(2)]
                rb16 = sb(st, "bcr2", [1, 2 * ns2], BF16)
                P.do("act", "copy", rb16[0:1, 0:ns2], mu_src, inc=sA)
                mu_cast = sA.n
                P.do("dve", "tensor_copy", rb16[0:1, ns2:2 * ns2], sc_src,
                     inc=sV)
                sc_cast = sV.n
                nsl = (ns2 + 511) // 512
                edA, edV = 0, 0
                for sl in range(nsl):
                    w = min(512, ns2 - 512 * sl)
                    P.wait("pe", sA, mu_cast)
                    P.wait("pe", sA, edA)
                    P.do("pe", "matmul", pb[0][:, 0:w], onesb[0:1, 0:128],
                         rb16[0:1, 512 * sl:512 * sl + w],
                         start=True, stop=True, inc=sP)
                    P.wait("act", sP)
                    P.do("act", "copy", bcA[:, 512 * sl:512 * sl + w],
                         pb[0][:, 0:w], inc=sA)
                    edA = sA.n
                    P.wait("pe", sV, sc_cast)
                    P.wait("pe", sV, edV)
                    if bcB is not None:
                        P.wait("pe", sA, edA)
                    P.do("pe", "matmul", pb[1][:, 0:w], onesb[0:1, 0:128],
                         rb16[0:1, ns2 + 512 * sl:ns2 + 512 * sl + w],
                         start=True, stop=True, inc=sP)
                    P.wait("dve", sP)
                    P.do("dve", "tensor_copy",
                         bcBb[:, 512 * sl:512 * sl + w], pb[1][:, 0:w],
                         inc=sV)
                    edV = sV.n
                    if bcB is not None:
                        P.wait("act", sP)
                        P.do("act", "copy", bcB[:, 512 * sl:512 * sl + w],
                             pb[1][:, 0:w], inc=sA)
                        edA = sA.n
            P.barrier(dma=False)

        def vec_fm(stack, name, src8x128):
            t8 = sb(stack, name + "_r", [8, 128])
            out = sb(stack, name, [128, 8])
            load_f32(t8[:], src8x128)
            P.barrier(dma=False)
            P.wait("pe", P.dL)
            with contextlib.ExitStack() as st:
                pp = psum(st, name + "_p", [128, 8], F32)
                P.do("pe", "transpose", pp[:], t8[:], idf8[:], inc=sP)
                P.wait("dve", sP)
                P.do("dve", "tensor_copy", out[:], pp[:], inc=sV)
            P.barrier(dma=False)
            return out

        def apply_ln(src_fm, dst, mu_row, sc_row, gcol=None, bcol=None,
                     s0=0, ns=S, dst_f32=False, src_dram=None):
            with contextlib.ExitStack() as st:
                bcA = sb(st, "bcAl", [128, ns])
                bcB = sb(st, "bcBl", [128, ns]) if dst_f32 else None
                bcBb = sb(st, "bcBbl", [128, ns], BF16)
                if dst_f32:
                    bcast(mu_row[:, s0:s0 + ns], ns, dst32=bcA)
                    bcast(sc_row[:, s0:s0 + ns], ns, dstb=bcBb, dst32=bcB)
                else:
                    bcast2(mu_row[:, s0:s0 + ns], sc_row[:, s0:s0 + ns], ns,
                           bcA, bcBb)
                t = sb(st, "apl_t", [128, ns], F32 if dst_f32 else BF16)
                tmp = sb(st, "apl_s", [128, ns]) if src_dram is not None else None
                for c in range(DC):
                    if src_dram is not None:
                        P.wait("sp", sG)
                        P.wait("sp", sV)
                        P.do("sp", "dma_start", out=tmp[:],
                             in_=src_dram[:, c * S + s0:c * S + s0 + ns], inc=dL)
                        P.wait("dve", dL)
                        srcap = tmp[:]
                    else:
                        srcap = src_fm[:, c * S + s0:c * S + s0 + ns]
                    P.wait("dve", sG)
                    P.do("dve", "tensor_tensor", t[:], srcap,
                         bcA[:], ALU.subtract, inc=sV)
                    P.wait("pool", sV)
                    dd = dst[:, c * ns:(c + 1) * ns]
                    if dst_f32:
                        P.do("pool", "tensor_tensor", dd, t[:],
                             bcB[:], ALU.mult, inc=sG)
                    elif gcol is None:
                        P.do("pool", "tensor_tensor", dd, t[:],
                             bcBb[:], ALU.mult, inc=sG)
                    else:
                        P.do("pool", "tensor_tensor", t[:], t[:],
                             bcBb[:], ALU.mult, inc=sG)
                        P.wait("dve", sG)
                        P.do("dve", "tensor_scalar", dd, t[:],
                             gcol[:, c:c + 1], bcol[:, c:c + 1],
                             ALU.mult, ALU.add, inc=sV)
            P.barrier(dma=False)

        def dump_sbuf(tag, src_fm):
            if tag in dump:
                P.barrier()
                P.do("sp", "dma_start", out=dbg_t[tag], in_=src_fm[:], inc=dL)
                P.barrier()

        def dump_dram(tag):
            if tag in dump:
                P.barrier()
                with contextlib.ExitStack() as st:
                    t = sb(st, "dmp", [128, S])
                    for c in range(DC):
                        P.do("sp", "dma_start", out=t[:],
                             in_=r_sp[:, c * S:(c + 1) * S], inc=dL)
                        P.wait("sp", dL)
                        P.do("sp", "dma_start",
                             out=dbg_t[tag][:, c * S:(c + 1) * S], in_=t[:],
                             inc=dL)
                        P.wait("sp", dL)
                P.barrier()

        def stage_A():
            with contextlib.ExitStack() as sa:
                h = sb(sa, "h", [128, DC * S], BF16)
                with contextlib.ExitStack() as s0:
                    rb = sb(s0, "rbA", [128, DC * S])
                    load_f32(rb[:].rearrange("p (c s) -> p c s", c=DC),
                             xT.rearrange("(c p) s -> p c s", p=128))
                    P.barrier()
                    rowCS = sb(s0, "rowCS", [1, S])
                    ln_stats(rb, rowMU[0:1, :], rowSC[0:1, :], EPS5, dbl=True, sc2_row=rowCS[0:1, :])
                    with contextlib.ExitStack() as gbs:
                        g_fm = vec_fm(gbs, "a_gfm",
                                      aB["ln_g"].rearrange("(c p) -> c p", c=8))
                        b_fm = vec_fm(gbs, "a_bfm",
                                      aB["ln_b"].rearrange("(c p) -> c p", c=8))
                        apply_ln(rb, h, rowMU[0:1, :], rowCS[0:1, :], gcol=g_fm, bcol=b_fm)
                    P.do("sp", "dma_start", out=r_sp, in_=rb[:], inc=dL)
                    P.barrier()

                with contextlib.ExitStack() as s12:
                    kvb = sb(s12, "kvb", [128, DC * 1025], BF16)
                    with contextlib.ExitStack() as s1:
                        kvacc = sb(s1, "kvacc", [128, DC * 1025])
                        wk = sb(s1, "wk", [128, DC * D], BF16)
                        wv = sb(s1, "wv", [128, DC * D], BF16)
                        brk = sb(s1, "brk", [1, D], BF16)
                        brv = sb(s1, "brv", [1, D], BF16)
                        w_fm(wk, aW["wk_lin"])
                        w_fm(wv, aW["wv_lin"])
                        row_bf16(brk[:], aB["bk_lin"])
                        row_bf16(brv[:], aB["bv_lin"])
                        ktok = sb(s1, "ktok", [128, 4 * D], BF16)
                        vtok = sb(s1, "vtok", [128, 4 * 1025], BF16)
                        scrf = sb(s1, "scrfA", [128, 512])
                        P.do("dve", "memset", kvacc[:], 0.0)
                        P.do("dve", "memset",
                             vtok[:].rearrange("p (t c) -> p t c", c=1025)
                             [:, :, 1024:1025], 1.0, inc=sV)
                        P.barrier()
                        with contextlib.ExitStack() as ps1:
                            pk = [psum(ps1, "pk%d" % i, [128, 512], F32)
                                  for i in range(4)]
                            pv2 = [psum(ps1, "pv%d" % i, [128, 512], F32)
                                   for i in range(2)]
                            pks = psum(ps1, "pks", [128, 512], F32)
                            kfree = {0: 0, 1: 0, 2: 0, 3: 0}
                            accdone = 0
                            for stile in range(4):
                                for tl in range(4):
                                    t0 = stile * 512 + tl * 128
                                    for proj in range(2):
                                        wt, brow = (wk, brk) if proj == 0 else (wv, brv)
                                        for hf in range(2):
                                            bank = proj * 2 + hf
                                            P.wait("pe", sV, kfree[bank])
                                            for c in range(DC):
                                                P.do("pe", "matmul", pk[bank][:],
                                                     h[:, c * S + t0:c * S + t0 + 128],
                                                     wt[:, c * D + hf * 512:c * D + (hf + 1) * 512],
                                                     start=(c == 0), stop=False)
                                            P.do("pe", "matmul", pk[bank][:],
                                                 onesb[0:1, 0:128],
                                                 brow[0:1, hf * 512:(hf + 1) * 512],
                                                 start=False, stop=True, inc=sP)
                                            P.wait("dve", sP)
                                            if proj == 0:
                                                dst = ktok[:, tl * D + hf * 512:tl * D + (hf + 1) * 512]
                                                P.do("dve", "tensor_scalar_min",
                                                     scrf[:], pk[bank][:], 0.0,
                                                     inc=sV)
                                                P.wait("act", sV)
                                                P.do("act", "activation", scrf[:],
                                                     scrf[:], AF.Exp, inc=sA)
                                                P.wait("dve", sA)
                                                P.do("dve", "scalar_tensor_tensor",
                                                     dst, pk[bank][:], 0.0, scrf[:],
                                                     ALU.max, ALU.add, inc=sV)
                                            else:
                                                dst = vtok[:, tl * 1025 + hf * 512:tl * 1025 + (hf + 1) * 512]
                                                P.do("dve", "tensor_copy", dst,
                                                     pk[bank][:], inc=sV)
                                            kfree[bank] = sV.n
                                ktv_ready = sV.n
                                for dt in range(DC):
                                    P.wait("pe", sV, ktv_ready)
                                    P.wait("pe", sV, accdone)
                                    for tl in range(4):
                                        st_, sp_ = (tl == 0), (tl == 3)
                                        for hf in range(2):
                                            P.do("pe", "matmul", pv2[hf][:],
                                                 ktok[:, tl * D + dt * 128:tl * D + dt * 128 + 128],
                                                 vtok[:, tl * 1025 + hf * 512:tl * 1025 + (hf + 1) * 512],
                                                 start=st_, stop=sp_)
                                        P.do("pe", "matmul", pks[:, dt:dt + 1],
                                             ktok[:, tl * D + dt * 128:tl * D + dt * 128 + 128],
                                             vtok[:, tl * 1025 + 1024:tl * 1025 + 1025],
                                             start=st_, stop=sp_,
                                             inc=(sP if sp_ else None))
                                    P.wait("dve", sP)
                                    for hf in range(2):
                                        P.do("dve", "tensor_tensor",
                                             kvacc[:, dt * 1025 + hf * 512:dt * 1025 + (hf + 1) * 512],
                                             kvacc[:, dt * 1025 + hf * 512:dt * 1025 + (hf + 1) * 512],
                                             pv2[hf][:], ALU.add)
                                    P.do("dve", "tensor_tensor",
                                         kvacc[:, dt * 1025 + 1024:dt * 1025 + 1025],
                                         kvacc[:, dt * 1025 + 1024:dt * 1025 + 1025],
                                         pks[:, dt:dt + 1], ALU.add, inc=sV)
                                    accdone = sV.n
                        P.barrier()
                        for dt in range(DC):
                            P.do("act", "copy", kvb[:, dt * 1025:(dt + 1) * 1025],
                                 kvacc[:, dt * 1025:(dt + 1) * 1025],
                                 inc=(sA if dt == DC - 1 else None))
                        P.barrier()

                    with contextlib.ExitStack() as s2:
                        linT = sb(s2, "linT", [128, DC * S], BF16)
                        wq = sb(s2, "wq", [128, DC * D], BF16)
                        brq = sb(s2, "brq", [1, D], BF16)
                        w_fm(wq, aW["wq_lin"])
                        row_bf16(brq[:], aB["bq_lin"])
                        qp = sb(s2, "qp", [128, DC * 512], BF16)
                        ltok = sb(s2, "ltok", [128, D], BF16)
                        scrf = sb(s2, "scrfB", [128, 512])
                        rc = sb(s2, "rc", [128, 1])
                        P.barrier()
                        with contextlib.ExitStack() as ps3:
                            pq = [psum(ps3, "pq%d" % i, [128, 512], F32)
                                  for i in range(2)]
                            pn = [psum(ps3, "pn%d" % i, [128, 512], F32)
                                  for i in range(2)]
                            pd = psum(ps3, "pd", [128, 512], F32)
                            pt = psum(ps3, "ptA", [128, 1024], BF16)
                            qfree = [0, 0]
                            nfree = 0
                            tfree = 0
                            for stile in range(4):
                                s0_ = stile * 512
                                for dco in range(DC):
                                    bank = dco % 2
                                    P.wait("pe", sV, qfree[bank])
                                    for c in range(DC):
                                        P.do("pe", "matmul", pq[bank][:],
                                             wq[:, c * D + dco * 128:c * D + dco * 128 + 128],
                                             h[:, c * S + s0_:c * S + s0_ + 512],
                                             start=(c == 0), stop=False)
                                    P.do("pe", "matmul", pq[bank][:],
                                         brq[0:1, dco * 128:dco * 128 + 128],
                                         onesb[0:1, 0:512], start=False,
                                         stop=True, inc=sP)
                                    P.wait("dve", sP)
                                    P.do("dve", "tensor_scalar_min", scrf[:],
                                         pq[bank][:], 0.0, inc=sV)
                                    P.wait("act", sV)
                                    P.do("act", "activation", scrf[:], scrf[:],
                                         AF.Exp, inc=sA)
                                    P.wait("dve", sA)
                                    P.do("dve", "scalar_tensor_tensor",
                                         qp[:, dco * 512:(dco + 1) * 512],
                                         pq[bank][:], 0.0, scrf[:], ALU.max,
                                         ALU.add, inc=sV)
                                    qfree[bank] = sV.n
                                for tl in range(4):
                                    tg = stile * 4 + tl
                                    P.wait("pe", sV)
                                    P.wait("pe", sV, nfree)
                                    for c in range(DC):
                                        st_, sp_ = (c == 0), (c == DC - 1)
                                        for hf in range(2):
                                            P.do("pe", "matmul", pn[hf][:],
                                                 qp[:, c * 512 + tl * 128:c * 512 + tl * 128 + 128],
                                                 kvb[:, c * 1025 + hf * 512:c * 1025 + (hf + 1) * 512],
                                                 start=st_, stop=sp_)
                                        P.do("pe", "matmul", pd[:, tl:tl + 1],
                                             qp[:, c * 512 + tl * 128:c * 512 + tl * 128 + 128],
                                             kvb[:, c * 1025 + 1024:c * 1025 + 1025],
                                             start=st_, stop=sp_,
                                             inc=(sP if sp_ else None))
                                    P.wait("dve", sP)
                                    P.do("dve", "tensor_scalar_add", rc[:],
                                         pd[:, tl:tl + 1], cst[:, 5:6])
                                    P.do("dve", "reciprocal", rc[:], rc[:])
                                    for hf in range(2):
                                        P.do("dve", "tensor_scalar_mul",
                                             ltok[:, hf * 512:(hf + 1) * 512],
                                             pn[hf][:], rc[:],
                                             inc=(sV if hf == 1 else None))
                                    nfree = sV.n
                                    P.wait("pe", sV)
                                    P.wait("pe", sV, tfree)
                                    for dc2 in range(DC):
                                        P.do("pe", "transpose",
                                             pt[:, dc2 * 128:(dc2 + 1) * 128],
                                             ltok[:, dc2 * 128:(dc2 + 1) * 128],
                                             idb[:],
                                             inc=(sP if dc2 == DC - 1 else None))
                                    P.wait("dve", sP)
                                    P.do("dve", "tensor_copy",
                                         linT[:].rearrange("p (c n) -> p c n", c=DC)
                                         [:, :, tg * 128:tg * 128 + 128],
                                         pt[:].rearrange("p (c n) -> p c n", c=DC),
                                         inc=sV)
                                    tfree = sV.n
                        P.barrier()
                        P.do("sp", "dma_start", out=lin_sp, in_=linT[:], inc=dL)
                        P.barrier()

                winT = sb(sa, "winT", [128, DC * S], BF16)
                with contextlib.ExitStack() as s3:
                    qloc = sb(s3, "qloc", [128, DC * S], BF16)
                    klocT = sb(s3, "klocT", [128, DC * 2112], BF16)
                    vloc = sb(s3, "vloc", [128, 17 * 1040], BF16)
                    vstage = sb(s3, "vstage", [96, 1040], BF16)
                    with contextlib.ExitStack() as ps4:
                        pj = [psum(ps4, "pj%d" % i, [128, 512], F32)
                              for i in range(4)]
                        for wname, bname, kind in (("wq_loc", "bq_loc", "q"),
                                                   ("wk_loc", "bk_loc", "k"),
                                                   ("wv_loc", "bv_loc", "v")):
                            with contextlib.ExitStack() as sw:
                                wt = sb(sw, "wloc", [128, DC * D], BF16)
                                brow = sb(sw, "bloc", [1, D], BF16)
                                w_fm(wt, aW[wname])
                                row_bf16(brow[:], aB[bname])
                                if kind == "k":
                                    P.do("dve", "memset",
                                         klocT[:].rearrange("p (c t) -> p c t", t=2112)
                                         [:, :, 0:32], 0.0)
                                    P.do("dve", "memset",
                                         klocT[:].rearrange("p (c t) -> p c t", t=2112)
                                         [:, :, 2080:2112], 0.0, inc=sV)
                                if kind == "v":
                                    P.do("dve", "memset",
                                         vstage[:].rearrange(
                                             "p (h c) -> p h c", c=65)[:, :, 64:65],
                                         1.0)
                                    P.do("dve", "memset", vloc[0:32, 0:1040], 0.0)
                                    P.do("dve", "memset",
                                         vloc[32:64, 16 * 1040:17 * 1040], 0.0)
                                    P.do("dve", "memset",
                                         vloc[64:128, 16 * 1040:17 * 1040], 0.0)
                                    P.do("dve", "memset",
                                         vloc[:].rearrange("p (t h c) -> p t h c",
                                                           h=16, c=65)[:, :, :, 64:65],
                                         1.0, inc=sV)
                                P.barrier()
                                if kind in ("q", "k"):
                                    pjdone = [0, 0, 0, 0]
                                    for stile in range(4):
                                        s0_ = stile * 512
                                        for dco in range(DC):
                                            bank = dco % 4
                                            P.wait("pe", sA, pjdone[bank])
                                            for c in range(DC):
                                                P.do("pe", "matmul", pj[bank][:],
                                                     wt[:, c * D + dco * 128:c * D + dco * 128 + 128],
                                                     h[:, c * S + s0_:c * S + s0_ + 512],
                                                     start=(c == 0), stop=False)
                                            P.do("pe", "matmul", pj[bank][:],
                                                 brow[0:1, dco * 128:dco * 128 + 128],
                                                 onesb[0:1, 0:512], start=False,
                                                 stop=True, inc=sP)
                                            P.wait("act", sP)
                                            if kind == "q":
                                                dd = qloc[:, dco * S + s0_:dco * S + s0_ + 512]
                                            else:
                                                dd = klocT[:, dco * 2112 + 32 + s0_:
                                                           dco * 2112 + 32 + s0_ + 512]
                                            P.do("act", "copy", dd, pj[bank][:],
                                                 inc=sA)
                                            pjdone[bank] = sA.n
                                else:
                                    for tg in range(17):
                                        if tg == 0:
                                            m0, mm_, pofs = 0, 96, 32
                                        elif tg == 16:
                                            m0, mm_, pofs = 2016, 32, 0
                                        else:
                                            m0, mm_, pofs = tg * 128 - 32, 128, 0
                                        P.wait("pe", sA)
                                        for c in range(DC):
                                            st_ = (c == 0)
                                            for hf in range(2):
                                                P.do("pe", "matmul",
                                                     pj[hf][0:mm_, :],
                                                     h[:, c * S + m0:c * S + m0 + mm_],
                                                     wt[:, c * D + hf * 512:c * D + (hf + 1) * 512],
                                                     start=st_, stop=False)
                                        for hf in range(2):
                                            P.do("pe", "matmul", pj[hf][0:mm_, :],
                                                 onesb[0:1, 0:mm_],
                                                 brow[0:1, hf * 512:(hf + 1) * 512],
                                                 start=False, stop=True,
                                                 inc=(sP if hf == 1 else None))
                                        P.wait("act", sP)
                                        for hf in range(2):
                                            srcp = pj[hf][0:mm_, :].rearrange(
                                                "p (h c) -> p h c", c=64)
                                            if pofs == 0:
                                                dsta = vloc[0:mm_,
                                                            tg * 1040:(tg + 1) * 1040].rearrange(
                                                    "p (h c) -> p h c", c=65)[:, 8 * hf:8 * (hf + 1), 0:64]
                                                P.do("act", "copy", dsta, srcp,
                                                     inc=sA)
                                            else:
                                                dsta = vstage[0:mm_, :].rearrange(
                                                    "p (h c) -> p h c", c=65)[:, 8 * hf:8 * (hf + 1), 0:64]
                                                P.do("act", "copy", dsta, srcp,
                                                     inc=sA)
                                        if pofs != 0:
                                            P.wait("sp", sA)
                                            P.do("sp", "dma_start",
                                                 out=vloc[pofs:pofs + mm_,
                                                          tg * 1040:(tg + 1) * 1040],
                                                 in_=vstage[0:mm_, :], inc=dL)
                                P.barrier()
                    mask8 = sb(s3, "mask8", [128, HA * 256])
                    with contextlib.ExitStack() as sm:
                        vbt = sb(sm, "vbt", [128, 256])
                        load_f32(mask8[:], TOEP)
                        load_f32(vbt[:], CN["VBIG"])
                        P.barrier()
                        for hh in range(HA):
                            P.do("dve", "scalar_tensor_tensor",
                                 mask8[:, hh * 256:(hh + 1) * 256],
                                 mask8[:, hh * 256:(hh + 1) * 256], 8.0, vbt[:],
                                 ALU.mult, ALU.add, inc=sV)
                        P.barrier()
                    scm = sb(s3, "scm", [128, 4 * 256])
                    pexp = sb(s3, "pexp", [128, 4 * 256], BF16)
                    win2 = sb(s3, "win2", [128, 256], BF16)
                    rc2 = sb(s3, "rc2", [128, 4])
                    P.do("dve", "memset", scm[:], -1e9, inc=sV)
                    P.barrier()
                    with contextlib.ExitStack() as ps5:
                        psc = [psum(ps5, "psc%d" % i, [128, 256], F32)
                               for i in range(4)]
                        pw = [psum(ps5, "pw%d" % i, [128, 512], F32)
                              for i in range(3)]
                        ptw = psum(ps5, "ptw", [128, 256], BF16)
                        for i in range(4):
                            P.do("dve", "memset", psc[i][64:128, 128:256], 0.0,
                                 inc=sV)
                        P.barrier(dma=False)
                        scfree = [0, 0, 0, 0]
                        scm_free = [0, 0, 0, 0]
                        pexp_done = [0, 0, 0, 0]
                        wfree = {0: 0, 1: 0, 2: 0}
                        wdone = {0: 0, 1: 0, 2: 0}
                        twfree = [0, 0]
                        for B_ in range(TC):
                            qs = B_ * 128
                            for hp in range(8):
                                wb = pw[hp % 3]
                                wcol = 130 * (hp // 3)
                                for k2 in range(2):
                                    hh = 2 * hp + k2
                                    si = hh % 4
                                    sc = psc[si]
                                    pb = 64 * (hh % 2)
                                    so = 256 * si
                                    P.wait("pe", sV, scfree[si])
                                    lT0 = klocT[pb:pb + 64,
                                                hp * 2112 + qs:hp * 2112 + qs + 128]
                                    lT1 = klocT[pb:pb + 64,
                                                hp * 2112 + qs + 128:hp * 2112 + qs + 192]
                                    rQ = qloc[pb:pb + 64, hp * S + qs:hp * S + qs + 128]
                                    P.do("pe", "matmul", sc[:, 0:128], lT0, rQ,
                                         start=True, stop=True)
                                    P.do("pe", "matmul", sc[0:64, 128:256], lT1,
                                         rQ, start=True, stop=True, inc=sP)
                                    P.wait("dve", sP)
                                    P.wait("dve", sA, scm_free[si])
                                    P.do("dve", "tensor_tensor",
                                         scm[:, so:so + 256], sc[:],
                                         mask8[:, hh * 256:hh * 256 + 256],
                                         ALU.add, inc=sV)
                                    scfree[si] = sV.n
                                    if B_ == 0:
                                        P.do("dve", "memset",
                                             scm[0:32, so:so + 128], -1e9,
                                             inc=sV)
                                    if B_ == TC - 1:
                                        P.do("dve", "memset",
                                             scm[32:64, so + 128:so + 256],
                                             -1e9, inc=sV)
                                    P.wait("act", sV)
                                    P.wait("act", sP, pexp_done[si])
                                    P.do("act", "activation",
                                         pexp[:, so:so + 256], scm[:, so:so + 256],
                                         AF.Exp, scale=0.125, inc=sA)
                                    scm_free[si] = sA.n
                                    P.wait("pe", sA)
                                    P.wait("pe", sV, wfree[hp % 3])
                                    c0 = wcol + 65 * k2
                                    P.do("pe", "matmul", wb[:, c0:c0 + 65],
                                         pexp[:, so:so + 128],
                                         vloc[:, B_ * 1040 + hh * 65:B_ * 1040 + hh * 65 + 65],
                                         start=True, stop=False)
                                    P.do("pe", "matmul", wb[:, c0:c0 + 65],
                                         pexp[0:64, so + 128:so + 256],
                                         vloc[0:64, (B_ + 1) * 1040 + hh * 65:(B_ + 1) * 1040 + hh * 65 + 65],
                                         start=False, stop=True, inc=sP)
                                    pexp_done[si] = sP.n
                                    wdone[hp % 3] = sP.n
                                wsl = hp % 2
                                P.wait("dve", sP, wdone[hp % 3])
                                rview = wb[:, wcol:wcol + 130].rearrange(
                                    "p (h c) -> p h c", c=65)[:, :, 64:65].rearrange(
                                    "p h o -> p (h o)")
                                P.do("dve", "reciprocal", rc2[:, 2 * wsl:2 * wsl + 2],
                                     rview)
                                P.wait("dve", sP, twfree[wsl])
                                P.do("dve", "tensor_scalar_mul",
                                     win2[:, 128 * wsl:128 * wsl + 64],
                                     wb[:, wcol:wcol + 64], rc2[:, 2 * wsl:2 * wsl + 1])
                                P.do("dve", "tensor_scalar_mul",
                                     win2[:, 128 * wsl + 64:128 * wsl + 128],
                                     wb[:, wcol + 65:wcol + 129],
                                     rc2[:, 2 * wsl + 1:2 * wsl + 2],
                                     inc=sV)
                                wfree[hp % 3] = sV.n
                                P.wait("pe", sV, wfree[hp % 3])
                                P.do("pe", "transpose", ptw[:, 128 * wsl:128 * wsl + 128],
                                     win2[:, 128 * wsl:128 * wsl + 128], idb[:],
                                     inc=sP)
                                twfree[wsl] = sP.n
                                P.wait("dve", sP)
                                P.do("dve", "tensor_copy",
                                     winT[:, hp * S + qs:hp * S + qs + 128],
                                     ptw[:, 128 * wsl:128 * wsl + 128], inc=sV)
                    P.barrier()

                with contextlib.ExitStack() as s4:
                    lsl = sb(s4, "lsl", [128, DC * 512], BF16)
                    ghid = sb(s4, "ghid", [128, DC * S], BF16)

                    def load_lsl(sl):
                        return load_bf16(
                            lsl[:].rearrange("p (c n) -> p c n", c=DC),
                            lin_sp.rearrange("p (c s) -> p c s", c=DC)
                            [:, :, sl * 512:(sl + 1) * 512])

                    with contextlib.ExitStack() as sg1:
                        wg1 = sb(sg1, "wg1", [128, 24 * D], BF16)
                        brg1 = sb(sg1, "brg1", [1, D], BF16)
                        w_fm(wg1, aW["wg1"])
                        row_bf16(brg1[:], aB["bg1"])
                        P.barrier()
                        with contextlib.ExitStack() as ps6:
                            pg = [psum(ps6, "pg%d" % i, [128, 512], F32)
                                  for i in range(2)]
                            for sl in range(4):
                                P.barrier()
                                load_lsl(sl)
                                P.barrier()
                                for dgt in range(DC):
                                    bank = dgt % 2
                                    P.wait("pe", sA)
                                    first = True
                                    for c in range(DC):
                                        P.do("pe", "matmul", pg[bank][:],
                                             wg1[:, c * D + dgt * 128:c * D + dgt * 128 + 128],
                                             lsl[:, c * 512:(c + 1) * 512],
                                             start=first, stop=False)
                                        first = False
                                    for c in range(DC):
                                        P.do("pe", "matmul", pg[bank][:],
                                             wg1[:, (8 + c) * D + dgt * 128:(8 + c) * D + dgt * 128 + 128],
                                             winT[:, c * S + sl * 512:c * S + (sl + 1) * 512],
                                             start=False, stop=False)
                                    for c in range(DC):
                                        P.do("pe", "matmul", pg[bank][:],
                                             wg1[:, (16 + c) * D + dgt * 128:(16 + c) * D + dgt * 128 + 128],
                                             h[:, c * S + sl * 512:c * S + (sl + 1) * 512],
                                             start=False, stop=False)
                                    P.do("pe", "matmul", pg[bank][:],
                                         brg1[0:1, dgt * 128:dgt * 128 + 128],
                                         onesb[0:1, 0:512], start=False,
                                         stop=True, inc=sP)
                                    P.wait("act", sP)
                                    P.do("act", "activation",
                                         ghid[:, dgt * S + sl * 512:dgt * S + (sl + 1) * 512],
                                         pg[bank][:], AF.Relu, inc=sA)
                        P.barrier()
                    cb = sb(s4, "cb", [128, DC * S], BF16)
                    with contextlib.ExitStack() as sg2:
                        wg2 = sb(sg2, "wg2", [128, DC * D], BF16)
                        brg2 = sb(sg2, "brg2", [1, D], BF16)
                        gsig = sb(sg2, "gsig", [128, 512], BF16)
                        d1 = sb(sg2, "d1", [128, 512], BF16)
                        w_fm(wg2, aW["wg2"])
                        row_bf16(brg2[:], aB["bg2"])
                        P.barrier()
                        with contextlib.ExitStack() as ps7:
                            pg = [psum(ps7, "pg2%d" % i, [128, 512], F32)
                                  for i in range(2)]
                            gdone = 0
                            for sl in range(4):
                                P.barrier()
                                load_lsl(sl)
                                P.barrier()
                                for dt in range(DC):
                                    bank = dt % 2
                                    P.wait("pe", sA)
                                    for c in range(DC):
                                        P.do("pe", "matmul", pg[bank][:],
                                             wg2[:, c * D + dt * 128:c * D + dt * 128 + 128],
                                             ghid[:, c * S + sl * 512:c * S + (sl + 1) * 512],
                                             start=(c == 0), stop=False)
                                    P.do("pe", "matmul", pg[bank][:],
                                         brg2[0:1, dt * 128:dt * 128 + 128],
                                         onesb[0:1, 0:512], start=False,
                                         stop=True, inc=sP)
                                    P.wait("act", sP)
                                    P.wait("act", sV, gdone)
                                    P.do("act", "activation", gsig[:],
                                         pg[bank][:], AF.Sigmoid, inc=sA)
                                    co = dt * S + sl * 512
                                    P.wait("pool", sV, gdone)
                                    P.do("pool", "tensor_tensor", d1[:],
                                         lsl[:, dt * 512:(dt + 1) * 512],
                                         winT[:, co:co + 512], ALU.subtract,
                                         inc=sG)
                                    P.wait("dve", sA)
                                    P.wait("dve", sG)
                                    P.do("dve", "tensor_tensor", gsig[:],
                                         gsig[:], d1[:], ALU.mult)
                                    P.do("dve", "tensor_tensor",
                                         cb[:, co:co + 512], gsig[:],
                                         winT[:, co:co + 512], ALU.add, inc=sV)
                                    gdone = sV.n
                        P.barrier()
                    with contextlib.ExitStack() as s5:
                        wo = sb(s5, "wo", [128, DC * D], BF16)
                        bro_ = sb(s5, "bro", [1, D], BF16)
                        of1 = sb(s5, "of1", [1, S], BF16)
                        rt = sb(s5, "rt", [128, S])
                        w_fm(wo, aW["wo"])
                        row_bf16(bro_[:], aB["bo"])
                        sc1p = sb(s5, "sc1p", [128, S])
                        P.do("dve", "tensor_tensor", rowMU[0:1, :],
                             rowMU[0:1, :], rowSC[0:1, :], ALU.mult)
                        P.do("dve", "tensor_scalar_mul", of1[:],
                             rowMU[0:1, :], -1.0, inc=sV)
                        P.barrier()
                        bcast(rowSC[0:1, :], S, dst32=sc1p, plus1=True)
                        with contextlib.ExitStack() as ps8:
                            po = [psum(ps8, "po%d" % i, [128, 512], F32)
                                  for i in range(2)]
                            for dt in range(DC):
                                P.wait("sp", sV)
                                P.wait("sp", dL)
                                P.do("sp", "dma_start", out=rt[:],
                                     in_=r_sp[:, dt * S:(dt + 1) * S], inc=dL)
                                for sl in range(4):
                                    bank = sl % 2
                                    P.wait("pe", sV)
                                    for c in range(DC):
                                        P.do("pe", "matmul", po[bank][:],
                                             wo[:, c * D + dt * 128:c * D + dt * 128 + 128],
                                             cb[:, c * S + sl * 512:c * S + (sl + 1) * 512],
                                             start=(c == 0), stop=False)
                                    P.do("pe", "matmul", po[bank][:],
                                         bro_[0:1, dt * 128:dt * 128 + 128],
                                         onesb[0:1, 0:512], start=False,
                                         stop=False)
                                    P.do("pe", "matmul", po[bank][:],
                                         onesb[0:1, 0:128],
                                         of1[0:1, sl * 512:(sl + 1) * 512],
                                         start=False, stop=True, inc=sP)
                                    P.wait("dve", dL)
                                    P.wait("dve", sP)
                                    cs_ = slice(sl * 512, (sl + 1) * 512)
                                    P.do("dve", "tensor_tensor", rt[:, cs_],
                                         rt[:, cs_], sc1p[:, cs_], ALU.mult)
                                    P.do("dve", "tensor_tensor", rt[:, cs_],
                                         rt[:, cs_], po[bank][:], ALU.add,
                                         inc=sV)
                                P.wait("sp", sV)
                                P.do("sp", "dma_start",
                                     out=r_sp[:, dt * S:(dt + 1) * S],
                                     in_=rt[:], inc=dL)
                            P.barrier()
            dump_dram("A")

        def stage_ffn(rbuf, tag):
            ln_stats(rbuf, rowMU[0:1, :], rowSC[0:1, :], EPS5)
            with contextlib.ExitStack() as sf:
                xnb = sb(sf, "xnb" + tag, [128, DC * S], BF16)
                apply_ln(rbuf, xnb, rowMU[0:1, :], rowSC[0:1, :])
                wt = sb(sf, "fwt" + tag, [128, DC * D], BF16)
                vr = sb(sf, "fvr" + tag, [128, DC * RK], BF16)
                ct = sb(sf, "fct" + tag, [RK, D], BF16)
                fbr = sb(sf, "fbr" + tag, [1, D], BF16)
                trb = sb(sf, "trb" + tag, [RK, S], BF16)
                w_fm(wt, fWt)
                load_bf16(vr[:].rearrange("p (c n) -> p c n", c=DC),
                          fVr.rearrange("(c p) n -> p c n", p=128))
                load_bf16(ct[:], fCt)
                row_bf16(fbr[:], fB)
                if tag == "B":
                    for li2 in range(2):
                        for cc in range(DC):
                            P.do("pool", "dma_start",
                                 out=w1c[li2].rearrange(
                                     "p (t c f) -> p t c f",
                                     t=32, c=DC)[:, :, cc, :],
                                 in_=bw1[li2].rearrange(
                                     "(c p) (t f) -> c p t f",
                                     p=128, t=32)[cc],
                                 inc=P.dU[0])
                    P.w1c_built = P.dU[0].n
                P.barrier()
                with contextlib.ExitStack() as ps:
                    pf = [psum(ps, "pf%d" % i, [128, 512], F32) for i in range(2)]
                    for sl in range(4):
                        bank = sl % 2
                        P.wait("pe", sA)
                        for c in range(DC):
                            P.do("pe", "matmul", pf[bank][0:RK, :],
                                 vr[:, c * RK:(c + 1) * RK],
                                 xnb[:, c * S + sl * 512:c * S + (sl + 1) * 512],
                                 start=(c == 0), stop=(c == DC - 1),
                                 inc=(sP if c == DC - 1 else None))
                        P.wait("act", sP)
                        P.do("act", "copy", trb[:, sl * 512:(sl + 1) * 512],
                             pf[bank][0:RK, :], inc=sA)
                    P.barrier()
                    bfree = [0, 0]
                    for dt in range(DC):
                        for sl in range(4):
                            bank = sl % 2
                            P.wait("pe", sV, bfree[bank])
                            for c in range(DC):
                                P.do("pe", "matmul", pf[bank][:],
                                     wt[:, c * D + dt * 128:c * D + dt * 128 + 128],
                                     xnb[:, c * S + sl * 512:c * S + (sl + 1) * 512],
                                     start=(c == 0), stop=False)
                            P.do("pe", "matmul", pf[bank][:],
                                 ct[0:RK, dt * 128:dt * 128 + 128],
                                 trb[0:RK, sl * 512:(sl + 1) * 512],
                                 start=False, stop=False)
                            P.do("pe", "matmul", pf[bank][:],
                                 fbr[0:1, dt * 128:dt * 128 + 128],
                                 onesb[0:1, 0:512], start=False, stop=True,
                                 inc=sP)
                            P.wait("dve", sP)
                            co = dt * S + sl * 512
                            P.do("dve", "tensor_tensor", rbuf[:, co:co + 512],
                                 rbuf[:, co:co + 512], pf[bank][:], ALU.add,
                                 inc=sV)
                            bfree[bank] = sV.n
                    P.barrier()

        def stage_C(rbuf, rstack):
            if not hasattr(P, "w1c_built"):
                for li2 in range(2):
                    for cc in range(DC):
                        P.do("pool", "dma_start",
                             out=w1c[li2].rearrange(
                                 "p (t c f) -> p t c f",
                                 t=32, c=DC)[:, :, cc, :],
                             in_=bw1[li2].rearrange(
                                 "(c p) (t f) -> c p t f", p=128, t=32)[cc],
                             inc=P.dU[0])
                P.w1c_built = P.dU[0].n
            ln_stats(rbuf, rowMU[0:1, :], rowSC[0:1, :], EPS5)
            P.do("sp", "dma_start", out=r_sp, in_=rbuf[:], inc=dL)
            P.barrier()
            rstack.close()
            with contextlib.ExitStack() as st:
                hi = sb(st, "tb_h", [128, S])
                lo = sb(st, "tb_l", [128, S])
                tt = sb(st, "tb_t", [128, S], BF16)
                for ix, nm in enumerate(("XCQ", "XSQ", "XCK", "XSK")):
                    load_f32(hi[:], CN[nm + "_h"])
                    load_f32(lo[:], CN[nm + "_l"])
                    P.barrier()
                    P.do("dve", "tensor_tensor", tt[:], hi[:], lo[:],
                         ALU.add, inc=sV)
                    P.barrier()
                    P.do("sp", "dma_start", out=tab_sp[:, ix * S:(ix + 1) * S],
                         in_=tt[:], inc=dL)
                    P.barrier()
            xbst = contextlib.ExitStack()
            xb = sb(xbst, "xb", [128, DC * S], side="right")
            apply_ln(None, xb, rowMU[0:1, :], rowSC[0:1, :], dst_f32=True, src_dram=r_sp)

            for li in range(2):
                L0 = contextlib.ExitStack()
                xnb = sb(L0, "xnbC", [128, DC * S], BF16)
                ygn = sb(L0, "ygn", [128, TC * D], BF16)
                ln_stats(xb, rowMU[0:1, :], rowSC[0:1, :], EPS6)
                with contextlib.ExitStack() as sv_:
                    s1c = vec_fm(sv_, "l1s",
                                 bV["ln1_scale"][li].rearrange("(c p) -> c p", c=8))
                    b1c = vec_fm(sv_, "l1b",
                                 bV["ln1_bias"][li].rearrange("(c p) -> c p", c=8))
                    apply_ln(xb, xnb, rowMU[0:1, :], rowSC[0:1, :], gcol=s1c, bcol=b1c)
                P.do("sp", "dma_start", out=xb_sp, in_=xb[:], inc=dL)
                P.barrier()
                xbst.close()

                L1 = contextlib.ExitStack()
                tabs_t = sb(L1, "tabs_t", [128, 4 * 512], BF16)
                dct = sb(L1, "dct", [128, HB * 128], BF16)
                dct2 = sb(L1, "dct2", [128, HB * 128], BF16)
                gqt = sb(L1, "gqt", [128, HB * 128], BF16)
                gq2t = sb(L1, "gq2t", [128, HB * 128], BF16)
                gkt = sb(L1, "gkt", [128, 2 * HB])
                gct = sb(L1, "gct", [128, HB])
                rott = sb(L1, "rott", [128, 128], BF16)
                gnS = sb(L1, "gnS", [128, D])
                gnB = sb(L1, "gnB", [128, D])
                Sf = sb(L1, "Sf", [128, HB * 128])
                Sb2 = sb(L1, "Sb2", [128, HB * 128], BF16)
                load_bf16(rott[:], CN["ROTT"])
                for ix in range(4):
                    P.do("sp", "dma_start",
                         out=tabs_t[:, ix * 512:(ix + 1) * 512],
                         in_=tab_sp[:, ix * S:ix * S + 512], inc=dL)
                tab_ld = dL.n
                with contextlib.ExitStack() as stx:
                    hi = sb(stx, "ch", [128, HB * 128])
                    lo = sb(stx, "cl", [128, HB * 128])
                    for dst, nm in ((dct, "DCT"), (gqt, "GQ"),
                                    (dct2, "DCT2"), (gq2t, "GQ2")):
                        load_f32(hi[:], CN[nm + "_h"])
                        load_f32(lo[:], CN[nm + "_l"])
                        P.barrier()
                        P.do("dve", "tensor_tensor", dst[:], hi[:], lo[:],
                             ALU.add, inc=sV)
                        P.barrier()
                    for dst, nm in ((gkt[:, 0:HB], "GK"),
                                    (gkt[:, HB:2 * HB], "GK255"),
                                    (gct[:], "GC")):
                        load_f32(hi[:, 0:HB], CN[nm + "_h"])
                        load_f32(lo[:, 0:HB], CN[nm + "_l"])
                        P.barrier()
                        P.do("dve", "tensor_tensor", dst, hi[:, 0:HB],
                             lo[:, 0:HB], ALU.add, inc=sV)
                        P.barrier()
                with contextlib.ExitStack() as sgn:
                    gnr1 = sb(sgn, "gnr1", [1, D])
                    gnr2 = sb(sgn, "gnr2", [1, D])
                    load_f32(gnr1[:], bV["gn_scale"][li:li + 1])
                    load_f32(gnr2[:], bV["gn_bias"][li:li + 1])
                    P.barrier()
                    bcast(gnr1[0:1, :], D, dst32=gnS)
                    bcast(gnr2[0:1, :], D, dst32=gnB)
                P.do("dve", "memset", Sf[:], 0.0)
                P.do("dve", "memset", Sb2[:], 0.0, inc=sV)
                P.barrier()
                with contextlib.ExitStack() as c1:
                    wqq = sb(c1, "wqq", [128, HB * D], BF16)
                    wkk = sb(c1, "wkk", [128, HB * D], BF16)
                    wvv = sb(c1, "wvv", [128, HB * D], BF16)
                    for dst, srcw in ((wqq, bWq), (wkk, bWk), (wvv, bWv)):
                        load_bf16(dst[:].rearrange("p (h c e) -> p h c e",
                                                   h=HB, c=DC),
                                  srcw[li].rearrange("h (c p) e -> p h c e",
                                                     p=128))
                    qf = sb(c1, "qf", [128, 3 * 512], BF16)
                    qx = sb(c1, "qx", [128, HB * 512], BF16)
                    kx = sb(c1, "kx", [128, HB * 512], BF16)
                    vtk = sb(c1, "vtk", [128, 4 * D], BF16)
                    kgt = sb(c1, "kgt", [128, 2 * D], BF16)
                    qg = sb(c1, "qg", [128, 2 * D], BF16)
                    scb = sb(c1, "scb", [128, 3 * D], BF16)
                    ret = sb(c1, "ret", [128, 2 * D], BF16)
                    sqb = sb(c1, "sqb", [128, D], BF16)
                    st8 = sb(c1, "st8", [128, 4 * HB])
                    m1s = sb(c1, "m1s", [128, 512], BF16)
                    m2s = sqb[:, 0:512]
                    P.barrier()
                    TB4 = [tabs_t[:, i * 512:(i + 1) * 512] for i in range(4)]
                    with contextlib.ExitStack() as cp:
                        pbank = [psum(cp, "cb%d" % i, [128, 512], F32)
                                 for i in range(6)]
                        pp = pbank[0:2]
                        prr = psum(cp, "prr", [128, 512], F32)
                        psc2 = pbank[0:2]
                        pou = pbank[2:4]
                        psd = pbank[4:6]
                        ptc = psum(cp, "ptc", [128, 1024], BF16)
                        scb_evac = [[0, 0], [0, 0]]
                        ret_evac = [[0, 0], [0, 0]]
                        scbj_evac = 0
                        supd_done = 0
                        kgt_evac = 0
                        gn_pool_done = 0
                        prj_evac = 0
                        qg_guard = 0
                        pending_gn = []

                        def flush_gn():
                            nonlocal gn_pool_done
                            for fn in pending_gn:
                                fn()
                            pending_gn.clear()
                            gn_pool_done = sG.n
                        for stile in range(4):
                            s0_ = stile * 512
                            P.wait("pool", dL, tab_ld)
                            P.wait("dve", dL, tab_ld)
                            for sel, (ww, dstx, tc_, ts_) in enumerate(
                                    ((wqq, qx, TB4[0], TB4[1]),
                                     (wkk, kx, TB4[2], TB4[3]))):
                                for h_ in range(HB):
                                    bank = h_ % 2
                                    qsl = (sel * 8 + h_) % 3
                                    P.wait("pe", sV, scb_evac[1][bank])
                                    P.wait("pe", sA, prj_evac)
                                    if sel == 0 and h_ == 0:
                                        P.wait("pe", sV, scbj_evac)
                                    for c in range(DC):
                                        P.do("pe", "matmul", pp[bank][:],
                                             ww[:, (h_ * 8 + c) * 128:(h_ * 8 + c) * 128 + 128],
                                             xnb[:, c * S + s0_:c * S + s0_ + 512],
                                             start=(c == 0), stop=(c == DC - 1),
                                             inc=(sP if c == DC - 1 else None))
                                    P.wait("act", sP)
                                    qfs = qf[:, qsl * 512:(qsl + 1) * 512]
                                    P.wait("act", sG)
                                    P.wait("act", sV)
                                    P.do("act", "copy", qfs, pp[bank][:], inc=sA)
                                    prj_evac = sA.n
                                    rb_ = prr if (h_ % 2 == 0) else pou[1]
                                    P.wait("pe", sA)
                                    if h_ % 2 == 1:
                                        P.wait("pe", sA, ret_evac[1][1])
                                    P.do("pe", "matmul", rb_[:], rott[:], qfs,
                                         start=True, stop=True, inc=sP)
                                    P.wait("pool", sA)
                                    P.do("pool", "tensor_tensor", m1s[:], qfs,
                                         tc_, ALU.mult, inc=sG)
                                    P.wait("dve", sP)
                                    P.do("dve", "tensor_tensor", m2s, rb_[:],
                                         ts_, ALU.mult, inc=sV)
                                    P.wait("dve", sG)
                                    P.do("dve", "tensor_tensor",
                                         dstx[:, h_ * 512:(h_ + 1) * 512],
                                         m1s[:], m2s, ALU.add, inc=sV)
                            qk_done = sV.n
                            if stile < 3:
                                P.wait("sp", sV, qk_done)
                                for ix in range(4):
                                    P.do("sp", "dma_start",
                                         out=TB4[ix],
                                         in_=tab_sp[:, ix * S + s0_ + 512:
                                                    ix * S + s0_ + 1024], inc=dL)
                                tab_ld = dL.n
                            for tl in range(4):
                                t0 = s0_ + tl * 128
                                P.wait("pe", sV, scb_evac[1][0])
                                P.wait("pe", sV, scb_evac[1][1])
                                P.wait("pe", sA, prj_evac)
                                for c in range(DC):
                                    for hf in range(2):
                                        P.do("pe", "matmul", pp[hf][:],
                                             xnb[:, c * S + t0:c * S + t0 + 128],
                                             wvv[:].rearrange("p (h c e) -> p h c e",
                                                              h=HB, c=DC)[:, 4 * hf:4 * (hf + 1), c, :],
                                             start=(c == 0), stop=(c == DC - 1),
                                             inc=(sP if (c == DC - 1 and hf == 1) else None))
                                P.wait("act", sP)
                                for hf in range(2):
                                    P.do("act", "copy",
                                         vtk[:, tl * D + hf * 512:tl * D + (hf + 1) * 512],
                                         pp[hf][:], inc=sA)
                                prj_evac = sA.n
                            for pl in range(2):
                                ta, tb = 2 * pl, 2 * pl + 1
                                P.wait("pe", sV, kgt_evac)
                                P.wait("pe", sV, qk_done)
                                for h_ in range(HB):
                                    P.do("pe", "transpose",
                                         ptc[:, h_ * 128:(h_ + 1) * 128],
                                         kx[:, h_ * 512 + ta * 128:h_ * 512 + ta * 128 + 128],
                                         idb[:], inc=(sP if h_ == HB - 1 else None))
                                P.wait("dve", sP)
                                for h_ in range(HB):
                                    P.do("dve", "tensor_scalar_mul",
                                         kgt[:, h_ * 128:(h_ + 1) * 128],
                                         ptc[:, h_ * 128:(h_ + 1) * 128],
                                         gkt[:, HB + h_:HB + h_ + 1],
                                         inc=(sV if h_ == HB - 1 else None))
                                kgt_a_done = sV.n
                                P.wait("pool", sV, qk_done)
                                P.wait("pool", sP, qg_guard)
                                for ch_, tt_, gt_ in ((0, ta, gqt), (1, tb, gq2t)):
                                    P.do("pool", "tensor_tensor",
                                         qg[:, ch_ * D:(ch_ + 1) * D].rearrange(
                                             "p (h e) -> p h e", h=HB),
                                         qx[:].rearrange("p (h c e) -> p h c e",
                                                         h=HB, c=4)[:, :, tt_, :],
                                         gt_[:].rearrange("p (h e) -> p h e", h=HB),
                                         ALU.mult,
                                         inc=(sG if ch_ == 1 else None))
                                qg_done = sG.n
                                flush_gn()
                                for grp in range(2):
                                    P.wait("pe", sV, scb_evac[0][grp])
                                    P.wait("pe", sA, prj_evac)
                                    for h_ in range(4 * grp, 4 * grp + 4):
                                        P.do("pe", "matmul",
                                             psc2[grp][:, (h_ % 4) * 128:(h_ % 4) * 128 + 128],
                                             kx[:, h_ * 512 + ta * 128:h_ * 512 + ta * 128 + 128],
                                             qx[:, h_ * 512 + ta * 128:h_ * 512 + ta * 128 + 128],
                                             start=True, stop=True,
                                             inc=(sP if h_ % 4 == 3 else None))
                                    P.wait("dve", sP)
                                    P.do("dve", "tensor_tensor",
                                         scb[:, grp * 512:(grp + 1) * 512],
                                         psc2[grp][:],
                                         dct[:, grp * 512:(grp + 1) * 512],
                                         ALU.mult, inc=sV)
                                    scb_evac[0][grp] = sV.n
                                P.wait("pe", sV, scbj_evac)
                                for h_ in range(4):
                                    P.do("pe", "matmul",
                                         prr[:, h_ * 128:(h_ + 1) * 128],
                                         kx[:, h_ * 512 + ta * 128:h_ * 512 + ta * 128 + 128],
                                         qx[:, h_ * 512 + tb * 128:h_ * 512 + tb * 128 + 128],
                                         start=True, stop=True,
                                         inc=(sP if h_ == 3 else None))
                                P.wait("dve", sP)
                                P.do("dve", "tensor_tensor",
                                     scb[:, 2 * D:2 * D + 512], prr[:],
                                     dct2[:, 0:512], ALU.mult, inc=sV)
                                scbj0 = sV.n
                                P.wait("pe", sV, kgt_a_done)
                                for h_ in range(HB):
                                    P.do("pe", "transpose",
                                         ptc[:, h_ * 128:(h_ + 1) * 128],
                                         kx[:, h_ * 512 + tb * 128:h_ * 512 + tb * 128 + 128],
                                         idb[:], inc=(sP if h_ == HB - 1 else None))
                                P.wait("dve", sP)
                                for h_ in range(HB):
                                    P.do("dve", "tensor_scalar_mul",
                                         kgt[:, D + h_ * 128:D + (h_ + 1) * 128],
                                         ptc[:, h_ * 128:(h_ + 1) * 128],
                                         gkt[:, h_:h_ + 1],
                                         inc=(sV if h_ == HB - 1 else None))
                                kgt_evac = sV.n
                                for grp in range(2):
                                    P.wait("pe", sV, scb_evac[0][grp])
                                    for h_ in range(4 * grp, 4 * grp + 4):
                                        P.do("pe", "matmul",
                                             psc2[grp][:, (h_ % 4) * 128:(h_ % 4) * 128 + 128],
                                             kx[:, h_ * 512 + tb * 128:h_ * 512 + tb * 128 + 128],
                                             qx[:, h_ * 512 + tb * 128:h_ * 512 + tb * 128 + 128],
                                             start=True, stop=True,
                                             inc=(sP if h_ % 4 == 3 else None))
                                    P.wait("dve", sP)
                                    P.do("dve", "tensor_tensor",
                                         scb[:, D + grp * 512:D + (grp + 1) * 512],
                                         psc2[grp][:],
                                         dct[:, grp * 512:(grp + 1) * 512],
                                         ALU.mult, inc=sV)
                                    scb_evac[1][grp] = sV.n
                                P.wait("pe", sV, scbj0)
                                for h_ in range(4, 8):
                                    P.do("pe", "matmul",
                                         prr[:, (h_ % 4) * 128:(h_ % 4) * 128 + 128],
                                         kx[:, h_ * 512 + ta * 128:h_ * 512 + ta * 128 + 128],
                                         qx[:, h_ * 512 + tb * 128:h_ * 512 + tb * 128 + 128],
                                         start=True, stop=True,
                                         inc=(sP if h_ == 7 else None))
                                P.wait("dve", sP)
                                P.do("dve", "tensor_tensor",
                                     scb[:, 2 * D + 512:3 * D], prr[:],
                                     dct2[:, 512:1024], ALU.mult, inc=sV)
                                scbj_evac = sV.n
                                P.wait("pe", sG, qg_done)
                                for grp in range(2):
                                    P.wait("pe", sA, ret_evac[1][grp])
                                    P.wait("pe", sV, supd_done)
                                    P.wait("pe", sV, scb_evac[0][grp])
                                    for h_ in range(4 * grp, 4 * grp + 4):
                                        cc0 = (h_ % 4) * 128
                                        P.do("pe", "matmul",
                                             pou[grp][:, cc0:cc0 + 128],
                                             scb[:, h_ * 128:(h_ + 1) * 128],
                                             vtk[:, ta * D + h_ * 128:ta * D + (h_ + 1) * 128],
                                             start=True, stop=False)
                                        P.do("pe", "matmul",
                                             pou[grp][:, cc0:cc0 + 128],
                                             qg[:, h_ * 128:(h_ + 1) * 128],
                                             Sb2[:, h_ * 128:(h_ + 1) * 128],
                                             start=False, stop=True,
                                             inc=(sP if h_ % 4 == 3 else None))
                                    P.wait("act", sP)
                                    P.wait("act", sG, gn_pool_done)
                                    P.do("act", "copy",
                                         ret[:, grp * 512:(grp + 1) * 512],
                                         pou[grp][:], inc=sA)
                                    ret_evac[0][grp] = sA.n
                                for grp in range(2):
                                    P.wait("pe", sA, ret_evac[0][grp])
                                    P.wait("pe", sV, scb_evac[1][grp])
                                    P.wait("pe", sV, scbj_evac)
                                    P.wait("pe", sV, kgt_evac)
                                    for h_ in range(4 * grp, 4 * grp + 4):
                                        cc0 = (h_ % 4) * 128
                                        P.do("pe", "matmul",
                                             pou[grp][:, cc0:cc0 + 128],
                                             scb[:, D + h_ * 128:D + (h_ + 1) * 128],
                                             vtk[:, tb * D + h_ * 128:tb * D + (h_ + 1) * 128],
                                             start=True, stop=False)
                                        P.do("pe", "matmul",
                                             pou[grp][:, cc0:cc0 + 128],
                                             scb[:, 2 * D + h_ * 128:2 * D + (h_ + 1) * 128],
                                             vtk[:, ta * D + h_ * 128:ta * D + (h_ + 1) * 128],
                                             start=False, stop=False)
                                        P.do("pe", "matmul",
                                             pou[grp][:, cc0:cc0 + 128],
                                             qg[:, D + h_ * 128:D + (h_ + 1) * 128],
                                             Sb2[:, h_ * 128:(h_ + 1) * 128],
                                             start=False, stop=True)
                                        P.do("pe", "matmul",
                                             psd[grp][:, cc0:cc0 + 128],
                                             kgt[:, h_ * 128:(h_ + 1) * 128],
                                             vtk[:, ta * D + h_ * 128:ta * D + (h_ + 1) * 128],
                                             start=True, stop=False)
                                        P.do("pe", "matmul",
                                             psd[grp][:, cc0:cc0 + 128],
                                             kgt[:, D + h_ * 128:D + (h_ + 1) * 128],
                                             vtk[:, tb * D + h_ * 128:tb * D + (h_ + 1) * 128],
                                             start=False, stop=True,
                                             inc=(sP if h_ % 4 == 3 else None))
                                    P.wait("act", sP)
                                    P.wait("act", sG, gn_pool_done)
                                    P.do("act", "copy",
                                         ret[:, D + grp * 512:D + (grp + 1) * 512],
                                         pou[grp][:], inc=sA)
                                    ret_evac[1][grp] = sA.n
                                qg_guard = sP.n
                                P.wait("dve", sP)
                                for h_ in range(HB):
                                    grp, cc0 = h_ // 4, (h_ % 4) * 128
                                    P.do("dve", "scalar_tensor_tensor",
                                         Sf[:, h_ * 128:(h_ + 1) * 128],
                                         Sf[:, h_ * 128:(h_ + 1) * 128],
                                         gct[:, h_:h_ + 1],
                                         psd[grp][:, cc0:cc0 + 128],
                                         ALU.mult, ALU.add)
                                P.do("dve", "tensor_copy", Sb2[:], Sf[:], inc=sV)
                                supd_done = sV.n
                                for ch_ in range(2):
                                    tg = stile * 4 + 2 * pl + ch_
                                    rv = ret[:, ch_ * D:(ch_ + 1) * D]
                                    P.wait("dve", sA, ret_evac[ch_][1])
                                    P.do("dve", "reduce_sum", st8[:, 0:HB],
                                         rv.rearrange("p (h e) -> p h e", h=HB),
                                         axis=mybir.AxisListType.X)
                                    P.do("dve", "tensor_tensor", sqb[:], rv, rv,
                                         ALU.mult)
                                    P.do("dve", "reduce_sum", st8[:, HB:2 * HB],
                                         sqb[:].rearrange("p (h e) -> p h e", h=HB),
                                         axis=mybir.AxisListType.X)
                                    P.wait("dve", sA)
                                    P.do("dve", "tensor_scalar_mul",
                                         st8[:, 2 * HB:3 * HB], st8[:, 0:HB],
                                         cst[:, 4:5])
                                    P.do("dve", "tensor_scalar_mul",
                                         st8[:, HB:2 * HB], st8[:, HB:2 * HB],
                                         cst[:, 4:5])
                                    P.do("dve", "tensor_tensor", st8[:, 0:HB],
                                         st8[:, 2 * HB:3 * HB],
                                         st8[:, 2 * HB:3 * HB], ALU.mult)
                                    P.do("dve", "tensor_tensor", st8[:, HB:2 * HB],
                                         st8[:, HB:2 * HB], st8[:, 0:HB],
                                         ALU.subtract, inc=sV)
                                    P.wait("act", sV)
                                    P.do("act", "activation", st8[:, 3 * HB:4 * HB],
                                         st8[:, HB:2 * HB], AF.Ln, bias=EPS6)
                                    P.do("act", "activation", st8[:, 3 * HB:4 * HB],
                                         st8[:, 3 * HB:4 * HB], AF.Exp, scale=-0.5,
                                         inc=sA)
                                    P.wait("dve", sA)
                                    for h_ in range(HB):
                                        P.do("dve", "tensor_scalar",
                                             rv[:, h_ * 128:(h_ + 1) * 128],
                                             rv[:, h_ * 128:(h_ + 1) * 128],
                                             st8[:, 2 * HB + h_:2 * HB + h_ + 1],
                                             st8[:, 3 * HB + h_:3 * HB + h_ + 1],
                                             ALU.subtract, ALU.mult, inc=sV)
                                    svn = sV.n

                                    def em(rv=rv, tg=tg, svn=svn):
                                        P.wait("pool", sV, svn)
                                        P.do("pool", "tensor_tensor", rv, rv,
                                             gnS[:], ALU.mult)
                                        P.do("pool", "tensor_tensor",
                                             ygn[:, tg * D:(tg + 1) * D], rv,
                                             gnB[:], ALU.add, inc=sG)
                                    pending_gn.append(em)
                        flush_gn()
                    P.barrier()
                L1.close()

                xbst = contextlib.ExitStack()
                xb = sb(xbst, "xb", [128, DC * S], side="right")
                P.do("sp", "dma_start", out=xb[:], in_=xb_sp, inc=dL)
                with contextlib.ExitStack() as c2:
                    wgg = sb(c2, "wgg", [128, DC * D], BF16)
                    woo = sb(c2, "woo", [128, DC * D], BF16)
                    load_bf16(wgg[:].rearrange("p (c n) -> p c n", c=DC),
                              bWg[li].rearrange("(c p) n -> p c n", p=128))
                    load_bf16(woo[:].rearrange("p (c n) -> p c n", c=DC),
                              bWo[li].rearrange("(c p) n -> p c n", p=128))
                    swt = sb(c2, "swt", [128, 2 * D], BF16)
                    ptok = sb(c2, "ptok", [128, 2 * D], BF16)
                    prT = sb(c2, "prT", [128, DC * 512], BF16)
                    P.barrier()
                    with contextlib.ExitStack() as ps9:
                        pg = [psum(ps9, "pg3%d" % i, [128, 512], F32)
                              for i in range(2)]
                        ptc = psum(ps9, "ptc2", [128, 1024], BF16)
                        pt_cons = 0
                        prT_evac = 0
                        swt_free = [0, 0]
                        for stile in range(4):
                            s0_ = stile * 512
                            for tl in range(4):
                                tg = stile * 4 + tl
                                t0 = s0_ + tl * 128
                                P.wait("pe", sA)
                                if tl == 0:
                                    P.wait("pe", sV)
                                for c in range(DC):
                                    for hf in range(2):
                                        P.do("pe", "matmul", pg[hf][:],
                                             xnb[:, c * S + t0:c * S + t0 + 128],
                                             wgg[:, c * D + hf * 512:c * D + (hf + 1) * 512],
                                             start=(c == 0), stop=(c == DC - 1),
                                             inc=(sP if (c == DC - 1 and hf == 1) else None))
                                P.wait("act", sP)
                                swsl = (tl % 2) * D
                                P.wait("act", sG, swt_free[tl % 2])
                                for hf in range(2):
                                    P.do("act", "activation",
                                         swt[:, swsl + hf * 512:swsl + (hf + 1) * 512],
                                         pg[hf][:], AF.Silu,
                                         inc=(sA if hf == 1 else None))
                                psl = (tl % 2) * D
                                P.wait("pool", sA)
                                P.wait("pool", sP, pt_cons)
                                P.do("pool", "tensor_tensor", ptok[:, psl:psl + D],
                                     ygn[:, tg * D:(tg + 1) * D],
                                     swt[:, swsl:swsl + D],
                                     ALU.mult, inc=sG)
                                swt_free[tl % 2] = sG.n
                                P.wait("pe", sG)
                                P.wait("pe", sV, prT_evac)
                                for dc2 in range(DC):
                                    P.do("pe", "transpose",
                                         ptc[:, dc2 * 128:(dc2 + 1) * 128],
                                         ptok[:, psl + dc2 * 128:psl + (dc2 + 1) * 128],
                                         idb[:],
                                         inc=(sP if dc2 == DC - 1 else None))
                                pt_cons = sP.n
                                P.wait("dve", sP)
                                P.do("dve", "tensor_copy",
                                     prT[:].rearrange("p (c n) -> p c n", c=DC)
                                     [:, :, tl * 128:tl * 128 + 128],
                                     ptc[:].rearrange("p (c n) -> p c n", c=DC),
                                     inc=sV)
                                prT_evac = sV.n
                            prdone = sV.n
                            P.wait("dve", dL)
                            for dt in range(DC):
                                bank = dt % 2
                                P.wait("pe", sV, prdone)
                                P.wait("pe", sV)
                                P.wait("pe", sA)
                                for c in range(DC):
                                    P.do("pe", "matmul", pg[bank][:],
                                         woo[:, c * D + dt * 128:c * D + dt * 128 + 128],
                                         prT[:, c * 512:(c + 1) * 512],
                                         start=(c == 0), stop=(c == DC - 1),
                                         inc=(sP if c == DC - 1 else None))
                                P.wait("dve", sP)
                                co = dt * S + s0_
                                P.do("dve", "tensor_tensor", xb[:, co:co + 512],
                                     xb[:, co:co + 512], pg[bank][:], ALU.add,
                                     inc=sV)
                    P.barrier()
                L0.close()

                with contextlib.ExitStack() as sv2:
                    w2r = sb(sv2, "w2r", [128, 32 * D], BF16)
                    load_bf16(w2r[:].rearrange("p (c n) -> p c n", c=32),
                              bw2[li].rearrange("(c p) n -> p c n", p=128))
                    ln_stats(xb, rowMU[0:1, :], rowSC[0:1, :], EPS6)
                    s2c = vec_fm(sv2, "l2s",
                                 bV["ln2_scale"][li].rearrange("(c p) -> c p", c=8))
                    b2c = vec_fm(sv2, "l2b",
                                 bV["ln2_bias"][li].rearrange("(c p) -> c p", c=8))
                    P.barrier()
                    for st2 in range(4):
                        with contextlib.ExitStack() as c3:
                            xn2 = sb(c3, "xn2", [128, DC * 512], BF16)
                            h1 = sb(c3, "h1", [128, 32 * 512], BF16)
                            w1s = [sb(c3, "w1s%d" % i, [128, DC * 128], BF16)
                                   for i in range(4)]
                            apply_ln(xb, xn2, rowMU[0:1, :], rowSC[0:1, :],
                                     gcol=s2c, bcol=b2c, s0=st2 * 512, ns=512)
                            with contextlib.ExitStack() as pc3:
                                p1 = [psum(pc3, "p31%d" % i, [128, 512], F32)
                                      for i in range(2)]
                                use_done = [0, 0, 0, 0]
                                ldd = {}

                                def w1_load(ft):
                                    sl_ = ft % 4
                                    P.wait("sp", P.dU[0], P.w1c_built)
                                    P.wait("pool", P.dU[0], P.w1c_built)
                                    P.wait("sp", sP, use_done[sl_])
                                    P.wait("pool", sP, use_done[sl_])
                                    P.do("sp", "dma_start",
                                         out=w1s[sl_][:, 0:512],
                                         in_=w1c[li][:, ft * 1024:ft * 1024 + 512],
                                         inc=P.dH[sl_])
                                    P.do("pool", "dma_start",
                                         out=w1s[sl_][:, 512:1024],
                                         in_=w1c[li][:, ft * 1024 + 512:(ft + 1) * 1024],
                                         inc=P.dU[sl_])
                                    ldd[ft] = (P.dH[sl_].n, P.dU[sl_].n)
                                w1_load(0)
                                w1_load(1)
                                w1_load(2)
                                for ft in range(32):
                                    sl_ = ft % 4
                                    if ft + 3 < 32:
                                        w1_load(ft + 3)
                                    P.wait("pe", P.dH[sl_], ldd[ft][0])
                                    P.wait("pe", P.dU[sl_], ldd[ft][1])
                                    P.wait("pe", sA)
                                    bank = ft % 2
                                    for c in range(DC):
                                        P.do("pe", "matmul", p1[bank][:],
                                             w1s[sl_][:, c * 128:(c + 1) * 128],
                                             xn2[:, c * 512:(c + 1) * 512],
                                             start=(c == 0), stop=(c == DC - 1),
                                             inc=(sP if c == DC - 1 else None))
                                    use_done[sl_] = sP.n
                                    P.wait("act", sP)
                                    P.do("act", "activation",
                                         h1[:, ft * 512:(ft + 1) * 512],
                                         p1[bank][:], AF.Gelu_apprx_tanh,
                                         inc=sA)
                            with contextlib.ExitStack() as pc4:
                                p8 = [psum(pc4, "p8%d" % i, [128, 512], F32)
                                      for i in range(8)]
                                P.wait("pe", sA)
                                P.wait("pe", sV)
                                for fc in range(32):
                                    for do in range(8):
                                        P.do("pe", "matmul", p8[do][:],
                                             w2r[:, fc * D + do * 128:fc * D + (do + 1) * 128],
                                             h1[:, fc * 512:(fc + 1) * 512],
                                             start=(fc == 0), stop=(fc == 31),
                                             inc=(sP if (fc == 31 and do == 7) else None))
                                P.wait("dve", sP)
                                for do in range(8):
                                    co = do * S + st2 * 512
                                    P.do("dve", "tensor_tensor",
                                         xb[:, co:co + 512],
                                         xb[:, co:co + 512], p8[do][:],
                                         ALU.add, inc=(sV if do == 7 else None))
                            P.barrier()
            with contextlib.ExitStack() as ce:
                t = sb(ce, "cet", [128, S])
                for c in range(DC):
                    P.wait("sp", sV)
                    P.wait("sp", dL)
                    P.do("sp", "dma_start", out=t[:],
                         in_=r_sp[:, c * S:(c + 1) * S], inc=dL)
                    P.wait("dve", dL)
                    P.do("dve", "tensor_tensor", t[:], t[:],
                         xb[:, c * S:(c + 1) * S], ALU.add, inc=sV)
                    P.wait("sp", sV)
                    P.do("sp", "dma_start", out=r_sp[:, c * S:(c + 1) * S],
                         in_=t[:], inc=dL)
                P.barrier()
            xbst.close()
            dump_dram("C")

        if "A" in stages:
            stage_A()
        else:
            with contextlib.ExitStack() as st:
                t = sb(st, "xcp", [128, DC * S])
                load_f32(t[:].rearrange("p (c s) -> p c s", c=DC),
                         xT.rearrange("(c p) s -> p c s", p=128))
                P.barrier()
                P.do("sp", "dma_start", out=r_sp, in_=t[:], inc=dL)
                P.barrier()

        rstack = contextlib.ExitStack()
        rbuf = None
        if "B" in stages or "C" in stages or "D" in stages:
            rbuf = sb(rstack, "rbB", [128, DC * S])
            P.do("sp", "dma_start", out=rbuf[:], in_=r_sp, inc=dL)
            P.barrier()
        if "B" in stages:
            stage_ffn(rbuf, "B")
            dump_sbuf("B", rbuf)
        if "C" in stages:
            stage_C(rbuf, rstack)
            rstack = contextlib.ExitStack()
            rbuf = sb(rstack, "rbD", [128, DC * S])
            P.do("sp", "dma_start", out=rbuf[:], in_=r_sp, inc=dL)
            P.barrier()
        if "D" in stages:
            stage_ffn(rbuf, "D")
        if rbuf is not None:
            P.barrier()
            P.do("sp", "dma_start", out=yT, in_=rbuf[:], inc=dL)
            P.barrier()
            rstack.close()
        else:
            with contextlib.ExitStack() as st:
                t = sb(st, "ycp", [128, S])
                for c in range(DC):
                    P.do("sp", "dma_start", out=t[:],
                         in_=r_sp[:, c * S:(c + 1) * S], inc=dL)
                    P.wait("sp", dL)
                    P.do("sp", "dma_start", out=yT[:, c * S:(c + 1) * S],
                         in_=t[:], inc=dL)
                    P.wait("sp", dL)
                P.barrier()

        with nc.Block() as block:
            P.finalize(block)
    return nc


def kernel(**inputs):
    n = 8
    consts = _host_consts()
    toep = _toep(inputs["L0_attn_a_rel_bias"])
    x = np.ascontiguousarray(inputs["L0_attn_a_x"])

    nc = bass.Bass("TRN2", target_bir_lowering=False)
    _build(nc, _CFG)

    common = {}
    for k in ["wq_lin", "bq_lin", "wk_lin", "bk_lin", "wv_lin", "bv_lin",
              "wq_loc", "bq_loc", "wk_loc", "bk_loc", "wv_loc", "bv_loc",
              "wo", "bo", "wg1", "bg1", "wg2", "bg2", "ln_g", "ln_b"]:
        common["a_" + k] = np.ascontiguousarray(
            np.asarray(inputs["L0_attn_a_" + k], np.float32))
    common["TOEP"] = toep
    common["f_Wt"] = np.ascontiguousarray(
        np.asarray(inputs["L0_ffn_W"], np.float32).T)
    common["f_b"] = np.ascontiguousarray(np.asarray(inputs["L0_ffn_b"], np.float32))
    common["f_Vr"] = np.ascontiguousarray(np.asarray(inputs["L0_ffn_V_r"], np.float32))
    common["f_Ct"] = np.ascontiguousarray(
        np.asarray(inputs["L0_ffn_C"], np.float32).T)
    for k in ["Wq", "Wk", "Wv", "Wg", "Wo", "gn_scale", "gn_bias",
              "ln1_scale", "ln1_bias", "ln2_scale", "ln2_bias"]:
        common["b_" + k] = np.ascontiguousarray(
            np.asarray(inputs["L0_attn_b_" + k], np.float32))
    common["b_w1"] = np.ascontiguousarray(
        np.asarray(inputs["L0_attn_b_ffn_w1"], np.float32))
    common["b_w2"] = np.ascontiguousarray(
        np.asarray(inputs["L0_attn_b_ffn_w2"], np.float32))
    for k, v in consts.items():
        common["C_" + k] = np.ascontiguousarray(v)

    in_maps = []
    for b in range(n):
        m = dict(common)
        m["xT"] = np.ascontiguousarray(np.asarray(x[b], np.float32).T)
        in_maps.append(m)

    res = run_bass_kernel_spmd(nc, in_maps, core_ids=list(range(n)))
    out = np.empty((n, S, D), np.float32)
    for b in range(n):
        yt = np.asarray(res.results[b]["yT"])
        out[b] = yt.reshape(128, DC, S).transpose(2, 1, 0).reshape(S, D)
    return out
